# version 15
# baseline (speedup 1.0000x reference)
import numpy as np
import ml_dtypes
import concourse.bass as bass
import concourse.mybir as mybir
from concourse.bass_utils import run_bass_kernel_spmd

F32 = mybir.dt.float32
BF16 = mybir.dt.bfloat16
AF = mybir.ActivationFunctionType
ALU = mybir.AluOpType
AX = mybir.AxisListType

NCORES = 8
ROUNDS = 4
TS = 512
H = 2048
FH = 5632
FT = 11
FB = 44
HT = 4
MAGIC = float(np.float32(1.5 * 2 ** 23))
EPS_LN = 1e-5
EPS_Q = 1e-5
NW = float(FH * H)

A_X = 0
A_X2 = 8192
A_YT = 16384
A_F = 20480
A_SI = 24576
A_ST = 28672
A_C = 32768
A_Q = 36864
A_SV = 40960
A_END = 45056
P_STRIPE = 8192
P_QBF = 9728
P_TST = 10496
A_SLAB = 12288
A_SG = 12288


class Sem:
    def __init__(self, name):
        self.name = name
        self.n = 0
        self.h = None


class Plan:
    ENGS = ("sync", "gpsimd", "tensor", "vector", "scalar")

    def __init__(self):
        self.items = {e: [] for e in self.ENGS}
        self.lastwait = {e: {} for e in self.ENGS}

    def em(self, eng, fn, waits=(), inc=None, tag=""):
        w = []
        for sem, val in waits:
            if val <= 0:
                continue
            if self.lastwait[eng].get(sem.name, -1) >= val:
                continue
            self.lastwait[eng][sem.name] = val
            w.append((sem, val))
        self.items[eng].append((tuple(w), fn, inc, tag))
        if inc is not None:
            inc[0].n += inc[1]

    def check(self):
        pos = {e: 0 for e in self.ENGS}
        cnt = {}
        progress = True
        while progress:
            progress = False
            for e in self.ENGS:
                while pos[e] < len(self.items[e]):
                    waits, _, inc, _ = self.items[e][pos[e]]
                    if all(cnt.get(s.name, 0) >= v for s, v in waits):
                        if inc is not None:
                            s, k = inc
                            cnt[s.name] = cnt.get(s.name, 0) + k
                        pos[e] += 1
                        progress = True
                    else:
                        break
        stuck = {e: pos[e] for e in self.ENGS if pos[e] < len(self.items[e])}
        if stuck:
            msg = []
            for e, p in stuck.items():
                waits, _, _, tag = self.items[e][p]
                wd = [(s.name, v, cnt.get(s.name, 0)) for s, v in waits]
                msg.append(f"{e}@{p}/{len(self.items[e])} tag={tag} waits={wd}")
            raise RuntimeError("plan deadlock: " + " | ".join(msg))

    def emit(self, block):
        def mk(eng_name):
            drainy = eng_name in ("vector", "scalar")
            def body(eng):
                for waits, fn, inc, _ in self.items[eng_name]:
                    for s, v in waits:
                        eng.wait_ge(s.h, v)
                    ins = fn(eng) if fn is not None else None
                    if inc is not None:
                        s, k = inc
                        if ins is not None:
                            ins.then_inc(s.h, k)
                        else:
                            eng.sem_inc(s.h, k)
                    if drainy and ins is not None:
                        eng.drain()
            return body
        block.sync(mk("sync"))
        block.gpsimd(mk("gpsimd"))
        block.tensor(mk("tensor"))
        block.vector(mk("vector"))
        block.scalar(mk("scalar"))


def build_nc():
    nc = bass.Bass("TRN2", target_bir_lowering=False)

    xin = nc.dram_tensor("xin", [ROUNDS * TS, H], F32, kind="ExternalInput").ap()
    wmine = nc.dram_tensor("wmine", [H, H], F32, kind="ExternalInput").ap()
    w1s = nc.dram_tensor("w1s", [FH, 256], F32, kind="ExternalInput").ap()
    w3s = nc.dram_tensor("w3s", [FH, 256], F32, kind="ExternalInput").ap()
    w2s = nc.dram_tensor("w2s", [H, 768], F32, kind="ExternalInput").ap()
    init_in = nc.dram_tensor("init_s", [128, 16], F32, kind="ExternalInput").ap()
    mm_in = nc.dram_tensor("mask_m", [128, 8], F32, kind="ExternalInput").ap()
    mn_in = nc.dram_tensor("mask_n", [128, 8], F32, kind="ExternalInput").ap()
    idn_in = nc.dram_tensor("idn", [128, 128], BF16, kind="ExternalInput").ap()
    out = nc.dram_tensor("out", [ROUNDS * TS, H], F32, kind="ExternalOutput").ap()

    wri = [nc.dram_tensor(f"wri{j}", [4, 128, 2048], BF16) for j in range(4)]
    wall = [nc.dram_tensor(f"wall{j}", [8, 4, 128, 2048], BF16, addr_space="Shared") for j in range(4)]
    wq_in = [nc.dram_tensor("w1q_in", [2, FT, 128, 512], BF16),
             nc.dram_tensor("w3q_in", [2, FT, 128, 512], BF16),
             nc.dram_tensor("w2q_in", [6, HT, 128, 512], BF16)]
    wq_ag = [nc.dram_tensor("w1q_ag", [16, FT, 128, 512], BF16, addr_space="Shared"),
             nc.dram_tensor("w3q_ag", [16, FT, 128, 512], BF16, addr_space="Shared"),
             nc.dram_tensor("w2q_ag", [48, HT, 128, 512], BF16, addr_space="Shared")]
    ar_in = nc.dram_tensor("ar_in", [128, 4], F32)
    ar_out = nc.dram_tensor("ar_out", [128, 4], F32, addr_space="Shared")
    cin = [nc.dram_tensor(f"cin{r}", [128, 32], F32) for r in range(ROUNDS)]
    cout = [nc.dram_tensor(f"cout{r}", [1024, 32], F32, addr_space="Shared") for r in range(ROUNDS)]

    arena = nc.alloc_sbuf_tensor("arena", [128, A_END], F32)
    wslot = nc.alloc_sbuf_tensor("wslot", [128, 6144], BF16)
    stats = nc.alloc_sbuf_tensor("stats", [128, 64], F32)
    summ = nc.alloc_sbuf_tensor("summ", [128, 32], F32)
    s_all = nc.alloc_sbuf_tensor("s_all", [128, 256], F32)
    s_pref = nc.alloc_sbuf_tensor("s_pref", [128, 16], F32)
    s_in = nc.alloc_sbuf_tensor("s_in", [128, 16], F32)
    t16a = nc.alloc_sbuf_tensor("t16a", [128, 16], F32)
    t16b = nc.alloc_sbuf_tensor("t16b", [128, 16], F32)
    maskm = nc.alloc_sbuf_tensor("maskm", [128, 8], F32)
    maskn = nc.alloc_sbuf_tensor("maskn", [128, 8], F32)
    initb = nc.alloc_sbuf_tensor("initb", [128, 16], F32)
    partials = nc.alloc_sbuf_tensor("partials", [128, 4], F32)
    ones_f32 = nc.alloc_sbuf_tensor("ones_f32", [128, 128], F32)
    means = nc.alloc_sbuf_tensor("means", [128, 4], F32)
    mcl = nc.alloc_sbuf_tensor("mcl", [128, 4], F32)
    swq = nc.alloc_sbuf_tensor("swq", [128, 4], F32)
    zcol = nc.alloc_sbuf_tensor("zcol", [128, 1], F32)
    ones_bf = nc.alloc_sbuf_tensor("ones_bf", [128, 128], BF16)
    idn = nc.alloc_sbuf_tensor("idn_s", [128, 128], BF16)

    pb = [nc.alloc_psum_tensor(f"pb{i}", [128, 512], F32) for i in range(8)]

    sMM = Sem("sMM")
    sEV = Sem("sEV")
    sES = Sem("sES")
    sV = Sem("sV")
    sS = Sem("sS")
    sCC = Sem("sCC")
    sK = Sem("sK")
    sX = Sem("sX")
    sP = [Sem("sP0"), Sem("sP1")]
    sSC = [Sem(f"sSC{m}") for m in range(2)]
    sQW3 = [Sem(f"sQW{i}") for i in range(3)]
    sW2 = [Sem("sW0"), Sem("sW1"), Sem("sW2s")]
    sST = Sem("sST")
    sGQ = Sem("sGQ")
    sAR = Sem("sAR")
    sSA = Sem("sSA")
    sCI = Sem("sCI")
    sO2 = [Sem("sO0"), Sem("sO1")]
    sems = [sMM, sEV, sES, sV, sS, sCC, sK, sX] + sP + sSC + sW2 + sQW3 + [
        sST, sGQ, sAR, sSA, sCI] + sO2

    def eqv(off, n):
        return arena[:, off:off + n]

    def bfv(off, neq):
        return arena[:, off:off + neq].bitcast(BF16)

    xV = eqv(A_X, 8192)
    y2stage = eqv(A_X, 2048)
    ln2scr = eqv(A_X + 2048, 2048)
    aV = bfv(A_X, 11264)
    hqstage = eqv(11264, 2816)
    sgV = bfv(A_SG, 4096)
    yTV = bfv(A_YT, 4096)
    y2qTV = bfv(A_YT, 4096)
    fV = bfv(A_F, 4096)
    ln1scr = eqv(A_F, 2048)
    out1V = eqv(A_F, 8192)
    yV = bfv(A_SI, 4096)
    siV = bfv(A_SI, 4096)
    stV = bfv(A_ST, 4096)
    cV = bfv(A_C, 4096)
    qV = bfv(A_Q, 4096)
    y2qV = bfv(A_Q, 4096)
    bV = bfv(A_ST, 11264)
    svV = bfv(A_SV, 4096)
    uV = bfv(A_SV, 4096)
    zV = bfv(A_SV, 4096)
    o2st = [eqv(A_SV + i * 512, 512) for i in range(2)]
    pstripe = eqv(P_STRIPE, 1536)
    qbf = bfv(P_QBF, 768)
    tst = [bfv(P_TST + i * 256, 256) for i in range(3)]
    slabv = bfv(A_SLAB, 2048)

    P = Plan()
    em = P.em

    def dma(eng, o, i, waits=(), inc=None, tag="", **kw):
        em(eng, (lambda o=o, i=i, kw=kw: (lambda E: E.dma_start(out=o, in_=i, **kw)))(),
           waits=waits, inc=inc, tag=tag)

    slot_ev = [(0, 0), (0, 0)]
    slot_ctr = [0]

    def acquire_slot():
        s = slot_ctr[0] % 2
        slot_ctr[0] += 1
        return s, slot_ev[s]

    def release_slot(s):
        slot_ev[s] = (sES.n, sEV.n)

    def evw(ev):
        return [(sES, ev[0]), (sEV, ev[1])]

    wfree = [0, 0, 0]
    wseq = [0]

    def wchunk_load(srcs, extra_waits=(), tag=""):
        si = wseq[0] % 3
        wseq[0] += 1
        first = True
        n = len(srcs)
        for idx, (lo, src) in enumerate(srcs):
            w = list(extra_waits) + ([(sMM, wfree[si])] if first else [])
            width = 2048 if n == 1 else 512
            dma("sync", wslot[:, si * 2048 + lo: si * 2048 + lo + width],
                src, waits=w, inc=(sW2[si], 16), tag=tag)
            first = False
        return si, sW2[si].n

    def wchunk_done(si):
        wfree[si] = sMM.n

    em("vector", lambda E: E.memset(zcol[:], 0.0), tag="z0")
    em("vector", lambda E: E.memset(ones_bf[:], 1.0), tag="ones")
    em("vector", lambda E: E.memset(ones_f32[:], 1.0), tag="ones32")
    dma("gpsimd", idn[:], idn_in[:, :], inc=(sK, 16), tag="idn")
    dma("gpsimd", maskm[:], mm_in[:, :], inc=(sK, 16), tag="mm")
    dma("gpsimd", maskn[:], mn_in[:, :], inc=(sK, 16), tag="mn")
    dma("gpsimd", initb[:], init_in[:, :], inc=(sK, 16), tag="init")
    g_consts = sK.n
    em("vector", lambda E: E.tensor_copy(s_pref[:], initb[:]),
       waits=[(sK, g_consts)], inc=(sV, 1), tag="spref0")

    ffn2ev = {-1: 0}
    xfree_mm = [0] * (ROUNDS + 1)
    xload_g = [0] * ROUNDS

    def load_x(r, eng="gpsimd"):
        dma(eng, xV.rearrange("p (t d) -> p t d", t=4),
            xin[r * TS:(r + 1) * TS, :].rearrange("(t p) d -> p t d", p=128),
            waits=[(sMM, xfree_mm[r])], inc=(sX, 16), tag=f"x{r}")
        xload_g[r] = sX.n

    load_x(0, eng="scalar")

    slabh = [bfv(A_SLAB, 2048).rearrange("p (t d) -> p t d", d=512),
             bfv(A_SLAB + 2048, 2048).rearrange("p (t d) -> p t d", d=512)]
    prev_writes_w = [None, None]
    ag_w_cc = {}
    for j in range(4):
        src_j = wmine[:, j * 512:(j + 1) * 512].rearrange("(t p) d -> p t d", p=128)
        for h in range(2):
            dma("gpsimd", slabh[h][:, :, :], src_j[:, 8 * h:8 * h + 8, :],
                waits=[prev_writes_w[h]] if prev_writes_w[h] else [],
                inc=(sP[h], 16), tag=f"slab{j}.{h}")
            lg = sP[h].n
            for q in (2 * h, 2 * h + 1):
                dma("sync", wri[j][q][:, :].rearrange("p (t d) -> p t d", d=512),
                    slabh[h][:, (q % 2) * 4:(q % 2) * 4 + 4, :],
                    waits=[(sP[h], lg)], inc=(sSC[h], 16), tag=f"scw{j}.{q}")
            prev_writes_w[h] = (sSC[h], sSC[h].n)
        em("gpsimd", (lambda j=j: (lambda E: E.collective_compute(
            "AllGather", ALU.bypass, replica_groups=[list(range(NCORES))],
            ins=[wri[j].ap().opt()], outs=[wall[j].ap().opt()])))(),
            waits=[(s, s.n) for s in sSC], inc=(sCC, 1), tag=f"AGw{j}")
        ag_w_cc[j] = sCC.n
    retile_done = [(s, s.n) for s in sSC]
    _FFNQ_PROLOGUE = [True]

    fst = {"qv": 0}
    tstseq = [0]

    def ffnq_stripes(mi):
        wsrc = [w1s, w3s, w2s][mi]
        if mi < 2:
            return [(ft, wsrc[ft * 512:(ft + 1) * 512, :]
                     .rearrange("(g p) d -> p g d", p=128), 1024) for ft in range(FT)]
        res = []
        for st in range(4):
            for hh in range(2):
                res.append(((st, hh),
                            wsrc[st * 512 + hh * 256: st * 512 + (hh + 1) * 256, :]
                            .rearrange("(g p) d -> p g d", p=128), 1536))
        return res

    phbuf = [eqv(P_STRIPE, 1536), eqv(A_SLAB, 1536)]
    phdisc = eqv(P_QBF, 1536)
    phsum = eqv(11264, 32)
    phred = {}

    def phA_works():
        k = 0
        for mi in range(3):
            for sid, src, ncols in ffnq_stripes(mi):
                yield (k, mi, src, ncols)
                k += 1

    def phA_emit(work):
        k, mi, src, ncols = work
        b = k % 2
        gdim = ncols // 256 if mi == 2 else 4
        w = []
        if k >= 2:
            w.append((sS, phred[k - 2]))
        if k == 1:
            w += list(retile_done)
        dma("scalar", phbuf[b][:, 0:ncols].rearrange("p (g d) -> p g d", g=gdim),
            src, waits=w, inc=(sST, 16), tag=f"pha{k}")
        g = sST.n
        em("scalar", (lambda ncols=ncols, b=b, k=k: (lambda E: E.activation(
            phdisc[:, 0:ncols], phbuf[b][:, 0:ncols], AF.Abs,
            accum_out=phsum[:, k:k + 1])))(),
            waits=[(sST, g)], inc=(sS, 1), tag=f"phr{k}")
        phred[k] = sS.n
        if k == 29:
            fst["phs"] = sS.n

    def phA_finale():
        em("vector", lambda E: E.memset(partials[:, 3:4], 0.0), tag="pz3")
        for mi, (c0, c1) in enumerate([(0, 11), (11, 22), (22, 30)]):
            em("vector", (lambda mi=mi, c0=c0, c1=c1: (lambda E: E.reduce_sum(
                partials[:, mi:mi + 1], phsum[:, c0:c1], axis=AX.X)))(),
                waits=([(sS, fst["phs"])] if mi == 0 else []),
                inc=((sV, 1) if mi == 2 else None), tag=f"phf{mi}")
        fin_v = sV.n
        dma("gpsimd", ar_in.ap()[:, :], partials[:], waits=[(sV, fin_v)],
            inc=(sAR, 16), tag="arin")
        em("gpsimd", lambda E: E.collective_compute(
            "AllReduce", ALU.add, replica_groups=[list(range(NCORES))],
            ins=[ar_in.ap().opt()], outs=[ar_out.ap().opt()]),
            waits=[(sAR, sAR.n)], inc=(sCC, 1), tag="AR")
        dma("gpsimd", partials[:], ar_out.ap()[:, :], waits=[(sCC, sCC.n)],
            inc=(sAR, 16), tag="arout")
        fst["ar_g"] = sAR.n

    def ffnq_means():
        em("vector", lambda E: E.engine_nop(),
           waits=[(sAR, fst["ar_g"])], inc=(sV, 1), tag="pcast")
        vv = sV.n
        s, ev = acquire_slot()
        em("tensor", (lambda s=s: (lambda E: E.matmul(
            pb[s * 4][:, 0:4], ones_f32[:], partials[:], start=True, stop=True)))(),
            waits=evw(ev) + [(sV, vv)], inc=(sMM, 1), tag="meanMM")
        mmv = sMM.n
        em("vector", (lambda s=s: (lambda E: E.tensor_scalar_mul(
            means[:], pb[s * 4][:, 0:4], 1.0 / NW)))(),
            waits=[(sMM, mmv)], inc=(sEV, 1), tag="means")
        release_slot(s)
        em("vector", lambda E: E.tensor_scalar_max(mcl[:], means[:], EPS_Q), tag="mcl")
        em("vector", lambda E: E.reciprocal(swq[:], mcl[:]), inc=(sV, 1), tag="swq")
        fst["means_v"] = sV.n

    def ffnq_quant_gen():
        for mi in range(3):
            for item in ffnq_quant_stripes(mi):
                yield item
        yield None

    def ffnq_quant_stripes(mi):
        mv = fst["means_v"]
        dst = wq_in[mi]
        for sid, src, ncols in ffnq_stripes(mi):
            yield (mi, sid, src, ncols, dst, mv)

    def ffnq_emit_stripe(work):
        mi, sid, src, ncols, dst, mv = work
        if True:
            gdim = ncols // 256 if mi == 2 else 4
            dma("sync", pstripe[:, 0:ncols].rearrange("p (g d) -> p g d", g=gdim),
                src, waits=[(sV, fst["qv"]), (sS, fst.get("phs", 0))],
                inc=(sST, 16), tag=f"qb{mi}")
            g = sST.n
            em("vector", (lambda ncols=ncols, mi=mi: (lambda E: E.tensor_scalar(
                pstripe[:, 0:ncols], pstripe[:, 0:ncols], swq[:, mi:mi + 1], MAGIC,
                ALU.mult, ALU.add)))(),
                waits=[(sST, g), (sV, mv), (sES, fst.get("qes", 0))],
                tag=f"qr1{mi}")
            em("vector", (lambda ncols=ncols: (lambda E: E.tensor_scalar(
                pstripe[:, 0:ncols], pstripe[:, 0:ncols], -MAGIC, -1.0,
                ALU.add, ALU.max)))(), tag=f"qr2{mi}")
            em("vector", (lambda ncols=ncols: (lambda E: E.tensor_scalar_min(
                qbf[:, 0:ncols], pstripe[:, 0:ncols], 1.0)))(),
                inc=(sV, 1), tag=f"qr3{mi}")
            fst["qv"] = sV.n
            qv = sV.n
            if mi < 2:
                ft = sid
                groups = [(db, [qbf[:, g4 * 256 + db * 128: g4 * 256 + (db + 1) * 128]
                                for g4 in range(4)], 512, dst[db, ft][:, :]) for db in range(2)]
            else:
                st, hh = sid
                groups = [(fb, [qbf[:, g2 * 768 + fb * 128: g2 * 768 + (fb + 1) * 128]
                                for g2 in range(2)], 256,
                           dst[fb, st][:, hh * 256:(hh + 1) * 256]) for fb in range(6)]
            for _, blocks, nbf, dstap in groups:
                s, ev = acquire_slot()
                bank_bf = pb[s * 4][:, 0:512].bitcast(BF16)
                first = True
                for i, blk in enumerate(blocks):
                    w = (evw(ev) + [(sV, qv)]) if first else []
                    em("tensor", (lambda i=i, blk=blk, bank_bf=bank_bf:
                        (lambda E: E.transpose(
                            bank_bf[:, i * 128:(i + 1) * 128], blk, idn[:])))(),
                        waits=w, inc=((sMM, 1) if i == len(blocks) - 1 else None),
                        tag=f"qT{mi}")
                    first = False
                mmv = sMM.n
                ti = tstseq[0] % 3
                tstseq[0] += 1
                em("scalar", (lambda nbf=nbf, bank_bf=bank_bf, ti=ti: (lambda E: E.activation(
                    tst[ti][:, 0:nbf], bank_bf[:, 0:nbf], AF.Copy)))(),
                    waits=[(sMM, mmv), (sQW3[ti], fst.get(f"tw{ti}", 0))],
                    inc=(sES, 1), tag=f"qTe{mi}")
                release_slot(s)
                ev2 = sES.n
                dma("gpsimd", dstap, tst[ti][:, 0:nbf],
                    waits=[(sES, ev2)], inc=(sQW3[ti], 16), tag=f"qw{mi}")
                fst[f"tw{ti}"] = sQW3[ti].n
        fst["qes"] = sES.n
        nst = FT if mi < 2 else 8
        fst[f"nst{mi}"] = fst.get(f"nst{mi}", 0) + 1
        if fst[f"nst{mi}"] == nst:
            fst[f"agq_ready{mi}"] = [(q, q.n) for q in sQW3]


    def ln_stats(r, tiles, discards, pre_s, pre_v, tag):
        for tb in range(4):
            em("vector", (lambda tb=tb: (lambda E: E.reduce_sum(
                stats[:, tb:tb + 1], tiles[tb], axis=AX.X)))(),
                waits=list(pre_v), tag=f"s1{tag}.{tb}")
            em("scalar", (lambda tb=tb: (lambda E: E.activation(
                discards[tb], tiles[tb], AF.Square,
                accum_out=stats[:, 4 + tb:5 + tb])))(),
                waits=list(pre_s), inc=(sS, 1), tag=f"s2{tag}.{tb}")
        sq_s = sS.n
        em("vector", lambda E: E.tensor_scalar_mul(stats[:, 8:12], stats[:, 0:4], 1.0 / H),
           tag=f"mu{tag}")
        em("vector", lambda E: E.tensor_scalar_mul(stats[:, 12:16], stats[:, 4:8], 1.0 / H),
           waits=[(sS, sq_s)], tag=f"e2{tag}")
        em("vector", lambda E: E.tensor_mul(stats[:, 56:60], stats[:, 8:12], stats[:, 8:12]),
           tag=f"msq{tag}")
        em("vector", lambda E: E.tensor_sub(stats[:, 12:16], stats[:, 12:16], stats[:, 56:60]),
           tag=f"var{tag}")
        em("vector", lambda E: E.tensor_scalar_add(stats[:, 12:16], stats[:, 12:16], EPS_LN),
           tag=f"ve{tag}")
        em("vector", lambda E: E.reciprocal(stats[:, 12:16], stats[:, 12:16]),
           inc=(sV, 1), tag=f"rv{tag}")
        rv_v = sV.n
        em("scalar", lambda E: E.activation(stats[:, 16:20], stats[:, 12:16], AF.Sqrt),
           waits=[(sV, rv_v)], inc=(sS, 1), tag=f"sqrt{tag}")
        sq2_s = sS.n
        em("vector", lambda E: E.tensor_scalar_mul(stats[:, 20:24], stats[:, 8:12], -1.0),
           tag=f"nm{tag}")
        em("vector", lambda E: E.tensor_mul(stats[:, 20:24], stats[:, 20:24], stats[:, 16:20]),
           waits=[(sS, sq2_s)], inc=(sV, 1), tag=f"nb{tag}")
        return sS.n, sV.n

    def transpose_batch3(blocks, dst3, wait_list, tag):
        s, ev = acquire_slot()
        bank_bf = pb[s * 4][:, 0:512].bitcast(BF16)
        n = len(blocks)
        first = True
        for i, blk in enumerate(blocks):
            w = (evw(ev) + list(wait_list)) if first else []
            em("tensor", (lambda i=i, blk=blk, bank_bf=bank_bf: (lambda E: E.transpose(
                bank_bf[:, i * 128:(i + 1) * 128], blk, idn[:])))(),
                waits=w, inc=((sMM, 1) if i == n - 1 else None),
                tag=f"T{tag}")
            first = False
        mmv = sMM.n
        bsrc = bank_bf.rearrange("p (i x) -> p i x", x=128)
        em("scalar", (lambda dst3=dst3, bsrc=bsrc, n=n:
            (lambda E: E.activation(dst3, bsrc[:, 0:n, :], AF.Copy)))(),
            waits=[(sMM, mmv)], inc=(sES, 1), tag=f"Te{tag}")
        release_slot(s)

    def transpose_batch(blocks, dst, dst_lo, nbf, wait_list, tag):
        s, ev = acquire_slot()
        bank_bf = pb[s * 4][:, 0:512].bitcast(BF16)
        first = True
        for i, blk in enumerate(blocks):
            w = (evw(ev) + list(wait_list)) if first else []
            em("tensor", (lambda i=i, blk=blk: (lambda E: E.transpose(
                bank_bf[:, i * 128:(i + 1) * 128], blk, idn[:])))(),
                waits=w, inc=((sMM, 1) if i == len(blocks) - 1 else None),
                tag=f"T{tag}")
            first = False
        mmv = sMM.n
        em("scalar", (lambda dst=dst, dst_lo=dst_lo, nbf=nbf, bank_bf=bank_bf:
            (lambda E: E.activation(dst[:, dst_lo:dst_lo + nbf],
                                    bank_bf[:, 0:nbf], AF.Copy)))(),
            waits=[(sMM, mmv)], inc=(sES, 1), tag=f"Te{tag}")
        release_slot(s)

    proj_cfg = [(fV, AF.Tanh), (siV, AF.Sigmoid), (svV, AF.Silu),
                (qV, AF.Copy), (uV, AF.Copy), (sgV, AF.Silu)]
    o2led = [0, 0]
    out_total = [()]

    def emit_round(r):
        x_tiles = [xV[:, tb * 2048:(tb + 1) * 2048] for tb in range(4)]
        pre = [(sX, xload_g[r]), (sEV, ffn2ev[r - 1])]
        ln1_disc = [out1V[:, tb * 2048:(tb + 1) * 2048] for tb in range(4)]
        s_ln1, v_ln1 = ln_stats(r, x_tiles, ln1_disc, pre, pre, f"L1.{r}")
        for tb in range(4):
            em("vector", (lambda tb=tb: (lambda E: E.tensor_scalar(
                yV[:, tb * 2048:(tb + 1) * 2048], x_tiles[tb],
                stats[:, 16 + tb:17 + tb], stats[:, 20 + tb:21 + tb],
                ALU.mult, ALU.add)))(),
                inc=((sV, 1) if tb == 3 else None), tag=f"y{r}.{tb}")
        y_s = sV.n

        for hb2 in range(8):
            blocks = [yV[:, tb * 2048 + (hb2 * 2 + hbl) * 128:
                         tb * 2048 + (hb2 * 2 + hbl) * 128 + 128]
                      for hbl in range(2) for tb in range(4)]
            transpose_batch(blocks, yTV, hb2 * 1024, 1024,
                            [(sV, y_s), (sK, g_consts)], f"y{r}.{hb2}")
        yT_e = sES.n

        proj_ev = {}
        g_v_box = [0]
        if r == 0:
            qgen = ffnq_quant_gen()
            qdone = [False]
            phgen = phA_works()
            phdone = [False]
        for m6 in range(6):
            dst, func = proj_cfg[m6]
            extra_ev = []
            if m6 in (0, 1):
                extra_ev = [(sEV, ffn2ev[r - 1])]
            if m6 in (2, 4):
                extra_ev = list(out_total[0])
            if m6 == 4:
                extra_ev.append((sV, g_v_box[0]))
            if m6 == 5 and r == 0:
                extra_ev = list(retile_done)
            for rbq in range(4):
                s, ev = acquire_slot()
                group_first = True
                for hbq in range(4):
                    si, wv = wchunk_load([(0, wall[rbq][m6, hbq][:, :])],
                                         extra_waits=[(sCC, ag_w_cc[rbq])],
                                         tag=f"pc{r}.{m6}.{rbq}.{hbq}")
                    for hbl in range(4):
                        hb = hbq * 4 + hbl
                        for rbi in range(4):
                            w = []
                            if group_first:
                                w = evw(ev) + [(sES, yT_e)]
                            if hbl == 0 and rbi == 0:
                                w.append((sW2[si], wv))
                            last_of_chunk = (hbl == 3 and rbi == 3)
                            em("tensor", (lambda s=s, rbi=rbi, si=si, hbl=hbl, hb=hb:
                                (lambda E: E.matmul(
                                    pb[s * 4 + rbi][:],
                                    wslot[:, si * 2048 + hbl * 512 + rbi * 128:
                                          si * 2048 + hbl * 512 + (rbi + 1) * 128],
                                    yTV[:, hb * 512:(hb + 1) * 512],
                                    start=(hb == 0), stop=(hb == 15))))(),
                                waits=w, inc=((sMM, 1) if last_of_chunk else None),
                                tag=f"P{r}.{m6}.{rbq}.{hbq}")
                            group_first = False
                    wchunk_done(si)
                mmv = sMM.n
                for rbi in range(4):
                    rb = rbq * 4 + rbi
                    em("scalar", (lambda dst=dst, func=func, rb=rb, s=s, rbi=rbi:
                        (lambda E: E.activation(
                            dst[:, rb * 512:(rb + 1) * 512], pb[s * 4 + rbi][:], func)
                         if func == AF.Copy else E.activation(
                            dst[:, rb * 512:(rb + 1) * 512], pb[s * 4 + rbi][:], func,
                            bias=zcol[:, 0:1])))(),
                        waits=[(sMM, mmv)] + extra_ev, inc=(sES, 1),
                        tag=f"Pe{r}.{m6}.{rbq}.{rbi}")
                release_slot(s)
                if r == 0 and m6 <= 1 and not phdone[0]:
                    for _ in range(4):
                        pw = next(phgen, None)
                        if pw is None:
                            phdone[0] = True
                            break
                        phA_emit(pw)
                if r == 0 and m6 >= 3 and not qdone[0]:
                    for _ in range(3):
                        wk = next(qgen)
                        if wk is None:
                            qdone[0] = True
                            break
                        ffnq_emit_stripe(wk)
            proj_ev[m6] = sES.n
            if m6 == 2:
                em("vector", lambda E: E.tensor_mul(
                    siV[:, 0:8192], siV[:, 0:8192], svV[:, 0:8192]),
                    waits=[(sES, proj_ev[2])], inc=(sV, 1), tag=f"g{r}")
                g_v_box[0] = sV.n
                for rg in range(16):
                    em("vector", (lambda rg=rg: (lambda E: E.tensor_tensor_scan(
                        stV[:, rg * 512:(rg + 1) * 512], fV[:, rg * 512:(rg + 1) * 512],
                        siV[:, rg * 512:(rg + 1) * 512], 0.0, ALU.mult, ALU.add)))(),
                        waits=[(sV, g_v_box[0])], tag=f"scS{r}.{rg}")
                for rg in range(16):
                    em("vector", (lambda rg=rg: (lambda E: E.tensor_tensor_scan(
                        cV[:, rg * 512:(rg + 1) * 512], fV[:, rg * 512:(rg + 1) * 512],
                        fV[:, rg * 512:(rg + 1) * 512], 1.0, ALU.mult, ALU.bypass)))(),
                        tag=f"scC{r}.{rg}")
                cV3 = cV.rearrange("p (rg t) -> p rg t", t=512)
                sV3 = stV.rearrange("p (rg t) -> p rg t", t=512)
                em("vector", lambda E: E.tensor_copy(
                    summ[:, 0:16], cV3[:, :, 511:512].squeeze(2)), tag=f"sP{r}")
                em("vector", lambda E: E.tensor_copy(
                    summ[:, 16:32], sV3[:, :, 511:512].squeeze(2)),
                    inc=(sV, 1), tag=f"sQ{r}")
                summ_v = sV.n
                dma("gpsimd", cin[r].ap()[:, :], summ[:],
                    waits=[(sV, summ_v)], inc=(sCI, 16), tag=f"cin{r}")
                em("gpsimd", (lambda r=r: (lambda E: E.collective_compute(
                    "AllGather", ALU.bypass, replica_groups=[list(range(NCORES))],
                    ins=[cin[r].ap().opt()], outs=[cout[r].ap().opt()])))(),
                    waits=[(sCI, sCI.n)], inc=(sCC, 1), tag=f"AGc{r}")
                dma("gpsimd", s_all[:].rearrange("p (j f) -> p j f", f=32),
                    cout[r].ap().rearrange("(j p) f -> p j f", p=128),
                    waits=[(sCC, sCC.n)], inc=(sSA, 16), tag=f"sall{r}")
                sall_g = sSA.n
            if r == 0 and m6 == 1:
                while not phdone[0]:
                    pw = next(phgen, None)
                    if pw is None:
                        phdone[0] = True
                        break
                    phA_emit(pw)
                phA_finale()
                ffnq_means()

        if r == 0:
            while not qdone[0]:
                wk = next(qgen)
                if wk is None:
                    qdone[0] = True
                    break
                ffnq_emit_stripe(wk)
            for mi in range(3):
                em("gpsimd", (lambda mi=mi: (lambda E: E.collective_compute(
                    "AllGather", ALU.bypass, replica_groups=[list(range(NCORES))],
                    ins=[wq_in[mi].ap().opt()], outs=[wq_ag[mi].ap().opt()])))(),
                    waits=list(fst[f"agq_ready{mi}"]), inc=(sCC, 1), tag=f"AGq{mi}")
                fst[f"ag{mi}_cc"] = sCC.n

        if r == 0:
            em("vector", lambda E: E.tensor_mul(
                uV[:, 0:8192], uV[:, 0:8192], sgV[:, 0:8192]),
                waits=[(sES, proj_ev[5])], inc=(sV, 1), tag=f"z{r}")
            z_v = sV.n

        em("vector", lambda E: E.tensor_copy(s_in[:], s_pref[:]),
           waits=[(sSA, sall_g)], tag=f"fi{r}")
        for j in range(8):
            Pj = s_all[:, j * 32:j * 32 + 16]
            Qj = s_all[:, j * 32 + 16:j * 32 + 32]
            em("vector", (lambda Pj=Pj, j=j: (lambda E: E.tensor_scalar(
                t16a[:], Pj, maskm[:, j:j + 1], maskn[:, j:j + 1], ALU.mult, ALU.add)))(),
                tag=f"f1.{j}")
            em("vector", lambda E: E.tensor_mul(s_in[:], s_in[:], t16a[:]), tag=f"f2.{j}")
            em("vector", (lambda Qj=Qj, j=j: (lambda E: E.tensor_scalar_mul(
                t16b[:], Qj, maskm[:, j:j + 1])))(), tag=f"f3.{j}")
            em("vector", lambda E: E.tensor_add(s_in[:], s_in[:], t16b[:]), tag=f"f4.{j}")
        for j in range(8):
            Pj = s_all[:, j * 32:j * 32 + 16]
            Qj = s_all[:, j * 32 + 16:j * 32 + 32]
            em("vector", (lambda Pj=Pj: (lambda E: E.tensor_mul(s_pref[:], s_pref[:], Pj)))(),
               tag=f"f5.{j}")
            em("vector", (lambda Qj=Qj: (lambda E: E.tensor_add(s_pref[:], s_pref[:], Qj)))(),
               tag=f"f6.{j}")
        for rg in range(16):
            em("vector", (lambda rg=rg: (lambda E: E.scalar_tensor_tensor(
                cV[:, rg * 512:(rg + 1) * 512], cV[:, rg * 512:(rg + 1) * 512],
                s_in[:, rg:rg + 1], stV[:, rg * 512:(rg + 1) * 512],
                ALU.mult, ALU.add)))(), tag=f"co{r}.{rg}")
        em("vector", lambda E: E.tensor_mul(stV[:, 0:8192], qV[:, 0:8192], cV[:, 0:8192]),
           waits=[(sES, proj_ev[3])], inc=(sV, 1), tag=f"tmp{r}")
        tmp_v = sV.n
        em("scalar", lambda E: E.activation(qV[:, 0:8192], stV[:, 0:8192], AF.Silu,
                                            bias=zcol[:, 0:1]),
           waits=[(sV, tmp_v)], inc=(sS, 1), tag=f"ro{r}")
        ro_s = sS.n
        if r > 0:
            em("vector", lambda E: E.tensor_mul(
                uV[:, 0:8192], uV[:, 0:8192], sgV[:, 0:8192]),
                waits=[(sES, proj_ev[5])], inc=(sV, 1), tag=f"z{r}")
            z_v = sV.n
        em("vector", lambda E: E.memset(stats[:, 0:8], 0.0), tag=f"l2z{r}")
        sqv_led = {}
        for ht in range(4):
            s, ev = acquire_slot()
            group_first = True
            for mseq, m2 in enumerate((1, 0)):
                stat_src = qV if m2 == 0 else zV
                for kq in range(4):
                    si, wv = wchunk_load([(0, wall[ht][6 + m2, kq][:, :])],
                                         extra_waits=[(sCC, ag_w_cc[ht])],
                                         tag=f"mc{r}.{ht}.{m2}.{kq}")
                    for kbl in range(4):
                        kb = kq * 4 + kbl
                        for tbi in range(4):
                            w = []
                            if group_first:
                                w = evw(ev) + [(sV, z_v)]
                            if mseq == 1 and kq == 0 and kbl == 0 and tbi == 0:
                                w.append((sS, ro_s))
                            if kbl == 0 and tbi == 0:
                                w.append((sW2[si], wv))
                            last_of_chunk = (kbl == 3 and tbi == 3)
                            em("tensor", (lambda s=s, tbi=tbi, si=si, kbl=kbl, kb=kb,
                                          stat_src=stat_src, mseq=mseq: (lambda E: E.matmul(
                                pb[s * 4 + tbi][:],
                                stat_src[:, kb * 512 + tbi * 128: kb * 512 + (tbi + 1) * 128],
                                wslot[:, si * 2048 + kbl * 512: si * 2048 + (kbl + 1) * 512],
                                start=(mseq == 0 and kb == 0), stop=(mseq == 1 and kb == 15))))(),
                                waits=w, inc=((sMM, 1) if last_of_chunk else None),
                                tag=f"M{r}.{ht}.{m2}.{kq}")
                            group_first = False
                    wchunk_done(si)
            mmv = sMM.n
            for tbi in range(4):
                em("vector", (lambda s=s, tbi=tbi, ht=ht: (lambda E: E.tensor_tensor(
                    out1V[:, tbi * 2048 + ht * 512: tbi * 2048 + (ht + 1) * 512],
                    pb[s * 4 + tbi][:],
                    xV[:, tbi * 2048 + ht * 512: tbi * 2048 + (ht + 1) * 512],
                    ALU.add)))(),
                    waits=[(sMM, mmv)], inc=(sEV, 1), tag=f"Me{r}.{ht}.{tbi}")
            evht = sEV.n
            for tbi in range(4):
                blk = out1V[:, tbi * 2048 + ht * 512: tbi * 2048 + (ht + 1) * 512]
                em("vector", (lambda blk=blk: (lambda E: E.reduce_sum(
                    stats[:, 59:60], blk, axis=AX.X)))(), tag=f"l2a{r}.{ht}.{tbi}")
                em("vector", (lambda tbi=tbi: (lambda E: E.tensor_add(
                    stats[:, tbi:tbi + 1], stats[:, tbi:tbi + 1],
                    stats[:, 59:60])))(), tag=f"l2b{r}.{ht}.{tbi}")
                sw = [(sEV, evht)]
                if (ht, tbi) in sqv_led:
                    pass
                if ht > 0:
                    sw.append((sV, sqv_led[(ht - 1, tbi)]))
                em("scalar", (lambda blk=blk, tbi=tbi: (lambda E: E.activation(
                    xV[:, tbi * 2048:tbi * 2048 + 512], blk, AF.Square,
                    accum_out=stats[:, 60 + tbi:61 + tbi])))(),
                    waits=sw, inc=(sS, 1), tag=f"l2s{r}.{ht}.{tbi}")
                sqn = sS.n
                em("vector", (lambda tbi=tbi: (lambda E: E.tensor_add(
                    stats[:, 4 + tbi:5 + tbi], stats[:, 4 + tbi:5 + tbi],
                    stats[:, 60 + tbi:61 + tbi])))(),
                    waits=[(sS, sqn)], inc=(sV, 1), tag=f"l2c{r}.{ht}.{tbi}")
                sqv_led[(ht, tbi)] = sV.n
            release_slot(s)
        mix_ev = sEV.n

        o1_tiles = [out1V[:, tb * 2048:(tb + 1) * 2048] for tb in range(4)]
        em("vector", lambda E: E.tensor_scalar_mul(stats[:, 8:12], stats[:, 0:4],
                                                   1.0 / H), tag=f"muL2{r}")
        em("vector", lambda E: E.tensor_scalar_mul(stats[:, 12:16], stats[:, 4:8],
                                                   1.0 / H), tag=f"e2L2{r}")
        em("vector", lambda E: E.tensor_mul(stats[:, 56:60], stats[:, 8:12],
                                            stats[:, 8:12]), tag=f"msqL2{r}")
        em("vector", lambda E: E.tensor_sub(stats[:, 12:16], stats[:, 12:16],
                                            stats[:, 56:60]), tag=f"varL2{r}")
        em("vector", lambda E: E.tensor_scalar_add(stats[:, 12:16], stats[:, 12:16],
                                                   EPS_LN), tag=f"veL2{r}")
        em("vector", lambda E: E.reciprocal(stats[:, 12:16], stats[:, 12:16]),
           inc=(sV, 1), tag=f"rvL2{r}")
        rv2_v = sV.n
        em("scalar", lambda E: E.activation(stats[:, 16:20], stats[:, 12:16], AF.Sqrt),
           waits=[(sV, rv2_v)], inc=(sS, 1), tag=f"sqrtL2{r}")
        s_ln2 = sS.n
        em("vector", lambda E: E.tensor_scalar_mul(stats[:, 20:24], stats[:, 8:12],
                                                   -1.0), tag=f"nmL2{r}")
        em("vector", lambda E: E.tensor_mul(stats[:, 20:24], stats[:, 20:24],
                                            stats[:, 16:20]),
           waits=[(sS, s_ln2)], inc=(sV, 1), tag=f"nbL2{r}")
        rq2_tb = [0, 0, 0, 0]
        for tb in range(4):
            em("vector", (lambda tb=tb: (lambda E: E.tensor_scalar(
                y2stage, o1_tiles[tb],
                stats[:, 16 + tb:17 + tb], stats[:, 20 + tb:21 + tb],
                ALU.mult, ALU.add)))(), tag=f"y2.{r}.{tb}")
            em("vector", (lambda tb=tb: (lambda E: E.reduce_max(
                stats[:, 24 + tb:25 + tb], y2stage, axis=AX.X,
                apply_absolute_value=True)))(), tag=f"qm{r}.{tb}")
            em("vector", (lambda tb=tb: (lambda E: E.tensor_scalar_max(
                stats[:, 24 + tb:25 + tb], stats[:, 24 + tb:25 + tb], EPS_Q)))(),
                tag=f"qc{r}.{tb}")
            em("vector", (lambda tb=tb: (lambda E: E.tensor_scalar_mul(
                stats[:, 32 + tb:33 + tb], stats[:, 24 + tb:25 + tb], 1.0 / 127.0)))(),
                tag=f"rs{r}.{tb}")
            em("vector", (lambda tb=tb: (lambda E: E.reciprocal(
                stats[:, 28 + tb:29 + tb], stats[:, 24 + tb:25 + tb])))(),
                tag=f"r1{r}.{tb}")
            em("vector", (lambda tb=tb: (lambda E: E.tensor_scalar_mul(
                stats[:, 28 + tb:29 + tb], stats[:, 28 + tb:29 + tb], 127.0)))(),
                tag=f"r2{r}.{tb}")
            em("vector", (lambda tb=tb: (lambda E: E.tensor_scalar(
                ln2scr, y2stage, stats[:, 28 + tb:29 + tb], MAGIC, ALU.mult, ALU.add)))(),
                tag=f"rq1{r}.{tb}")
            em("vector", (lambda tb=tb: (lambda E: E.tensor_scalar_add(
                y2qV[:, tb * 2048:(tb + 1) * 2048], ln2scr, -MAGIC)))(),
                inc=(sV, 1), tag=f"rq2{r}.{tb}")
            rq2_tb[tb] = sV.n
        y2q_v = sV.n
        em("vector", lambda E: E.tensor_scalar(
            stats[:, 44:48], stats[:, 32:36], mcl[:, 0:1], None, ALU.mult),
            waits=[(sV, fst.get("means_v", 0))], tag=f"sc1{r}")
        em("vector", lambda E: E.tensor_scalar(
            stats[:, 48:52], stats[:, 32:36], mcl[:, 1:2], None, ALU.mult),
            inc=(sV, 1), tag=f"sc3{r}")
        sc_v = sV.n

        y2qT3 = y2qTV.rearrange("p (dg x) -> p dg x", x=512)
        for tb in range(4):
            for grp in range(2):
                blocks = [y2qV[:, tb * 2048 + (grp * 8 + i) * 128:
                              tb * 2048 + (grp * 8 + i) * 128 + 128] for i in range(8)]
                transpose_batch3(blocks,
                                 y2qT3[:, grp * 8:(grp + 1) * 8,
                                       tb * 128:(tb + 1) * 128],
                                 [(sV, rq2_tb[tb])], f"q{r}.{tb}.{grp}")
        y2qT_e = sES.n

        for mi, (dstV, func, sccol) in enumerate(
                [(aV, AF.Silu, 44), (bV, AF.Copy, 48)]):
            agw = [(sCC, fst.get("ag2_cc", 0))]
            for ft in range(FT):
                s, ev = acquire_slot()
                group_first = True
                for q in range(4):
                    srcs = [(i * 512, wq_ag[mi][4 * q + i, ft][:, :]) for i in range(4)]
                    si, wv = wchunk_load(srcs, extra_waits=agw, tag=f"fc{r}.{mi}.{ft}.{q}")
                    for dgl in range(4):
                        dgb = q * 4 + dgl
                        for tbi in range(4):
                            w = []
                            if group_first:
                                w = evw(ev) + [(sES, y2qT_e), (sV, sc_v)]
                            if dgl == 0 and tbi == 0:
                                w.append((sW2[si], wv))
                            last_of_chunk = (dgl == 3 and tbi == 3)
                            em("tensor", (lambda s=s, tbi=tbi, si=si, dgl=dgl, dgb=dgb:
                                (lambda E: E.matmul(
                                    pb[s * 4 + tbi][:],
                                    y2qTV[:, dgb * 512 + tbi * 128: dgb * 512 + (tbi + 1) * 128],
                                    wslot[:, si * 2048 + dgl * 512: si * 2048 + (dgl + 1) * 512],
                                    start=(dgb == 0), stop=(dgb == 15))))(),
                                waits=w, inc=((sMM, 1) if last_of_chunk else None),
                                tag=f"F1.{r}.{mi}.{ft}.{q}")
                            group_first = False
                    wchunk_done(si)
                mmv = sMM.n
                for tbi in range(4):
                    em("scalar", (lambda dstV=dstV, func=func, tbi=tbi, ft=ft, s=s,
                                  sccol=sccol: (lambda E: E.activation(
                        dstV[:, tbi * FH + ft * 512: tbi * FH + ft * 512 + 512],
                        pb[s * 4 + tbi][:], func,
                        scale=stats[:, sccol + tbi:sccol + tbi + 1])
                        if func == AF.Copy else E.activation(
                        dstV[:, tbi * FH + ft * 512: tbi * FH + ft * 512 + 512],
                        pb[s * 4 + tbi][:], func,
                        bias=zcol[:, 0:1], scale=stats[:, sccol + tbi:sccol + tbi + 1])))(),
                        waits=[(sMM, mmv)], inc=(sES, 1), tag=f"F1e{r}.{mi}.{ft}.{tbi}")
                release_slot(s)
                if mi == 1:
                    if ft == 0:
                        em("vector", lambda E: E.memset(stats[:, 24:28], 0.0),
                           tag=f"hz{r}")
                    ev_g = sES.n
                    for tbi in range(4):
                        lo = tbi * FH + ft * 512
                        em("vector", (lambda lo=lo: (lambda E: E.tensor_mul(
                            aV[:, lo:lo + 512], aV[:, lo:lo + 512],
                            bV[:, lo:lo + 512])))(),
                            waits=([(sES, ev_g)] if tbi == 0 else []),
                            tag=f"h{r}.{ft}.{tbi}")
                        em("vector", (lambda lo=lo: (lambda E: E.reduce_max(
                            stats[:, 60:61], aV[:, lo:lo + 512], axis=AX.X,
                            apply_absolute_value=True)))(), tag=f"hm{r}.{ft}.{tbi}")
                        em("vector", (lambda tbi=tbi: (lambda E: E.tensor_max(
                            stats[:, 24 + tbi:25 + tbi], stats[:, 24 + tbi:25 + tbi],
                            stats[:, 60:61])))(), tag=f"hx{r}.{ft}.{tbi}")
        ffn1_ev = sES.n

        hq_tb = [0, 0, 0, 0]
        st_v = [0, 0, 0, 0]
        for tb in range(4):
            em("vector", (lambda tb=tb: (lambda E: E.tensor_scalar_max(
                stats[:, 24 + tb:25 + tb], stats[:, 24 + tb:25 + tb], EPS_Q)))(),
                tag=f"hc{r}.{tb}")
            em("vector", (lambda tb=tb: (lambda E: E.tensor_scalar_mul(
                stats[:, 40 + tb:41 + tb], stats[:, 24 + tb:25 + tb], 1.0 / 127.0)))(),
                tag=f"hr{r}.{tb}")
            em("vector", (lambda tb=tb: (lambda E: E.reciprocal(
                stats[:, 36 + tb:37 + tb], stats[:, 24 + tb:25 + tb])))(),
                tag=f"h1{r}.{tb}")
            em("vector", (lambda tb=tb: (lambda E: E.tensor_scalar_mul(
                stats[:, 36 + tb:37 + tb], stats[:, 36 + tb:37 + tb], 127.0)))(),
                inc=(sV, 1), tag=f"h2{r}.{tb}")
            st_v[tb] = sV.n
        hqst2 = [hqstage, eqv(14080, 2816)]
        hq1_led = {}
        hq2_led = {}
        for tb in range(4):
            for hh in range(2):
                k = tb * 2 + hh
                w = [(sV, st_v[tb])]
                if k >= 2:
                    w.append((sV, hq2_led[k - 2]))
                em("scalar", (lambda tb=tb, hh=hh, k=k: (lambda E: E.activation(
                    hqst2[k % 2],
                    aV[:, tb * FH + hh * 2816: tb * FH + (hh + 1) * 2816],
                    AF.Copy, bias=MAGIC,
                    scale=stats[:, 36 + tb:37 + tb])))(),
                    waits=w, inc=(sES, 1), tag=f"hq1{r}.{tb}.{hh}")
                hq1_led[k] = sES.n
                em("vector", (lambda tb=tb, hh=hh, k=k: (lambda E: E.tensor_scalar_add(
                    bV[:, tb * FH + hh * 2816: tb * FH + (hh + 1) * 2816],
                    hqst2[k % 2], -MAGIC)))(),
                    waits=[(sES, hq1_led[k])], inc=(sV, 1), tag=f"hq2{r}.{tb}.{hh}")
                hq2_led[k] = sV.n
            hq_tb[tb] = hq2_led[tb * 2 + 1]
        em("vector", lambda E: E.tensor_scalar(
            stats[:, 52:56], stats[:, 40:44], mcl[:, 2:3], None, ALU.mult),
            inc=(sV, 1), tag=f"sc2{r}")
        hq_v = sV.n

        hqT3 = aV.rearrange("p (fb x) -> p fb x", x=512)
        for tb in range(4):
            for grp in range(6):
                fb0 = grp * 8
                nb = min(8, FB - fb0)
                blocks = [bV[:, tb * FH + (fb0 + i) * 128:
                             tb * FH + (fb0 + i) * 128 + 128] for i in range(nb)]
                transpose_batch3(blocks,
                                 hqT3[:, fb0:fb0 + nb, tb * 128:(tb + 1) * 128],
                                 [(sV, hq_tb[tb])], f"hq{r}.{tb}.{grp}")
        hqT_e = sES.n

        agw2 = [(sCC, fst.get("ag2_cc", 0))]
        for ht in range(4):
            s, ev = acquire_slot()
            group_first = True
            for q in range(11):
                srcs = [(i * 512, wq_ag[2][4 * q + i, ht][:, :]) for i in range(4)]
                si, wv = wchunk_load(srcs, extra_waits=agw2, tag=f"gc{r}.{ht}.{q}")
                for fbl in range(4):
                    fb = q * 4 + fbl
                    for tbi in range(4):
                        w = []
                        if group_first:
                            w = evw(ev) + [(sES, hqT_e)]
                        if fbl == 0 and tbi == 0:
                            w.append((sW2[si], wv))
                        last_of_chunk = (fbl == 3 and tbi == 3)
                        em("tensor", (lambda s=s, tbi=tbi, si=si, fbl=fbl, fb=fb:
                            (lambda E: E.matmul(
                                pb[s * 4 + tbi][:],
                                aV[:, fb * 512 + tbi * 128: fb * 512 + (tbi + 1) * 128],
                                wslot[:, si * 2048 + fbl * 512: si * 2048 + (fbl + 1) * 512],
                                start=(fb == 0), stop=(fb == FB - 1))))(),
                            waits=w, inc=((sMM, 1) if last_of_chunk else None),
                            tag=f"F2.{r}.{ht}.{q}")
                        group_first = False
                wchunk_done(si)
            mmv = sMM.n
            for tbi in range(4):
                oi = (ht * 4 + tbi) % 2
                em("vector", (lambda s=s, tbi=tbi, ht=ht, oi=oi: (lambda E: E.scalar_tensor_tensor(
                    o2st[oi], pb[s * 4 + tbi][:], stats[:, 52 + tbi:53 + tbi],
                    out1V[:, tbi * 2048 + ht * 512: tbi * 2048 + (ht + 1) * 512],
                    ALU.mult, ALU.add)))(),
                    waits=[(sMM, mmv), (sO2[oi], o2led[oi])], inc=(sEV, 1),
                    tag=f"F2e{r}.{ht}.{tbi}")
                evn = sEV.n
                dma("gpsimd",
                    out[r * TS + tbi * 128: r * TS + (tbi + 1) * 128,
                        ht * 512:(ht + 1) * 512],
                    o2st[oi], waits=[(sEV, evn)], inc=(sO2[oi], 16), tag=f"o{r}.{ht}.{tbi}")
                o2led[oi] = sO2[oi].n
            release_slot(s)
        ffn2ev[r] = sEV.n
        xfree_mm[r + 1] = sMM.n
        out_total[0] = ((sO2[0], sO2[0].n), (sO2[1], sO2[1].n))

    for r in range(ROUNDS):
        emit_round(r)
        if r + 1 < ROUNDS:
            load_x(r + 1)

    em("gpsimd", None, waits=[(sO2[0], sO2[0].n), (sO2[1], sO2[1].n)], tag="fin")

    P.check()

    for s in sems:
        s.h = nc.alloc_semaphore(s.name)
    with nc.Block() as block:
        P.emit(block)
    return nc


_nc_cache = {}


def _get_nc():
    if "nc" not in _nc_cache:
        _nc_cache["nc"] = build_nc()
    return _nc_cache["nc"]


def kernel(L0_attn_x, L0_attn_w_f, L0_attn_w_i, L0_attn_w_v, L0_attn_w_q,
           L0_attn_w_ro, L0_attn_w_aux, L0_attn_w_up, L0_attn_w_gate,
           L0_attn_w_down, L0_attn_init_state, L0_ffn_w1, L0_ffn_w2, L0_ffn_w3):
    x = np.ascontiguousarray(np.asarray(L0_attn_x, dtype=np.float32).reshape(16384, 2048))
    wlist = [np.ascontiguousarray(np.asarray(a, dtype=np.float32)) for a in [
        L0_attn_w_f, L0_attn_w_i, L0_attn_w_v, L0_attn_w_q, L0_attn_w_up,
        L0_attn_w_gate, L0_attn_w_ro, L0_attn_w_down]]
    w1 = np.asarray(L0_ffn_w1, dtype=np.float32)
    w2 = np.asarray(L0_ffn_w2, dtype=np.float32)
    w3 = np.asarray(L0_ffn_w3, dtype=np.float32)
    init = np.asarray(L0_attn_init_state, dtype=np.float32)
    init_s = np.ascontiguousarray(init.reshape(16, 128).T)
    idn = np.eye(128, dtype=ml_dtypes.bfloat16)

    nc = _get_nc()
    in_maps = []
    for c in range(NCORES):
        xin = np.ascontiguousarray(np.concatenate(
            [x[(8 * r + c) * TS:(8 * r + c + 1) * TS] for r in range(ROUNDS)], axis=0))
        mask_m = np.zeros((128, 8), dtype=np.float32)
        mask_m[:, :c] = 1.0
        mask_n = np.ascontiguousarray(1.0 - mask_m)
        w1sh = np.ascontiguousarray(w1[:, c * 256:(c + 1) * 256])
        w3sh = np.ascontiguousarray(w3[:, c * 256:(c + 1) * 256])
        w2sh = np.zeros((2048, 768), dtype=np.float32)
        lo = c * 768
        hi = min(FH, lo + 768)
        if lo < FH:
            w2sh[:, :hi - lo] = w2[:, lo:hi]
        im = dict(xin=xin, init_s=init_s, mask_m=mask_m, mask_n=mask_n,
                  idn=idn, w1s=w1sh, w3s=w3sh, w2s=w2sh, wmine=wlist[c])
        in_maps.append(im)

    res = run_bass_kernel_spmd(nc, in_maps, core_ids=list(range(NCORES)))

    full = np.empty((16384, 2048), dtype=np.float32)
    for c in range(NCORES):
        oc = np.asarray(res.results[c]["out"])
        for r in range(ROUNDS):
            full[(8 * r + c) * TS:(8 * r + c + 1) * TS] = oc[r * TS:(r + 1) * TS]
    return full.reshape(1, 16384, 2048)


# version 16
# speedup vs baseline: 1.0103x; 1.0103x over previous
import numpy as np
import ml_dtypes
import concourse.bass as bass
import concourse.mybir as mybir
from concourse.bass_utils import run_bass_kernel_spmd

F32 = mybir.dt.float32
BF16 = mybir.dt.bfloat16
AF = mybir.ActivationFunctionType
ALU = mybir.AluOpType
AX = mybir.AxisListType

NCORES = 8
ROUNDS = 4
TS = 512
H = 2048
FH = 5632
FT = 11
FB = 44
HT = 4
MAGIC = float(np.float32(1.5 * 2 ** 23))
EPS_LN = 1e-5
EPS_Q = 1e-5
NW = float(FH * H)

A_X = 0
A_X2 = 8192
A_YT = 16384
A_F = 20480
A_SI = 24576
A_ST = 28672
A_C = 32768
A_Q = 36864
A_SV = 40960
A_END = 45056
P_STRIPE = 8192
P_QBF = 9728
P_TST = 10496
A_SLAB = 12288
A_SG = 12288


class Sem:
    def __init__(self, name):
        self.name = name
        self.n = 0
        self.h = None


class Plan:
    ENGS = ("sync", "gpsimd", "tensor", "vector", "scalar")

    def __init__(self):
        self.items = {e: [] for e in self.ENGS}
        self.lastwait = {e: {} for e in self.ENGS}

    def em(self, eng, fn, waits=(), inc=None, tag=""):
        w = []
        for sem, val in waits:
            if val <= 0:
                continue
            if self.lastwait[eng].get(sem.name, -1) >= val:
                continue
            self.lastwait[eng][sem.name] = val
            w.append((sem, val))
        self.items[eng].append((tuple(w), fn, inc, tag))
        if inc is not None:
            inc[0].n += inc[1]

    def check(self):
        pos = {e: 0 for e in self.ENGS}
        cnt = {}
        progress = True
        while progress:
            progress = False
            for e in self.ENGS:
                while pos[e] < len(self.items[e]):
                    waits, _, inc, _ = self.items[e][pos[e]]
                    if all(cnt.get(s.name, 0) >= v for s, v in waits):
                        if inc is not None:
                            s, k = inc
                            cnt[s.name] = cnt.get(s.name, 0) + k
                        pos[e] += 1
                        progress = True
                    else:
                        break
        stuck = {e: pos[e] for e in self.ENGS if pos[e] < len(self.items[e])}
        if stuck:
            msg = []
            for e, p in stuck.items():
                waits, _, _, tag = self.items[e][p]
                wd = [(s.name, v, cnt.get(s.name, 0)) for s, v in waits]
                msg.append(f"{e}@{p}/{len(self.items[e])} tag={tag} waits={wd}")
            raise RuntimeError("plan deadlock: " + " | ".join(msg))

    def emit(self, block):
        def mk(eng_name):
            drainy = eng_name in ("vector", "scalar")
            def body(eng):
                for waits, fn, inc, _ in self.items[eng_name]:
                    for s, v in waits:
                        eng.wait_ge(s.h, v)
                    ins = fn(eng) if fn is not None else None
                    if inc is not None:
                        s, k = inc
                        if ins is not None:
                            ins.then_inc(s.h, k)
                        else:
                            eng.sem_inc(s.h, k)
                    if drainy and ins is not None:
                        eng.drain()
            return body
        block.sync(mk("sync"))
        block.gpsimd(mk("gpsimd"))
        block.tensor(mk("tensor"))
        block.vector(mk("vector"))
        block.scalar(mk("scalar"))


def build_nc():
    nc = bass.Bass("TRN2", target_bir_lowering=False)

    xin = nc.dram_tensor("xin", [ROUNDS * TS, H], F32, kind="ExternalInput").ap()
    wmine = nc.dram_tensor("wmine", [H, H], F32, kind="ExternalInput").ap()
    w1s = nc.dram_tensor("w1s", [FH, 256], F32, kind="ExternalInput").ap()
    w3s = nc.dram_tensor("w3s", [FH, 256], F32, kind="ExternalInput").ap()
    w2s = nc.dram_tensor("w2s", [H, 768], F32, kind="ExternalInput").ap()
    init_in = nc.dram_tensor("init_s", [128, 16], F32, kind="ExternalInput").ap()
    mm_in = nc.dram_tensor("mask_m", [128, 8], F32, kind="ExternalInput").ap()
    mn_in = nc.dram_tensor("mask_n", [128, 8], F32, kind="ExternalInput").ap()
    idn_in = nc.dram_tensor("idn", [128, 128], BF16, kind="ExternalInput").ap()
    out = nc.dram_tensor("out", [ROUNDS * TS, H], F32, kind="ExternalOutput").ap()

    wri = [nc.dram_tensor(f"wri{j}", [4, 128, 2048], BF16) for j in range(4)]
    wall = [nc.dram_tensor(f"wall{j}", [8, 4, 128, 2048], BF16, addr_space="Shared") for j in range(4)]
    wq_in = [nc.dram_tensor("w1q_in", [2, FT, 128, 512], BF16),
             nc.dram_tensor("w3q_in", [2, FT, 128, 512], BF16),
             nc.dram_tensor("w2q_in", [6, HT, 128, 512], BF16)]
    wq_ag = [nc.dram_tensor("w1q_ag", [16, FT, 128, 512], BF16, addr_space="Shared"),
             nc.dram_tensor("w3q_ag", [16, FT, 128, 512], BF16, addr_space="Shared"),
             nc.dram_tensor("w2q_ag", [48, HT, 128, 512], BF16, addr_space="Shared")]
    ar_in = nc.dram_tensor("ar_in", [128, 4], F32)
    ar_out = nc.dram_tensor("ar_out", [128, 4], F32, addr_space="Shared")
    cin = [nc.dram_tensor(f"cin{r}", [128, 32], F32) for r in range(ROUNDS)]
    cout = [nc.dram_tensor(f"cout{r}", [1024, 32], F32, addr_space="Shared") for r in range(ROUNDS)]

    arena = nc.alloc_sbuf_tensor("arena", [128, A_END], F32)
    wslot = nc.alloc_sbuf_tensor("wslot", [128, 6144], BF16)
    stats = nc.alloc_sbuf_tensor("stats", [128, 64], F32)
    summ = nc.alloc_sbuf_tensor("summ", [128, 32], F32)
    s_all = nc.alloc_sbuf_tensor("s_all", [128, 256], F32)
    s_pref = nc.alloc_sbuf_tensor("s_pref", [128, 16], F32)
    s_in = nc.alloc_sbuf_tensor("s_in", [128, 16], F32)
    t16a = nc.alloc_sbuf_tensor("t16a", [128, 16], F32)
    t16b = nc.alloc_sbuf_tensor("t16b", [128, 16], F32)
    maskm = nc.alloc_sbuf_tensor("maskm", [128, 8], F32)
    maskn = nc.alloc_sbuf_tensor("maskn", [128, 8], F32)
    initb = nc.alloc_sbuf_tensor("initb", [128, 16], F32)
    partials = nc.alloc_sbuf_tensor("partials", [128, 4], F32)
    ones_f32 = nc.alloc_sbuf_tensor("ones_f32", [128, 128], F32)
    means = nc.alloc_sbuf_tensor("means", [128, 4], F32)
    mcl = nc.alloc_sbuf_tensor("mcl", [128, 4], F32)
    swq = nc.alloc_sbuf_tensor("swq", [128, 4], F32)
    zcol = nc.alloc_sbuf_tensor("zcol", [128, 1], F32)
    ones_bf = nc.alloc_sbuf_tensor("ones_bf", [128, 128], BF16)
    idn = nc.alloc_sbuf_tensor("idn_s", [128, 128], BF16)

    pb = [nc.alloc_psum_tensor(f"pb{i}", [128, 512], F32) for i in range(8)]

    sMM = Sem("sMM")
    sEV = Sem("sEV")
    sES = Sem("sES")
    sV = Sem("sV")
    sS = Sem("sS")
    sCC = Sem("sCC")
    sK = Sem("sK")
    sX = Sem("sX")
    sP = [Sem("sP0"), Sem("sP1")]
    sSC = [Sem(f"sSC{m}") for m in range(2)]
    sQW3 = [Sem(f"sQW{i}") for i in range(3)]
    sW2 = [Sem("sW0"), Sem("sW1"), Sem("sW2s")]
    sST = Sem("sST")
    sGQ = Sem("sGQ")
    sAR = Sem("sAR")
    sSA = Sem("sSA")
    sCI = Sem("sCI")
    sO2 = [Sem("sO0"), Sem("sO1")]
    sFIN = Sem("sFIN")
    sems = [sMM, sEV, sES, sV, sS, sCC, sK, sX] + sP + sSC + sW2 + sQW3 + [
        sST, sGQ, sAR, sSA, sCI] + sO2 + [sFIN]

    def eqv(off, n):
        return arena[:, off:off + n]

    def bfv(off, neq):
        return arena[:, off:off + neq].bitcast(BF16)

    xV = eqv(A_X, 8192)
    y2stage = eqv(A_X, 2048)
    ln2scr = eqv(A_X + 2048, 2048)
    aV = bfv(A_X, 11264)
    hqstage = eqv(11264, 2816)
    sgV = bfv(A_SG, 4096)
    yTV = bfv(A_YT, 4096)
    y2qTV = bfv(A_YT, 4096)
    fV = bfv(A_F, 4096)
    ln1scr = eqv(A_F, 2048)
    out1V = eqv(A_F, 8192)
    yV = bfv(A_SI, 4096)
    siV = bfv(A_SI, 4096)
    stV = bfv(A_ST, 4096)
    cV = bfv(A_C, 4096)
    qV = bfv(A_Q, 4096)
    y2qV = bfv(A_Q, 4096)
    bV = bfv(A_ST, 11264)
    svV = bfv(A_SV, 4096)
    uV = bfv(A_SV, 4096)
    zV = bfv(A_SV, 4096)
    o2st = [eqv(A_SV + i * 512, 512) for i in range(2)]
    pstripe = eqv(P_STRIPE, 1536)
    qbf = bfv(P_QBF, 768)
    tst = [bfv(P_TST + i * 256, 256) for i in range(3)]
    slabv = bfv(A_SLAB, 2048)

    P = Plan()
    em = P.em

    def dma(eng, o, i, waits=(), inc=None, tag="", **kw):
        em(eng, (lambda o=o, i=i, kw=kw: (lambda E: E.dma_start(out=o, in_=i, **kw)))(),
           waits=waits, inc=inc, tag=tag)

    slot_ev = [(0, 0), (0, 0)]
    slot_ctr = [0]

    def acquire_slot():
        s = slot_ctr[0] % 2
        slot_ctr[0] += 1
        return s, slot_ev[s]

    def release_slot(s):
        slot_ev[s] = (sES.n, sEV.n)

    def evw(ev):
        return [(sES, ev[0]), (sEV, ev[1])]

    wfree = [0, 0, 0]
    wseq = [0]

    def wchunk_load(srcs, extra_waits=(), tag=""):
        si = wseq[0] % 3
        wseq[0] += 1
        first = True
        n = len(srcs)
        for idx, (lo, src) in enumerate(srcs):
            w = list(extra_waits) + ([(sMM, wfree[si])] if first else [])
            width = 2048 if n == 1 else 512
            dma("sync", wslot[:, si * 2048 + lo: si * 2048 + lo + width],
                src, waits=w, inc=(sW2[si], 16), tag=tag)
            first = False
        return si, sW2[si].n

    def wchunk_done(si):
        wfree[si] = sMM.n

    em("vector", lambda E: E.memset(zcol[:], 0.0), tag="z0")
    em("vector", lambda E: E.memset(ones_bf[:], 1.0), tag="ones")
    em("vector", lambda E: E.memset(ones_f32[:], 1.0), tag="ones32")
    dma("gpsimd", idn[:], idn_in[:, :], inc=(sK, 16), tag="idn")
    dma("gpsimd", maskm[:], mm_in[:, :], inc=(sK, 16), tag="mm")
    dma("gpsimd", maskn[:], mn_in[:, :], inc=(sK, 16), tag="mn")
    dma("gpsimd", initb[:], init_in[:, :], inc=(sK, 16), tag="init")
    g_consts = sK.n
    em("vector", lambda E: E.tensor_copy(s_pref[:], initb[:]),
       waits=[(sK, g_consts)], inc=(sV, 1), tag="spref0")

    ffn2ev = {-1: 0}
    xfree_mm = [0] * (ROUNDS + 1)
    xload_g = [0] * ROUNDS

    def load_x(r, eng="gpsimd"):
        dma(eng, xV.rearrange("p (t d) -> p t d", t=4),
            xin[r * TS:(r + 1) * TS, :].rearrange("(t p) d -> p t d", p=128),
            waits=[(sMM, xfree_mm[r])], inc=(sX, 16), tag=f"x{r}")
        xload_g[r] = sX.n

    load_x(0, eng="scalar")

    slabh = [bfv(A_SLAB, 2048).rearrange("p (t d) -> p t d", d=512),
             bfv(A_SLAB + 2048, 2048).rearrange("p (t d) -> p t d", d=512)]
    prev_writes_w = [None, None]
    ag_w_cc = {}
    agw_waits = {}
    for j in range(4):
        src_j = wmine[:, j * 512:(j + 1) * 512].rearrange("(t p) d -> p t d", p=128)
        for h in range(2):
            dma("gpsimd", slabh[h][:, :, :], src_j[:, 8 * h:8 * h + 8, :],
                waits=[prev_writes_w[h]] if prev_writes_w[h] else [],
                inc=(sP[h], 16), tag=f"slab{j}.{h}")
            lg = sP[h].n
            for q in (2 * h, 2 * h + 1):
                dma("sync", wri[j][q][:, :].rearrange("p (t d) -> p t d", d=512),
                    slabh[h][:, (q % 2) * 4:(q % 2) * 4 + 4, :],
                    waits=[(sP[h], lg)], inc=(sSC[h], 16), tag=f"scw{j}.{q}")
            prev_writes_w[h] = (sSC[h], sSC[h].n)
        agw_waits[j] = [(s, s.n) for s in sSC]

    def emit_agw(j):
        em("gpsimd", (lambda j=j: (lambda E: E.collective_compute(
            "AllGather", ALU.bypass, replica_groups=[list(range(NCORES))],
            ins=[wri[j].ap().opt()], outs=[wall[j].ap().opt()])))(),
            waits=list(agw_waits[j]), inc=(sCC, 1), tag=f"AGw{j}")
        ag_w_cc[j] = sCC.n

    emit_agw(0)
    emit_agw(1)
    dma("gpsimd", ar_in.ap()[:, :], partials[:], waits=[(sFIN, 1)],
        inc=(sAR, 16), tag="arin")
    em("gpsimd", lambda E: E.collective_compute(
        "AllReduce", ALU.add, replica_groups=[list(range(NCORES))],
        ins=[ar_in.ap().opt()], outs=[ar_out.ap().opt()]),
        waits=[(sAR, sAR.n)], inc=(sCC, 1), tag="AR")
    dma("gpsimd", partials[:], ar_out.ap()[:, :], waits=[(sCC, sCC.n)],
        inc=(sAR, 16), tag="arout")
    emit_agw(2)
    emit_agw(3)
    retile_done = [(s, s.n) for s in sSC]
    _FFNQ_PROLOGUE = [True]

    fst = {"qv": 0}
    tstseq = [0]

    def ffnq_stripes(mi):
        wsrc = [w1s, w3s, w2s][mi]
        if mi < 2:
            return [(ft, wsrc[ft * 512:(ft + 1) * 512, :]
                     .rearrange("(g p) d -> p g d", p=128), 1024) for ft in range(FT)]
        res = []
        for st in range(4):
            for hh in range(2):
                res.append(((st, hh),
                            wsrc[st * 512 + hh * 256: st * 512 + (hh + 1) * 256, :]
                            .rearrange("(g p) d -> p g d", p=128), 1536))
        return res

    phbuf = [eqv(P_STRIPE, 1536), eqv(A_SLAB, 1536)]
    phdisc = eqv(P_QBF, 1536)
    phsum = eqv(11264, 32)
    phred = {}

    def phA_works():
        k = 0
        for mi in range(3):
            for sid, src, ncols in ffnq_stripes(mi):
                yield (k, mi, src, ncols)
                k += 1

    def phA_emit(work):
        k, mi, src, ncols = work
        b = k % 2
        gdim = ncols // 256 if mi == 2 else 4
        w = []
        if k >= 2:
            w.append((sS, phred[k - 2]))
        if k == 1:
            w += list(retile_done)
        dma("scalar", phbuf[b][:, 0:ncols].rearrange("p (g d) -> p g d", g=gdim),
            src, waits=w, inc=(sST, 16), tag=f"pha{k}")
        g = sST.n
        em("scalar", (lambda ncols=ncols, b=b, k=k: (lambda E: E.activation(
            phdisc[:, 0:ncols], phbuf[b][:, 0:ncols], AF.Abs,
            accum_out=phsum[:, k:k + 1])))(),
            waits=[(sST, g)], inc=(sS, 1), tag=f"phr{k}")
        phred[k] = sS.n
        if k == 29:
            fst["phs"] = sS.n

    def phA_finale():
        em("vector", lambda E: E.memset(partials[:, 3:4], 0.0), tag="pz3")
        for mi, (c0, c1) in enumerate([(0, 11), (11, 22), (22, 30)]):
            em("vector", (lambda mi=mi, c0=c0, c1=c1: (lambda E: E.reduce_sum(
                partials[:, mi:mi + 1], phsum[:, c0:c1], axis=AX.X)))(),
                waits=([(sS, fst["phs"])] if mi == 0 else []),
                inc=((sV, 1) if mi == 2 else None), tag=f"phf{mi}")
        em("vector", lambda E: E.engine_nop(), inc=(sFIN, 1), tag="phfd")
        fst["ar_g"] = 32

    for _pw in phA_works():
        phA_emit(_pw)
    phA_finale()

    def ffnq_means():
        em("vector", lambda E: E.engine_nop(),
           waits=[(sAR, fst["ar_g"])], inc=(sV, 1), tag="pcast")
        vv = sV.n
        s, ev = acquire_slot()
        em("tensor", (lambda s=s: (lambda E: E.matmul(
            pb[s * 4][:, 0:4], ones_f32[:], partials[:], start=True, stop=True)))(),
            waits=evw(ev) + [(sV, vv)], inc=(sMM, 1), tag="meanMM")
        mmv = sMM.n
        em("vector", (lambda s=s: (lambda E: E.tensor_scalar_mul(
            means[:], pb[s * 4][:, 0:4], 1.0 / NW)))(),
            waits=[(sMM, mmv)], inc=(sEV, 1), tag="means")
        release_slot(s)
        em("vector", lambda E: E.tensor_scalar_max(mcl[:], means[:], EPS_Q), tag="mcl")
        em("vector", lambda E: E.reciprocal(swq[:], mcl[:]), inc=(sV, 1), tag="swq")
        fst["means_v"] = sV.n

    def ffnq_quant_gen():
        for mi in range(3):
            for item in ffnq_quant_stripes(mi):
                yield item
        yield None

    def ffnq_quant_stripes(mi):
        mv = fst["means_v"]
        dst = wq_in[mi]
        for sid, src, ncols in ffnq_stripes(mi):
            yield (mi, sid, src, ncols, dst, mv)

    def ffnq_emit_stripe(work):
        mi, sid, src, ncols, dst, mv = work
        if True:
            gdim = ncols // 256 if mi == 2 else 4
            dma("sync", pstripe[:, 0:ncols].rearrange("p (g d) -> p g d", g=gdim),
                src, waits=[(sV, fst["qv"]), (sS, fst.get("phs", 0))],
                inc=(sST, 16), tag=f"qb{mi}")
            g = sST.n
            em("vector", (lambda ncols=ncols, mi=mi: (lambda E: E.tensor_scalar(
                pstripe[:, 0:ncols], pstripe[:, 0:ncols], swq[:, mi:mi + 1], MAGIC,
                ALU.mult, ALU.add)))(),
                waits=[(sST, g), (sV, mv), (sES, fst.get("qes", 0))],
                tag=f"qr1{mi}")
            em("vector", (lambda ncols=ncols: (lambda E: E.tensor_scalar(
                pstripe[:, 0:ncols], pstripe[:, 0:ncols], -MAGIC, -1.0,
                ALU.add, ALU.max)))(), tag=f"qr2{mi}")
            em("vector", (lambda ncols=ncols: (lambda E: E.tensor_scalar_min(
                qbf[:, 0:ncols], pstripe[:, 0:ncols], 1.0)))(),
                inc=(sV, 1), tag=f"qr3{mi}")
            fst["qv"] = sV.n
            qv = sV.n
            if mi < 2:
                ft = sid
                groups = [(db, [qbf[:, g4 * 256 + db * 128: g4 * 256 + (db + 1) * 128]
                                for g4 in range(4)], 512, dst[db, ft][:, :]) for db in range(2)]
            else:
                st, hh = sid
                groups = [(fb, [qbf[:, g2 * 768 + fb * 128: g2 * 768 + (fb + 1) * 128]
                                for g2 in range(2)], 256,
                           dst[fb, st][:, hh * 256:(hh + 1) * 256]) for fb in range(6)]
            for _, blocks, nbf, dstap in groups:
                s, ev = acquire_slot()
                bank_bf = pb[s * 4][:, 0:512].bitcast(BF16)
                first = True
                for i, blk in enumerate(blocks):
                    w = (evw(ev) + [(sV, qv)]) if first else []
                    em("tensor", (lambda i=i, blk=blk, bank_bf=bank_bf:
                        (lambda E: E.transpose(
                            bank_bf[:, i * 128:(i + 1) * 128], blk, idn[:])))(),
                        waits=w, inc=((sMM, 1) if i == len(blocks) - 1 else None),
                        tag=f"qT{mi}")
                    first = False
                mmv = sMM.n
                ti = tstseq[0] % 3
                tstseq[0] += 1
                em("scalar", (lambda nbf=nbf, bank_bf=bank_bf, ti=ti: (lambda E: E.activation(
                    tst[ti][:, 0:nbf], bank_bf[:, 0:nbf], AF.Copy)))(),
                    waits=[(sMM, mmv), (sQW3[ti], fst.get(f"tw{ti}", 0))],
                    inc=(sES, 1), tag=f"qTe{mi}")
                release_slot(s)
                ev2 = sES.n
                dma("gpsimd", dstap, tst[ti][:, 0:nbf],
                    waits=[(sES, ev2)], inc=(sQW3[ti], 16), tag=f"qw{mi}")
                fst[f"tw{ti}"] = sQW3[ti].n
        fst["qes"] = sES.n
        nst = FT if mi < 2 else 8
        fst[f"nst{mi}"] = fst.get(f"nst{mi}", 0) + 1
        if fst[f"nst{mi}"] == nst:
            fst[f"agq_ready{mi}"] = [(q, q.n) for q in sQW3]


    def ln_stats(r, tiles, discards, pre_s, pre_v, tag):
        for tb in range(4):
            em("vector", (lambda tb=tb: (lambda E: E.reduce_sum(
                stats[:, tb:tb + 1], tiles[tb], axis=AX.X)))(),
                waits=list(pre_v), tag=f"s1{tag}.{tb}")
            em("scalar", (lambda tb=tb: (lambda E: E.activation(
                discards[tb], tiles[tb], AF.Square,
                accum_out=stats[:, 4 + tb:5 + tb])))(),
                waits=list(pre_s), inc=(sS, 1), tag=f"s2{tag}.{tb}")
        sq_s = sS.n
        em("vector", lambda E: E.tensor_scalar_mul(stats[:, 8:12], stats[:, 0:4], 1.0 / H),
           tag=f"mu{tag}")
        em("vector", lambda E: E.tensor_scalar_mul(stats[:, 12:16], stats[:, 4:8], 1.0 / H),
           waits=[(sS, sq_s)], tag=f"e2{tag}")
        em("vector", lambda E: E.tensor_mul(stats[:, 56:60], stats[:, 8:12], stats[:, 8:12]),
           tag=f"msq{tag}")
        em("vector", lambda E: E.tensor_sub(stats[:, 12:16], stats[:, 12:16], stats[:, 56:60]),
           tag=f"var{tag}")
        em("vector", lambda E: E.tensor_scalar_add(stats[:, 12:16], stats[:, 12:16], EPS_LN),
           tag=f"ve{tag}")
        em("vector", lambda E: E.reciprocal(stats[:, 12:16], stats[:, 12:16]),
           inc=(sV, 1), tag=f"rv{tag}")
        rv_v = sV.n
        em("scalar", lambda E: E.activation(stats[:, 16:20], stats[:, 12:16], AF.Sqrt),
           waits=[(sV, rv_v)], inc=(sS, 1), tag=f"sqrt{tag}")
        sq2_s = sS.n
        em("vector", lambda E: E.tensor_scalar_mul(stats[:, 20:24], stats[:, 8:12], -1.0),
           tag=f"nm{tag}")
        em("vector", lambda E: E.tensor_mul(stats[:, 20:24], stats[:, 20:24], stats[:, 16:20]),
           waits=[(sS, sq2_s)], inc=(sV, 1), tag=f"nb{tag}")
        return sS.n, sV.n

    def transpose_batch3(blocks, dst3, wait_list, tag):
        s, ev = acquire_slot()
        bank_bf = pb[s * 4][:, 0:512].bitcast(BF16)
        n = len(blocks)
        first = True
        for i, blk in enumerate(blocks):
            w = (evw(ev) + list(wait_list)) if first else []
            em("tensor", (lambda i=i, blk=blk, bank_bf=bank_bf: (lambda E: E.transpose(
                bank_bf[:, i * 128:(i + 1) * 128], blk, idn[:])))(),
                waits=w, inc=((sMM, 1) if i == n - 1 else None),
                tag=f"T{tag}")
            first = False
        mmv = sMM.n
        bsrc = bank_bf.rearrange("p (i x) -> p i x", x=128)
        em("scalar", (lambda dst3=dst3, bsrc=bsrc, n=n:
            (lambda E: E.activation(dst3, bsrc[:, 0:n, :], AF.Copy)))(),
            waits=[(sMM, mmv)], inc=(sES, 1), tag=f"Te{tag}")
        release_slot(s)

    def transpose_batch(blocks, dst, dst_lo, nbf, wait_list, tag):
        s, ev = acquire_slot()
        bank_bf = pb[s * 4][:, 0:512].bitcast(BF16)
        first = True
        for i, blk in enumerate(blocks):
            w = (evw(ev) + list(wait_list)) if first else []
            em("tensor", (lambda i=i, blk=blk: (lambda E: E.transpose(
                bank_bf[:, i * 128:(i + 1) * 128], blk, idn[:])))(),
                waits=w, inc=((sMM, 1) if i == len(blocks) - 1 else None),
                tag=f"T{tag}")
            first = False
        mmv = sMM.n
        em("scalar", (lambda dst=dst, dst_lo=dst_lo, nbf=nbf, bank_bf=bank_bf:
            (lambda E: E.activation(dst[:, dst_lo:dst_lo + nbf],
                                    bank_bf[:, 0:nbf], AF.Copy)))(),
            waits=[(sMM, mmv)], inc=(sES, 1), tag=f"Te{tag}")
        release_slot(s)

    proj_cfg = [(fV, AF.Tanh), (siV, AF.Sigmoid), (svV, AF.Silu),
                (qV, AF.Copy), (uV, AF.Copy), (sgV, AF.Silu)]
    o2led = [0, 0]
    out_total = [()]

    def emit_round(r):
        x_tiles = [xV[:, tb * 2048:(tb + 1) * 2048] for tb in range(4)]
        pre = [(sX, xload_g[r]), (sEV, ffn2ev[r - 1])]
        ln1_disc = [out1V[:, tb * 2048:(tb + 1) * 2048] for tb in range(4)]
        s_ln1, v_ln1 = ln_stats(r, x_tiles, ln1_disc, pre, pre, f"L1.{r}")
        for tb in range(4):
            em("vector", (lambda tb=tb: (lambda E: E.tensor_scalar(
                yV[:, tb * 2048:(tb + 1) * 2048], x_tiles[tb],
                stats[:, 16 + tb:17 + tb], stats[:, 20 + tb:21 + tb],
                ALU.mult, ALU.add)))(),
                inc=((sV, 1) if tb == 3 else None), tag=f"y{r}.{tb}")
        y_s = sV.n

        for hb2 in range(8):
            blocks = [yV[:, tb * 2048 + (hb2 * 2 + hbl) * 128:
                         tb * 2048 + (hb2 * 2 + hbl) * 128 + 128]
                      for hbl in range(2) for tb in range(4)]
            transpose_batch(blocks, yTV, hb2 * 1024, 1024,
                            [(sV, y_s), (sK, g_consts)], f"y{r}.{hb2}")
        yT_e = sES.n

        proj_ev = {}
        g_v_box = [0]
        if r == 0:
            qgen = ffnq_quant_gen()
            qdone = [False]
        for m6 in range(6):
            dst, func = proj_cfg[m6]
            extra_ev = []
            if m6 in (0, 1):
                extra_ev = [(sEV, ffn2ev[r - 1])]
            if m6 in (2, 4):
                extra_ev = list(out_total[0])
            if m6 == 4:
                extra_ev.append((sV, g_v_box[0]))
            if m6 == 5 and r == 0:
                extra_ev = list(retile_done)
            for rbq in range(4):
                s, ev = acquire_slot()
                group_first = True
                for hbq in range(4):
                    si, wv = wchunk_load([(0, wall[rbq][m6, hbq][:, :])],
                                         extra_waits=[(sCC, ag_w_cc[rbq])],
                                         tag=f"pc{r}.{m6}.{rbq}.{hbq}")
                    for hbl in range(4):
                        hb = hbq * 4 + hbl
                        for rbi in range(4):
                            w = []
                            if group_first:
                                w = evw(ev) + [(sES, yT_e)]
                            if hbl == 0 and rbi == 0:
                                w.append((sW2[si], wv))
                            last_of_chunk = (hbl == 3 and rbi == 3)
                            em("tensor", (lambda s=s, rbi=rbi, si=si, hbl=hbl, hb=hb:
                                (lambda E: E.matmul(
                                    pb[s * 4 + rbi][:],
                                    wslot[:, si * 2048 + hbl * 512 + rbi * 128:
                                          si * 2048 + hbl * 512 + (rbi + 1) * 128],
                                    yTV[:, hb * 512:(hb + 1) * 512],
                                    start=(hb == 0), stop=(hb == 15))))(),
                                waits=w, inc=((sMM, 1) if last_of_chunk else None),
                                tag=f"P{r}.{m6}.{rbq}.{hbq}")
                            group_first = False
                    wchunk_done(si)
                mmv = sMM.n
                for rbi in range(4):
                    rb = rbq * 4 + rbi
                    em("scalar", (lambda dst=dst, func=func, rb=rb, s=s, rbi=rbi:
                        (lambda E: E.activation(
                            dst[:, rb * 512:(rb + 1) * 512], pb[s * 4 + rbi][:], func)
                         if func == AF.Copy else E.activation(
                            dst[:, rb * 512:(rb + 1) * 512], pb[s * 4 + rbi][:], func,
                            bias=zcol[:, 0:1])))(),
                        waits=[(sMM, mmv)] + extra_ev, inc=(sES, 1),
                        tag=f"Pe{r}.{m6}.{rbq}.{rbi}")
                release_slot(s)
                if r == 0 and m6 >= 3 and not qdone[0]:
                    for _ in range(3):
                        wk = next(qgen)
                        if wk is None:
                            qdone[0] = True
                            break
                        ffnq_emit_stripe(wk)
            proj_ev[m6] = sES.n
            if m6 == 2:
                em("vector", lambda E: E.tensor_mul(
                    siV[:, 0:8192], siV[:, 0:8192], svV[:, 0:8192]),
                    waits=[(sES, proj_ev[2])], inc=(sV, 1), tag=f"g{r}")
                g_v_box[0] = sV.n
                for rg in range(16):
                    em("vector", (lambda rg=rg: (lambda E: E.tensor_tensor_scan(
                        stV[:, rg * 512:(rg + 1) * 512], fV[:, rg * 512:(rg + 1) * 512],
                        siV[:, rg * 512:(rg + 1) * 512], 0.0, ALU.mult, ALU.add)))(),
                        waits=[(sV, g_v_box[0])], tag=f"scS{r}.{rg}")
                for rg in range(16):
                    em("vector", (lambda rg=rg: (lambda E: E.tensor_tensor_scan(
                        cV[:, rg * 512:(rg + 1) * 512], fV[:, rg * 512:(rg + 1) * 512],
                        fV[:, rg * 512:(rg + 1) * 512], 1.0, ALU.mult, ALU.bypass)))(),
                        tag=f"scC{r}.{rg}")
                cV3 = cV.rearrange("p (rg t) -> p rg t", t=512)
                sV3 = stV.rearrange("p (rg t) -> p rg t", t=512)
                em("vector", lambda E: E.tensor_copy(
                    summ[:, 0:16], cV3[:, :, 511:512].squeeze(2)), tag=f"sP{r}")
                em("vector", lambda E: E.tensor_copy(
                    summ[:, 16:32], sV3[:, :, 511:512].squeeze(2)),
                    inc=(sV, 1), tag=f"sQ{r}")
                summ_v = sV.n
                dma("gpsimd", cin[r].ap()[:, :], summ[:],
                    waits=[(sV, summ_v)], inc=(sCI, 16), tag=f"cin{r}")
                em("gpsimd", (lambda r=r: (lambda E: E.collective_compute(
                    "AllGather", ALU.bypass, replica_groups=[list(range(NCORES))],
                    ins=[cin[r].ap().opt()], outs=[cout[r].ap().opt()])))(),
                    waits=[(sCI, sCI.n)], inc=(sCC, 1), tag=f"AGc{r}")
                dma("gpsimd", s_all[:].rearrange("p (j f) -> p j f", f=32),
                    cout[r].ap().rearrange("(j p) f -> p j f", p=128),
                    waits=[(sCC, sCC.n)], inc=(sSA, 16), tag=f"sall{r}")
                sall_g = sSA.n
            if r == 0 and m6 == 0:
                ffnq_means()

        if r == 0:
            while not qdone[0]:
                wk = next(qgen)
                if wk is None:
                    qdone[0] = True
                    break
                ffnq_emit_stripe(wk)
            for mi in range(3):
                em("gpsimd", (lambda mi=mi: (lambda E: E.collective_compute(
                    "AllGather", ALU.bypass, replica_groups=[list(range(NCORES))],
                    ins=[wq_in[mi].ap().opt()], outs=[wq_ag[mi].ap().opt()])))(),
                    waits=list(fst[f"agq_ready{mi}"]), inc=(sCC, 1), tag=f"AGq{mi}")
                fst[f"ag{mi}_cc"] = sCC.n

        if r == 0:
            em("vector", lambda E: E.tensor_mul(
                uV[:, 0:8192], uV[:, 0:8192], sgV[:, 0:8192]),
                waits=[(sES, proj_ev[5])], inc=(sV, 1), tag=f"z{r}")
            z_v = sV.n

        em("vector", lambda E: E.tensor_copy(s_in[:], s_pref[:]),
           waits=[(sSA, sall_g)], tag=f"fi{r}")
        for j in range(8):
            Pj = s_all[:, j * 32:j * 32 + 16]
            Qj = s_all[:, j * 32 + 16:j * 32 + 32]
            em("vector", (lambda Pj=Pj, j=j: (lambda E: E.tensor_scalar(
                t16a[:], Pj, maskm[:, j:j + 1], maskn[:, j:j + 1], ALU.mult, ALU.add)))(),
                tag=f"f1.{j}")
            em("vector", lambda E: E.tensor_mul(s_in[:], s_in[:], t16a[:]), tag=f"f2.{j}")
            em("vector", (lambda Qj=Qj, j=j: (lambda E: E.tensor_scalar_mul(
                t16b[:], Qj, maskm[:, j:j + 1])))(), tag=f"f3.{j}")
            em("vector", lambda E: E.tensor_add(s_in[:], s_in[:], t16b[:]), tag=f"f4.{j}")
        for j in range(8):
            Pj = s_all[:, j * 32:j * 32 + 16]
            Qj = s_all[:, j * 32 + 16:j * 32 + 32]
            em("vector", (lambda Pj=Pj: (lambda E: E.tensor_mul(s_pref[:], s_pref[:], Pj)))(),
               tag=f"f5.{j}")
            em("vector", (lambda Qj=Qj: (lambda E: E.tensor_add(s_pref[:], s_pref[:], Qj)))(),
               tag=f"f6.{j}")
        for rg in range(16):
            em("vector", (lambda rg=rg: (lambda E: E.scalar_tensor_tensor(
                cV[:, rg * 512:(rg + 1) * 512], cV[:, rg * 512:(rg + 1) * 512],
                s_in[:, rg:rg + 1], stV[:, rg * 512:(rg + 1) * 512],
                ALU.mult, ALU.add)))(), tag=f"co{r}.{rg}")
        em("vector", lambda E: E.tensor_mul(stV[:, 0:8192], qV[:, 0:8192], cV[:, 0:8192]),
           waits=[(sES, proj_ev[3])], inc=(sV, 1), tag=f"tmp{r}")
        tmp_v = sV.n
        em("scalar", lambda E: E.activation(qV[:, 0:8192], stV[:, 0:8192], AF.Silu,
                                            bias=zcol[:, 0:1]),
           waits=[(sV, tmp_v)], inc=(sS, 1), tag=f"ro{r}")
        ro_s = sS.n
        if r > 0:
            em("vector", lambda E: E.tensor_mul(
                uV[:, 0:8192], uV[:, 0:8192], sgV[:, 0:8192]),
                waits=[(sES, proj_ev[5])], inc=(sV, 1), tag=f"z{r}")
            z_v = sV.n
        em("vector", lambda E: E.memset(stats[:, 0:8], 0.0), tag=f"l2z{r}")
        sqv_led = {}
        for ht in range(4):
            s, ev = acquire_slot()
            group_first = True
            for mseq, m2 in enumerate((1, 0)):
                stat_src = qV if m2 == 0 else zV
                for kq in range(4):
                    si, wv = wchunk_load([(0, wall[ht][6 + m2, kq][:, :])],
                                         extra_waits=[(sCC, ag_w_cc[ht])],
                                         tag=f"mc{r}.{ht}.{m2}.{kq}")
                    for kbl in range(4):
                        kb = kq * 4 + kbl
                        for tbi in range(4):
                            w = []
                            if group_first:
                                w = evw(ev) + [(sV, z_v)]
                            if mseq == 1 and kq == 0 and kbl == 0 and tbi == 0:
                                w.append((sS, ro_s))
                            if kbl == 0 and tbi == 0:
                                w.append((sW2[si], wv))
                            last_of_chunk = (kbl == 3 and tbi == 3)
                            em("tensor", (lambda s=s, tbi=tbi, si=si, kbl=kbl, kb=kb,
                                          stat_src=stat_src, mseq=mseq: (lambda E: E.matmul(
                                pb[s * 4 + tbi][:],
                                stat_src[:, kb * 512 + tbi * 128: kb * 512 + (tbi + 1) * 128],
                                wslot[:, si * 2048 + kbl * 512: si * 2048 + (kbl + 1) * 512],
                                start=(mseq == 0 and kb == 0), stop=(mseq == 1 and kb == 15))))(),
                                waits=w, inc=((sMM, 1) if last_of_chunk else None),
                                tag=f"M{r}.{ht}.{m2}.{kq}")
                            group_first = False
                    wchunk_done(si)
            mmv = sMM.n
            for tbi in range(4):
                em("vector", (lambda s=s, tbi=tbi, ht=ht: (lambda E: E.tensor_tensor(
                    out1V[:, tbi * 2048 + ht * 512: tbi * 2048 + (ht + 1) * 512],
                    pb[s * 4 + tbi][:],
                    xV[:, tbi * 2048 + ht * 512: tbi * 2048 + (ht + 1) * 512],
                    ALU.add)))(),
                    waits=[(sMM, mmv)], inc=(sEV, 1), tag=f"Me{r}.{ht}.{tbi}")
            evht = sEV.n
            for tbi in range(4):
                blk = out1V[:, tbi * 2048 + ht * 512: tbi * 2048 + (ht + 1) * 512]
                em("vector", (lambda blk=blk: (lambda E: E.reduce_sum(
                    stats[:, 59:60], blk, axis=AX.X)))(), tag=f"l2a{r}.{ht}.{tbi}")
                em("vector", (lambda tbi=tbi: (lambda E: E.tensor_add(
                    stats[:, tbi:tbi + 1], stats[:, tbi:tbi + 1],
                    stats[:, 59:60])))(), tag=f"l2b{r}.{ht}.{tbi}")
                sw = [(sEV, evht)]
                if (ht, tbi) in sqv_led:
                    pass
                if ht > 0:
                    sw.append((sV, sqv_led[(ht - 1, tbi)]))
                em("scalar", (lambda blk=blk, tbi=tbi: (lambda E: E.activation(
                    xV[:, tbi * 2048:tbi * 2048 + 512], blk, AF.Square,
                    accum_out=stats[:, 60 + tbi:61 + tbi])))(),
                    waits=sw, inc=(sS, 1), tag=f"l2s{r}.{ht}.{tbi}")
                sqn = sS.n
                em("vector", (lambda tbi=tbi: (lambda E: E.tensor_add(
                    stats[:, 4 + tbi:5 + tbi], stats[:, 4 + tbi:5 + tbi],
                    stats[:, 60 + tbi:61 + tbi])))(),
                    waits=[(sS, sqn)], inc=(sV, 1), tag=f"l2c{r}.{ht}.{tbi}")
                sqv_led[(ht, tbi)] = sV.n
            release_slot(s)
        mix_ev = sEV.n

        o1_tiles = [out1V[:, tb * 2048:(tb + 1) * 2048] for tb in range(4)]
        em("vector", lambda E: E.tensor_scalar_mul(stats[:, 8:12], stats[:, 0:4],
                                                   1.0 / H), tag=f"muL2{r}")
        em("vector", lambda E: E.tensor_scalar_mul(stats[:, 12:16], stats[:, 4:8],
                                                   1.0 / H), tag=f"e2L2{r}")
        em("vector", lambda E: E.tensor_mul(stats[:, 56:60], stats[:, 8:12],
                                            stats[:, 8:12]), tag=f"msqL2{r}")
        em("vector", lambda E: E.tensor_sub(stats[:, 12:16], stats[:, 12:16],
                                            stats[:, 56:60]), tag=f"varL2{r}")
        em("vector", lambda E: E.tensor_scalar_add(stats[:, 12:16], stats[:, 12:16],
                                                   EPS_LN), tag=f"veL2{r}")
        em("vector", lambda E: E.reciprocal(stats[:, 12:16], stats[:, 12:16]),
           inc=(sV, 1), tag=f"rvL2{r}")
        rv2_v = sV.n
        em("scalar", lambda E: E.activation(stats[:, 16:20], stats[:, 12:16], AF.Sqrt),
           waits=[(sV, rv2_v)], inc=(sS, 1), tag=f"sqrtL2{r}")
        s_ln2 = sS.n
        em("vector", lambda E: E.tensor_scalar_mul(stats[:, 20:24], stats[:, 8:12],
                                                   -1.0), tag=f"nmL2{r}")
        em("vector", lambda E: E.tensor_mul(stats[:, 20:24], stats[:, 20:24],
                                            stats[:, 16:20]),
           waits=[(sS, s_ln2)], inc=(sV, 1), tag=f"nbL2{r}")
        rq2_tb = [0, 0, 0, 0]
        for tb in range(4):
            em("vector", (lambda tb=tb: (lambda E: E.tensor_scalar(
                y2stage, o1_tiles[tb],
                stats[:, 16 + tb:17 + tb], stats[:, 20 + tb:21 + tb],
                ALU.mult, ALU.add)))(), tag=f"y2.{r}.{tb}")
            em("vector", (lambda tb=tb: (lambda E: E.reduce_max(
                stats[:, 24 + tb:25 + tb], y2stage, axis=AX.X,
                apply_absolute_value=True)))(), tag=f"qm{r}.{tb}")
            em("vector", (lambda tb=tb: (lambda E: E.tensor_scalar_max(
                stats[:, 24 + tb:25 + tb], stats[:, 24 + tb:25 + tb], EPS_Q)))(),
                tag=f"qc{r}.{tb}")
            em("vector", (lambda tb=tb: (lambda E: E.tensor_scalar_mul(
                stats[:, 32 + tb:33 + tb], stats[:, 24 + tb:25 + tb], 1.0 / 127.0)))(),
                tag=f"rs{r}.{tb}")
            em("vector", (lambda tb=tb: (lambda E: E.reciprocal(
                stats[:, 28 + tb:29 + tb], stats[:, 24 + tb:25 + tb])))(),
                tag=f"r1{r}.{tb}")
            em("vector", (lambda tb=tb: (lambda E: E.tensor_scalar_mul(
                stats[:, 28 + tb:29 + tb], stats[:, 28 + tb:29 + tb], 127.0)))(),
                tag=f"r2{r}.{tb}")
            em("vector", (lambda tb=tb: (lambda E: E.tensor_scalar(
                ln2scr, y2stage, stats[:, 28 + tb:29 + tb], MAGIC, ALU.mult, ALU.add)))(),
                tag=f"rq1{r}.{tb}")
            em("vector", (lambda tb=tb: (lambda E: E.tensor_scalar_add(
                y2qV[:, tb * 2048:(tb + 1) * 2048], ln2scr, -MAGIC)))(),
                inc=(sV, 1), tag=f"rq2{r}.{tb}")
            rq2_tb[tb] = sV.n
        y2q_v = sV.n
        em("vector", lambda E: E.tensor_scalar(
            stats[:, 44:48], stats[:, 32:36], mcl[:, 0:1], None, ALU.mult),
            waits=[(sV, fst.get("means_v", 0))], tag=f"sc1{r}")
        em("vector", lambda E: E.tensor_scalar(
            stats[:, 48:52], stats[:, 32:36], mcl[:, 1:2], None, ALU.mult),
            inc=(sV, 1), tag=f"sc3{r}")
        sc_v = sV.n

        y2qT3 = y2qTV.rearrange("p (dg x) -> p dg x", x=512)
        for tb in range(4):
            for grp in range(2):
                blocks = [y2qV[:, tb * 2048 + (grp * 8 + i) * 128:
                              tb * 2048 + (grp * 8 + i) * 128 + 128] for i in range(8)]
                transpose_batch3(blocks,
                                 y2qT3[:, grp * 8:(grp + 1) * 8,
                                       tb * 128:(tb + 1) * 128],
                                 [(sV, rq2_tb[tb])], f"q{r}.{tb}.{grp}")
        y2qT_e = sES.n

        for mi, (dstV, func, sccol) in enumerate(
                [(aV, AF.Silu, 44), (bV, AF.Copy, 48)]):
            agw = [(sCC, fst.get("ag2_cc", 0))]
            for ft in range(FT):
                s, ev = acquire_slot()
                group_first = True
                for q in range(4):
                    srcs = [(i * 512, wq_ag[mi][4 * q + i, ft][:, :]) for i in range(4)]
                    si, wv = wchunk_load(srcs, extra_waits=agw, tag=f"fc{r}.{mi}.{ft}.{q}")
                    for dgl in range(4):
                        dgb = q * 4 + dgl
                        for tbi in range(4):
                            w = []
                            if group_first:
                                w = evw(ev) + [(sES, y2qT_e), (sV, sc_v)]
                            if dgl == 0 and tbi == 0:
                                w.append((sW2[si], wv))
                            last_of_chunk = (dgl == 3 and tbi == 3)
                            em("tensor", (lambda s=s, tbi=tbi, si=si, dgl=dgl, dgb=dgb:
                                (lambda E: E.matmul(
                                    pb[s * 4 + tbi][:],
                                    y2qTV[:, dgb * 512 + tbi * 128: dgb * 512 + (tbi + 1) * 128],
                                    wslot[:, si * 2048 + dgl * 512: si * 2048 + (dgl + 1) * 512],
                                    start=(dgb == 0), stop=(dgb == 15))))(),
                                waits=w, inc=((sMM, 1) if last_of_chunk else None),
                                tag=f"F1.{r}.{mi}.{ft}.{q}")
                            group_first = False
                    wchunk_done(si)
                mmv = sMM.n
                for tbi in range(4):
                    em("scalar", (lambda dstV=dstV, func=func, tbi=tbi, ft=ft, s=s,
                                  sccol=sccol: (lambda E: E.activation(
                        dstV[:, tbi * FH + ft * 512: tbi * FH + ft * 512 + 512],
                        pb[s * 4 + tbi][:], func,
                        scale=stats[:, sccol + tbi:sccol + tbi + 1])
                        if func == AF.Copy else E.activation(
                        dstV[:, tbi * FH + ft * 512: tbi * FH + ft * 512 + 512],
                        pb[s * 4 + tbi][:], func,
                        bias=zcol[:, 0:1], scale=stats[:, sccol + tbi:sccol + tbi + 1])))(),
                        waits=[(sMM, mmv)], inc=(sES, 1), tag=f"F1e{r}.{mi}.{ft}.{tbi}")
                release_slot(s)
                if mi == 1:
                    if ft == 0:
                        em("vector", lambda E: E.memset(stats[:, 24:28], 0.0),
                           tag=f"hz{r}")
                    ev_g = sES.n
                    for tbi in range(4):
                        lo = tbi * FH + ft * 512
                        em("vector", (lambda lo=lo: (lambda E: E.tensor_mul(
                            aV[:, lo:lo + 512], aV[:, lo:lo + 512],
                            bV[:, lo:lo + 512])))(),
                            waits=([(sES, ev_g)] if tbi == 0 else []),
                            tag=f"h{r}.{ft}.{tbi}")
                        em("vector", (lambda lo=lo: (lambda E: E.reduce_max(
                            stats[:, 60:61], aV[:, lo:lo + 512], axis=AX.X,
                            apply_absolute_value=True)))(), tag=f"hm{r}.{ft}.{tbi}")
                        em("vector", (lambda tbi=tbi: (lambda E: E.tensor_max(
                            stats[:, 24 + tbi:25 + tbi], stats[:, 24 + tbi:25 + tbi],
                            stats[:, 60:61])))(), tag=f"hx{r}.{ft}.{tbi}")
        ffn1_ev = sES.n

        hq_tb = [0, 0, 0, 0]
        st_v = [0, 0, 0, 0]
        for tb in range(4):
            em("vector", (lambda tb=tb: (lambda E: E.tensor_scalar_max(
                stats[:, 24 + tb:25 + tb], stats[:, 24 + tb:25 + tb], EPS_Q)))(),
                tag=f"hc{r}.{tb}")
            em("vector", (lambda tb=tb: (lambda E: E.tensor_scalar_mul(
                stats[:, 40 + tb:41 + tb], stats[:, 24 + tb:25 + tb], 1.0 / 127.0)))(),
                tag=f"hr{r}.{tb}")
            em("vector", (lambda tb=tb: (lambda E: E.reciprocal(
                stats[:, 36 + tb:37 + tb], stats[:, 24 + tb:25 + tb])))(),
                tag=f"h1{r}.{tb}")
            em("vector", (lambda tb=tb: (lambda E: E.tensor_scalar_mul(
                stats[:, 36 + tb:37 + tb], stats[:, 36 + tb:37 + tb], 127.0)))(),
                inc=(sV, 1), tag=f"h2{r}.{tb}")
            st_v[tb] = sV.n
        hqst2 = [hqstage, eqv(14080, 2816)]
        hq1_led = {}
        hq2_led = {}
        for tb in range(4):
            for hh in range(2):
                k = tb * 2 + hh
                w = [(sV, st_v[tb])]
                if k >= 2:
                    w.append((sV, hq2_led[k - 2]))
                em("scalar", (lambda tb=tb, hh=hh, k=k: (lambda E: E.activation(
                    hqst2[k % 2],
                    aV[:, tb * FH + hh * 2816: tb * FH + (hh + 1) * 2816],
                    AF.Copy, bias=MAGIC,
                    scale=stats[:, 36 + tb:37 + tb])))(),
                    waits=w, inc=(sES, 1), tag=f"hq1{r}.{tb}.{hh}")
                hq1_led[k] = sES.n
                em("vector", (lambda tb=tb, hh=hh, k=k: (lambda E: E.tensor_scalar_add(
                    bV[:, tb * FH + hh * 2816: tb * FH + (hh + 1) * 2816],
                    hqst2[k % 2], -MAGIC)))(),
                    waits=[(sES, hq1_led[k])], inc=(sV, 1), tag=f"hq2{r}.{tb}.{hh}")
                hq2_led[k] = sV.n
            hq_tb[tb] = hq2_led[tb * 2 + 1]
        em("vector", lambda E: E.tensor_scalar(
            stats[:, 52:56], stats[:, 40:44], mcl[:, 2:3], None, ALU.mult),
            inc=(sV, 1), tag=f"sc2{r}")
        hq_v = sV.n

        hqT3 = aV.rearrange("p (fb x) -> p fb x", x=512)
        for tb in range(4):
            for grp in range(6):
                fb0 = grp * 8
                nb = min(8, FB - fb0)
                blocks = [bV[:, tb * FH + (fb0 + i) * 128:
                             tb * FH + (fb0 + i) * 128 + 128] for i in range(nb)]
                transpose_batch3(blocks,
                                 hqT3[:, fb0:fb0 + nb, tb * 128:(tb + 1) * 128],
                                 [(sV, hq_tb[tb])], f"hq{r}.{tb}.{grp}")
        hqT_e = sES.n

        agw2 = [(sCC, fst.get("ag2_cc", 0))]
        for ht in range(4):
            s, ev = acquire_slot()
            group_first = True
            for q in range(11):
                srcs = [(i * 512, wq_ag[2][4 * q + i, ht][:, :]) for i in range(4)]
                si, wv = wchunk_load(srcs, extra_waits=agw2, tag=f"gc{r}.{ht}.{q}")
                for fbl in range(4):
                    fb = q * 4 + fbl
                    for tbi in range(4):
                        w = []
                        if group_first:
                            w = evw(ev) + [(sES, hqT_e)]
                        if fbl == 0 and tbi == 0:
                            w.append((sW2[si], wv))
                        last_of_chunk = (fbl == 3 and tbi == 3)
                        em("tensor", (lambda s=s, tbi=tbi, si=si, fbl=fbl, fb=fb:
                            (lambda E: E.matmul(
                                pb[s * 4 + tbi][:],
                                aV[:, fb * 512 + tbi * 128: fb * 512 + (tbi + 1) * 128],
                                wslot[:, si * 2048 + fbl * 512: si * 2048 + (fbl + 1) * 512],
                                start=(fb == 0), stop=(fb == FB - 1))))(),
                            waits=w, inc=((sMM, 1) if last_of_chunk else None),
                            tag=f"F2.{r}.{ht}.{q}")
                        group_first = False
                wchunk_done(si)
            mmv = sMM.n
            for tbi in range(4):
                oi = (ht * 4 + tbi) % 2
                em("vector", (lambda s=s, tbi=tbi, ht=ht, oi=oi: (lambda E: E.scalar_tensor_tensor(
                    o2st[oi], pb[s * 4 + tbi][:], stats[:, 52 + tbi:53 + tbi],
                    out1V[:, tbi * 2048 + ht * 512: tbi * 2048 + (ht + 1) * 512],
                    ALU.mult, ALU.add)))(),
                    waits=[(sMM, mmv), (sO2[oi], o2led[oi])], inc=(sEV, 1),
                    tag=f"F2e{r}.{ht}.{tbi}")
                evn = sEV.n
                dma("gpsimd",
                    out[r * TS + tbi * 128: r * TS + (tbi + 1) * 128,
                        ht * 512:(ht + 1) * 512],
                    o2st[oi], waits=[(sEV, evn)], inc=(sO2[oi], 16), tag=f"o{r}.{ht}.{tbi}")
                o2led[oi] = sO2[oi].n
            release_slot(s)
        ffn2ev[r] = sEV.n
        xfree_mm[r + 1] = sMM.n
        out_total[0] = ((sO2[0], sO2[0].n), (sO2[1], sO2[1].n))

    for r in range(ROUNDS):
        emit_round(r)
        if r + 1 < ROUNDS:
            load_x(r + 1)

    em("gpsimd", None, waits=[(sO2[0], sO2[0].n), (sO2[1], sO2[1].n)], tag="fin")

    P.check()

    for s in sems:
        s.h = nc.alloc_semaphore(s.name)
    with nc.Block() as block:
        P.emit(block)
    return nc


_nc_cache = {}


def _get_nc():
    if "nc" not in _nc_cache:
        _nc_cache["nc"] = build_nc()
    return _nc_cache["nc"]


def kernel(L0_attn_x, L0_attn_w_f, L0_attn_w_i, L0_attn_w_v, L0_attn_w_q,
           L0_attn_w_ro, L0_attn_w_aux, L0_attn_w_up, L0_attn_w_gate,
           L0_attn_w_down, L0_attn_init_state, L0_ffn_w1, L0_ffn_w2, L0_ffn_w3):
    x = np.ascontiguousarray(np.asarray(L0_attn_x, dtype=np.float32).reshape(16384, 2048))
    wlist = [np.ascontiguousarray(np.asarray(a, dtype=np.float32)) for a in [
        L0_attn_w_f, L0_attn_w_i, L0_attn_w_v, L0_attn_w_q, L0_attn_w_up,
        L0_attn_w_gate, L0_attn_w_ro, L0_attn_w_down]]
    w1 = np.asarray(L0_ffn_w1, dtype=np.float32)
    w2 = np.asarray(L0_ffn_w2, dtype=np.float32)
    w3 = np.asarray(L0_ffn_w3, dtype=np.float32)
    init = np.asarray(L0_attn_init_state, dtype=np.float32)
    init_s = np.ascontiguousarray(init.reshape(16, 128).T)
    idn = np.eye(128, dtype=ml_dtypes.bfloat16)

    nc = _get_nc()
    in_maps = []
    for c in range(NCORES):
        xin = np.ascontiguousarray(np.concatenate(
            [x[(8 * r + c) * TS:(8 * r + c + 1) * TS] for r in range(ROUNDS)], axis=0))
        mask_m = np.zeros((128, 8), dtype=np.float32)
        mask_m[:, :c] = 1.0
        mask_n = np.ascontiguousarray(1.0 - mask_m)
        w1sh = np.ascontiguousarray(w1[:, c * 256:(c + 1) * 256])
        w3sh = np.ascontiguousarray(w3[:, c * 256:(c + 1) * 256])
        w2sh = np.zeros((2048, 768), dtype=np.float32)
        lo = c * 768
        hi = min(FH, lo + 768)
        if lo < FH:
            w2sh[:, :hi - lo] = w2[:, lo:hi]
        im = dict(xin=xin, init_s=init_s, mask_m=mask_m, mask_n=mask_n,
                  idn=idn, w1s=w1sh, w3s=w3sh, w2s=w2sh, wmine=wlist[c])
        in_maps.append(im)

    res = run_bass_kernel_spmd(nc, in_maps, core_ids=list(range(NCORES)))

    full = np.empty((16384, 2048), dtype=np.float32)
    for c in range(NCORES):
        oc = np.asarray(res.results[c]["out"])
        for r in range(ROUNDS):
            full[(8 * r + c) * TS:(8 * r + c + 1) * TS] = oc[r * TS:(r + 1) * TS]
    return full.reshape(1, 16384, 2048)


# version 17
# speedup vs baseline: 1.0288x; 1.0183x over previous
import numpy as np
import ml_dtypes
import concourse.bass as bass
import concourse.mybir as mybir
from concourse.bass_utils import run_bass_kernel_spmd

F32 = mybir.dt.float32
BF16 = mybir.dt.bfloat16
AF = mybir.ActivationFunctionType
ALU = mybir.AluOpType
AX = mybir.AxisListType

NCORES = 8
ROUNDS = 4
TS = 512
H = 2048
FH = 5632
FT = 11
FB = 44
HT = 4
MAGIC = float(np.float32(1.5 * 2 ** 23))
EPS_LN = 1e-5
EPS_Q = 1e-5
NW = float(FH * H)

A_X = 0
A_X2 = 8192
A_YT = 16384
A_F = 20480
A_SI = 24576
A_ST = 28672
A_C = 32768
A_Q = 36864
A_SV = 40960
A_END = 45056
P_STRIPE = 8192
P_QBF = 9728
P_TST = 10496
A_SLAB = 12288
A_SG = 12288


class Sem:
    def __init__(self, name):
        self.name = name
        self.n = 0
        self.h = None


class Plan:
    ENGS = ("sync", "gpsimd", "tensor", "vector", "scalar")

    def __init__(self):
        self.items = {e: [] for e in self.ENGS}
        self.lastwait = {e: {} for e in self.ENGS}

    def em(self, eng, fn, waits=(), inc=None, tag=""):
        w = []
        for sem, val in waits:
            if val <= 0:
                continue
            if self.lastwait[eng].get(sem.name, -1) >= val:
                continue
            self.lastwait[eng][sem.name] = val
            w.append((sem, val))
        self.items[eng].append((tuple(w), fn, inc, tag))
        if inc is not None:
            inc[0].n += inc[1]

    def check(self):
        pos = {e: 0 for e in self.ENGS}
        cnt = {}
        progress = True
        while progress:
            progress = False
            for e in self.ENGS:
                while pos[e] < len(self.items[e]):
                    waits, _, inc, _ = self.items[e][pos[e]]
                    if all(cnt.get(s.name, 0) >= v for s, v in waits):
                        if inc is not None:
                            s, k = inc
                            cnt[s.name] = cnt.get(s.name, 0) + k
                        pos[e] += 1
                        progress = True
                    else:
                        break
        stuck = {e: pos[e] for e in self.ENGS if pos[e] < len(self.items[e])}
        if stuck:
            msg = []
            for e, p in stuck.items():
                waits, _, _, tag = self.items[e][p]
                wd = [(s.name, v, cnt.get(s.name, 0)) for s, v in waits]
                msg.append(f"{e}@{p}/{len(self.items[e])} tag={tag} waits={wd}")
            raise RuntimeError("plan deadlock: " + " | ".join(msg))

    def emit(self, block):
        def mk(eng_name):
            drainy = eng_name in ("vector", "scalar")
            def body(eng):
                for waits, fn, inc, _ in self.items[eng_name]:
                    for s, v in waits:
                        eng.wait_ge(s.h, v)
                    ins = fn(eng) if fn is not None else None
                    if inc is not None:
                        s, k = inc
                        if ins is not None:
                            ins.then_inc(s.h, k)
                        else:
                            eng.sem_inc(s.h, k)
                    if drainy and ins is not None:
                        eng.drain()
            return body
        block.sync(mk("sync"))
        block.gpsimd(mk("gpsimd"))
        block.tensor(mk("tensor"))
        block.vector(mk("vector"))
        block.scalar(mk("scalar"))


def build_nc():
    nc = bass.Bass("TRN2", target_bir_lowering=False)

    xin = nc.dram_tensor("xin", [ROUNDS * TS, H], F32, kind="ExternalInput").ap()
    wmine = nc.dram_tensor("wmine", [H, H], F32, kind="ExternalInput").ap()
    w1s = nc.dram_tensor("w1s", [FH, 256], F32, kind="ExternalInput").ap()
    w3s = nc.dram_tensor("w3s", [FH, 256], F32, kind="ExternalInput").ap()
    w2s = nc.dram_tensor("w2s", [H, 768], F32, kind="ExternalInput").ap()
    init_in = nc.dram_tensor("init_s", [128, 16], F32, kind="ExternalInput").ap()
    mm_in = nc.dram_tensor("mask_m", [128, 8], F32, kind="ExternalInput").ap()
    mn_in = nc.dram_tensor("mask_n", [128, 8], F32, kind="ExternalInput").ap()
    idn_in = nc.dram_tensor("idn", [128, 128], BF16, kind="ExternalInput").ap()
    out = nc.dram_tensor("out", [ROUNDS * TS, H], F32, kind="ExternalOutput").ap()

    wri = [nc.dram_tensor(f"wri{j}", [4, 128, 2048], BF16) for j in range(4)]
    wall = [nc.dram_tensor(f"wall{j}", [8, 4, 128, 2048], BF16, addr_space="Shared") for j in range(4)]
    wq_in = [nc.dram_tensor("w1q_in", [2, FT, 128, 512], BF16),
             nc.dram_tensor("w3q_in", [2, FT, 128, 512], BF16),
             nc.dram_tensor("w2q_in", [6, HT, 128, 512], BF16)]
    wq_ag = [nc.dram_tensor("w1q_ag", [16, FT, 128, 512], BF16, addr_space="Shared"),
             nc.dram_tensor("w3q_ag", [16, FT, 128, 512], BF16, addr_space="Shared"),
             nc.dram_tensor("w2q_ag", [48, HT, 128, 512], BF16, addr_space="Shared")]
    ar_in = nc.dram_tensor("ar_in", [128, 4], F32)
    ar_out = nc.dram_tensor("ar_out", [128, 4], F32, addr_space="Shared")
    cin = [nc.dram_tensor(f"cin{r}", [128, 32], F32) for r in range(ROUNDS)]
    cout = [nc.dram_tensor(f"cout{r}", [1024, 32], F32, addr_space="Shared") for r in range(ROUNDS)]

    arena = nc.alloc_sbuf_tensor("arena", [128, A_END], F32)
    wslot = nc.alloc_sbuf_tensor("wslot", [128, 6144], BF16)
    stats = nc.alloc_sbuf_tensor("stats", [128, 64], F32)
    summ = nc.alloc_sbuf_tensor("summ", [128, 32], F32)
    s_all = nc.alloc_sbuf_tensor("s_all", [128, 256], F32)
    s_pref = nc.alloc_sbuf_tensor("s_pref", [128, 16], F32)
    s_in = nc.alloc_sbuf_tensor("s_in", [128, 16], F32)
    t16a = nc.alloc_sbuf_tensor("t16a", [128, 16], F32)
    t16b = nc.alloc_sbuf_tensor("t16b", [128, 16], F32)
    maskm = nc.alloc_sbuf_tensor("maskm", [128, 8], F32)
    maskn = nc.alloc_sbuf_tensor("maskn", [128, 8], F32)
    initb = nc.alloc_sbuf_tensor("initb", [128, 16], F32)
    partials = nc.alloc_sbuf_tensor("partials", [128, 4], F32)
    ones_f32 = nc.alloc_sbuf_tensor("ones_f32", [128, 128], F32)
    means = nc.alloc_sbuf_tensor("means", [128, 4], F32)
    mcl = nc.alloc_sbuf_tensor("mcl", [128, 4], F32)
    swq = nc.alloc_sbuf_tensor("swq", [128, 4], F32)
    zcol = nc.alloc_sbuf_tensor("zcol", [128, 1], F32)
    ones_bf = nc.alloc_sbuf_tensor("ones_bf", [128, 128], BF16)
    idn = nc.alloc_sbuf_tensor("idn_s", [128, 128], BF16)

    pb = [nc.alloc_psum_tensor(f"pb{i}", [128, 512], F32) for i in range(8)]

    sMM = Sem("sMM")
    sEV = Sem("sEV")
    sES = Sem("sES")
    sV = Sem("sV")
    sS = Sem("sS")
    sCC = Sem("sCC")
    sK = Sem("sK")
    sX = Sem("sX")
    sP = [Sem("sP0"), Sem("sP1")]
    sSC = [Sem(f"sSC{m}") for m in range(2)]
    sQW3 = [Sem(f"sQW{i}") for i in range(3)]
    sW2 = [Sem("sW0"), Sem("sW1"), Sem("sW2s")]
    sST = Sem("sST")
    sGQ = Sem("sGQ")
    sAR = Sem("sAR")
    sSA = Sem("sSA")
    sCI = Sem("sCI")
    sO2 = [Sem("sO0"), Sem("sO1")]
    sFIN = Sem("sFIN")
    sST2 = Sem("sST2")
    sems = [sMM, sEV, sES, sV, sS, sCC, sK, sX] + sP + sSC + sW2 + sQW3 + [
        sST, sGQ, sAR, sSA, sCI] + sO2 + [sFIN, sST2]

    def eqv(off, n):
        return arena[:, off:off + n]

    def bfv(off, neq):
        return arena[:, off:off + neq].bitcast(BF16)

    xV = eqv(A_X, 8192)
    y2stage = eqv(A_X, 2048)
    ln2scr = eqv(A_X + 2048, 2048)
    aV = bfv(A_X, 11264)
    hqstage = eqv(11264, 2816)
    sgV = bfv(A_SG, 4096)
    yTV = bfv(A_YT, 4096)
    y2qTV = bfv(A_YT, 4096)
    fV = bfv(A_F, 4096)
    ln1scr = eqv(A_F, 2048)
    out1V = eqv(A_F, 8192)
    yV = bfv(A_SI, 4096)
    siV = bfv(A_SI, 4096)
    stV = bfv(A_ST, 4096)
    cV = bfv(A_C, 4096)
    qV = bfv(A_Q, 4096)
    y2qV = bfv(A_Q, 4096)
    bV = bfv(A_ST, 11264)
    svV = bfv(A_SV, 4096)
    uV = bfv(A_SV, 4096)
    zV = bfv(A_SV, 4096)
    o2st = [eqv(A_SV + i * 512, 512) for i in range(2)]
    pstripe = eqv(P_STRIPE, 1536)
    qbf = bfv(P_QBF, 768)
    tst = [bfv(P_TST + i * 256, 256) for i in range(3)]
    slabv = bfv(A_SLAB, 2048)

    P = Plan()
    em = P.em

    def dma(eng, o, i, waits=(), inc=None, tag="", **kw):
        em(eng, (lambda o=o, i=i, kw=kw: (lambda E: E.dma_start(out=o, in_=i, **kw)))(),
           waits=waits, inc=inc, tag=tag)

    slot_ev = [(0, 0), (0, 0)]
    slot_ctr = [0]

    def acquire_slot():
        s = slot_ctr[0] % 2
        slot_ctr[0] += 1
        return s, slot_ev[s]

    def release_slot(s):
        slot_ev[s] = (sES.n, sEV.n)

    def evw(ev):
        return [(sES, ev[0]), (sEV, ev[1])]

    wfree = [0, 0, 0]
    wseq = [0]

    def wchunk_load(srcs, extra_waits=(), tag=""):
        si = wseq[0] % 3
        wseq[0] += 1
        first = True
        n = len(srcs)
        for idx, (lo, src) in enumerate(srcs):
            w = list(extra_waits) + ([(sMM, wfree[si])] if first else [])
            width = 2048 if n == 1 else 512
            dma("sync", wslot[:, si * 2048 + lo: si * 2048 + lo + width],
                src, waits=w, inc=(sW2[si], 16), tag=tag)
            first = False
        return si, sW2[si].n

    def wchunk_done(si):
        wfree[si] = sMM.n

    em("vector", lambda E: E.memset(zcol[:], 0.0), tag="z0")
    em("vector", lambda E: E.memset(ones_bf[:], 1.0), tag="ones")
    em("vector", lambda E: E.memset(ones_f32[:], 1.0), tag="ones32")
    dma("gpsimd", idn[:], idn_in[:, :], inc=(sK, 16), tag="idn")
    dma("gpsimd", maskm[:], mm_in[:, :], inc=(sK, 16), tag="mm")
    dma("gpsimd", maskn[:], mn_in[:, :], inc=(sK, 16), tag="mn")
    dma("gpsimd", initb[:], init_in[:, :], inc=(sK, 16), tag="init")
    g_consts = sK.n
    em("vector", lambda E: E.tensor_copy(s_pref[:], initb[:]),
       waits=[(sK, g_consts)], inc=(sV, 1), tag="spref0")

    ffn2ev = {-1: 0}
    xfree_mm = [0] * (ROUNDS + 1)
    xload_g = [0] * ROUNDS

    def load_x(r, eng="gpsimd"):
        dma(eng, xV.rearrange("p (t d) -> p t d", t=4),
            xin[r * TS:(r + 1) * TS, :].rearrange("(t p) d -> p t d", p=128),
            waits=[(sMM, xfree_mm[r])], inc=(sX, 16), tag=f"x{r}")
        xload_g[r] = sX.n

    load_x(0, eng="scalar")

    slabh = [bfv(A_SLAB, 2048).rearrange("p (t d) -> p t d", d=512),
             bfv(A_SLAB + 2048, 2048).rearrange("p (t d) -> p t d", d=512)]
    prev_writes_w = [None, None]
    ag_w_cc = {}
    agw_waits = {}

    def emit_agw(j):
        em("gpsimd", (lambda j=j: (lambda E: E.collective_compute(
            "AllGather", ALU.bypass, replica_groups=[list(range(NCORES))],
            ins=[wri[j].ap().opt()], outs=[wall[j].ap().opt()])))(),
            waits=list(agw_waits[j]), inc=(sCC, 1), tag=f"AGw{j}")
        ag_w_cc[j] = sCC.n

    for j in range(4):
        src_j = wmine[:, j * 512:(j + 1) * 512].rearrange("(t p) d -> p t d", p=128)
        for h in range(2):
            dma("gpsimd", slabh[h][:, :, :], src_j[:, 8 * h:8 * h + 8, :],
                waits=[prev_writes_w[h]] if prev_writes_w[h] else [],
                inc=(sP[h], 16), tag=f"slab{j}.{h}")
            lg = sP[h].n
            for q in (2 * h, 2 * h + 1):
                dma("sync", wri[j][q][:, :].rearrange("p (t d) -> p t d", d=512),
                    slabh[h][:, (q % 2) * 4:(q % 2) * 4 + 4, :],
                    waits=[(sP[h], lg)], inc=(sSC[h], 16), tag=f"scw{j}.{q}")
            prev_writes_w[h] = (sSC[h], sSC[h].n)
        agw_waits[j] = [(s, s.n) for s in sSC]
        if j <= 1:
            emit_agw(j)
    dma("gpsimd", ar_in.ap()[:, :], partials[:], waits=[(sFIN, 1)],
        inc=(sAR, 16), tag="arin")
    em("gpsimd", lambda E: E.collective_compute(
        "AllReduce", ALU.add, replica_groups=[list(range(NCORES))],
        ins=[ar_in.ap().opt()], outs=[ar_out.ap().opt()]),
        waits=[(sAR, sAR.n)], inc=(sCC, 1), tag="AR")
    dma("gpsimd", partials[:], ar_out.ap()[:, :], waits=[(sCC, sCC.n)],
        inc=(sAR, 16), tag="arout")
    emit_agw(2)
    emit_agw(3)
    retile_done = [(s, s.n) for s in sSC]
    _FFNQ_PROLOGUE = [True]

    fst = {"qv": 0}
    tstseq = [0]

    def ffnq_stripes(mi):
        wsrc = [w1s, w3s, w2s][mi]
        if mi < 2:
            return [(ft, wsrc[ft * 512:(ft + 1) * 512, :]
                     .rearrange("(g p) d -> p g d", p=128), 1024) for ft in range(FT)]
        res = []
        for st in range(4):
            for hh in range(2):
                res.append(((st, hh),
                            wsrc[st * 512 + hh * 256: st * 512 + (hh + 1) * 256, :]
                            .rearrange("(g p) d -> p g d", p=128), 1536))
        return res

    phbuf = [eqv(P_STRIPE, 1536), eqv(A_SLAB, 1536)]
    phdisc = eqv(P_QBF, 1536)
    phsum = eqv(11264, 32)
    phred = {}

    def phA_works():
        k = 0
        for mi in range(3):
            for sid, src, ncols in ffnq_stripes(mi):
                yield (k, mi, src, ncols)
                k += 1

    def phA_emit(work):
        k, mi, src, ncols = work
        b = k % 2
        gdim = ncols // 256 if mi == 2 else 4
        w = []
        if k >= 2:
            w.append((sS, phred[k - 2]))
        if k == 1:
            w += list(retile_done)
        sq = sST if b == 0 else sST2
        dma("scalar" if b == 0 else "sync",
            phbuf[b][:, 0:ncols].rearrange("p (g d) -> p g d", g=gdim),
            src, waits=w, inc=(sq, 16), tag=f"pha{k}")
        g = (sq, sq.n)
        em("scalar", (lambda ncols=ncols, b=b, k=k: (lambda E: E.activation(
            phdisc[:, 0:ncols], phbuf[b][:, 0:ncols], AF.Abs,
            accum_out=phsum[:, k:k + 1])))(),
            waits=[g], inc=(sS, 1), tag=f"phr{k}")
        phred[k] = sS.n
        if k == 29:
            fst["phs"] = sS.n

    def phA_finale():
        em("vector", lambda E: E.memset(partials[:, 3:4], 0.0), tag="pz3")
        for mi, (c0, c1) in enumerate([(0, 11), (11, 22), (22, 30)]):
            em("vector", (lambda mi=mi, c0=c0, c1=c1: (lambda E: E.reduce_sum(
                partials[:, mi:mi + 1], phsum[:, c0:c1], axis=AX.X)))(),
                waits=([(sS, fst["phs"])] if mi == 0 else []),
                inc=((sV, 1) if mi == 2 else None), tag=f"phf{mi}")
        em("vector", lambda E: E.engine_nop(), inc=(sFIN, 1), tag="phfd")
        fst["ar_g"] = 32

    for _pw in phA_works():
        phA_emit(_pw)
    phA_finale()

    def ffnq_means():
        em("vector", lambda E: E.engine_nop(),
           waits=[(sAR, fst["ar_g"])], inc=(sV, 1), tag="pcast")
        vv = sV.n
        s, ev = acquire_slot()
        em("tensor", (lambda s=s: (lambda E: E.matmul(
            pb[s * 4][:, 0:4], ones_f32[:], partials[:], start=True, stop=True)))(),
            waits=evw(ev) + [(sV, vv)], inc=(sMM, 1), tag="meanMM")
        mmv = sMM.n
        em("vector", (lambda s=s: (lambda E: E.tensor_scalar_mul(
            means[:], pb[s * 4][:, 0:4], 1.0 / NW)))(),
            waits=[(sMM, mmv)], inc=(sEV, 1), tag="means")
        release_slot(s)
        em("vector", lambda E: E.tensor_scalar_max(mcl[:], means[:], EPS_Q), tag="mcl")
        em("vector", lambda E: E.reciprocal(swq[:], mcl[:]), inc=(sV, 1), tag="swq")
        fst["means_v"] = sV.n

    def ffnq_quant_gen():
        for mi in range(3):
            for item in ffnq_quant_stripes(mi):
                yield item
        yield None

    def ffnq_quant_stripes(mi):
        mv = fst["means_v"]
        dst = wq_in[mi]
        for sid, src, ncols in ffnq_stripes(mi):
            yield (mi, sid, src, ncols, dst, mv)

    def ffnq_emit_stripe(work):
        mi, sid, src, ncols, dst, mv = work
        if True:
            gdim = ncols // 256 if mi == 2 else 4
            dma("sync", pstripe[:, 0:ncols].rearrange("p (g d) -> p g d", g=gdim),
                src, waits=[(sV, fst["qv"]), (sS, fst.get("phs", 0))],
                inc=(sST, 16), tag=f"qb{mi}")
            g = sST.n
            em("vector", (lambda ncols=ncols, mi=mi: (lambda E: E.tensor_scalar(
                pstripe[:, 0:ncols], pstripe[:, 0:ncols], swq[:, mi:mi + 1], MAGIC,
                ALU.mult, ALU.add)))(),
                waits=[(sST, g), (sV, mv), (sES, fst.get("qes", 0))],
                tag=f"qr1{mi}")
            em("vector", (lambda ncols=ncols: (lambda E: E.tensor_scalar(
                pstripe[:, 0:ncols], pstripe[:, 0:ncols], -MAGIC, -1.0,
                ALU.add, ALU.max)))(), tag=f"qr2{mi}")
            em("vector", (lambda ncols=ncols: (lambda E: E.tensor_scalar_min(
                qbf[:, 0:ncols], pstripe[:, 0:ncols], 1.0)))(),
                inc=(sV, 1), tag=f"qr3{mi}")
            fst["qv"] = sV.n
            qv = sV.n
            if mi < 2:
                ft = sid
                groups = [(db, [qbf[:, g4 * 256 + db * 128: g4 * 256 + (db + 1) * 128]
                                for g4 in range(4)], 512, dst[db, ft][:, :]) for db in range(2)]
            else:
                st, hh = sid
                groups = [(fb, [qbf[:, g2 * 768 + fb * 128: g2 * 768 + (fb + 1) * 128]
                                for g2 in range(2)], 256,
                           dst[fb, st][:, hh * 256:(hh + 1) * 256]) for fb in range(6)]
            for _, blocks, nbf, dstap in groups:
                s, ev = acquire_slot()
                bank_bf = pb[s * 4][:, 0:512].bitcast(BF16)
                first = True
                for i, blk in enumerate(blocks):
                    w = (evw(ev) + [(sV, qv)]) if first else []
                    em("tensor", (lambda i=i, blk=blk, bank_bf=bank_bf:
                        (lambda E: E.transpose(
                            bank_bf[:, i * 128:(i + 1) * 128], blk, idn[:])))(),
                        waits=w, inc=((sMM, 1) if i == len(blocks) - 1 else None),
                        tag=f"qT{mi}")
                    first = False
                mmv = sMM.n
                ti = tstseq[0] % 3
                tstseq[0] += 1
                em("scalar", (lambda nbf=nbf, bank_bf=bank_bf, ti=ti: (lambda E: E.activation(
                    tst[ti][:, 0:nbf], bank_bf[:, 0:nbf], AF.Copy)))(),
                    waits=[(sMM, mmv), (sQW3[ti], fst.get(f"tw{ti}", 0))],
                    inc=(sES, 1), tag=f"qTe{mi}")
                release_slot(s)
                ev2 = sES.n
                dma("gpsimd", dstap, tst[ti][:, 0:nbf],
                    waits=[(sES, ev2)], inc=(sQW3[ti], 16), tag=f"qw{mi}")
                fst[f"tw{ti}"] = sQW3[ti].n
        fst["qes"] = sES.n
        nst = FT if mi < 2 else 8
        fst[f"nst{mi}"] = fst.get(f"nst{mi}", 0) + 1
        if fst[f"nst{mi}"] == nst:
            fst[f"agq_ready{mi}"] = [(q, q.n) for q in sQW3]


    def ln_stats(r, tiles, discards, pre_s, pre_v, tag):
        for tb in range(4):
            em("vector", (lambda tb=tb: (lambda E: E.reduce_sum(
                stats[:, tb:tb + 1], tiles[tb], axis=AX.X)))(),
                waits=list(pre_v), tag=f"s1{tag}.{tb}")
            em("scalar", (lambda tb=tb: (lambda E: E.activation(
                discards[tb], tiles[tb], AF.Square,
                accum_out=stats[:, 4 + tb:5 + tb])))(),
                waits=list(pre_s), inc=(sS, 1), tag=f"s2{tag}.{tb}")
        sq_s = sS.n
        em("vector", lambda E: E.tensor_scalar_mul(stats[:, 8:12], stats[:, 0:4], 1.0 / H),
           tag=f"mu{tag}")
        em("vector", lambda E: E.tensor_scalar_mul(stats[:, 12:16], stats[:, 4:8], 1.0 / H),
           waits=[(sS, sq_s)], tag=f"e2{tag}")
        em("vector", lambda E: E.tensor_mul(stats[:, 56:60], stats[:, 8:12], stats[:, 8:12]),
           tag=f"msq{tag}")
        em("vector", lambda E: E.tensor_sub(stats[:, 12:16], stats[:, 12:16], stats[:, 56:60]),
           tag=f"var{tag}")
        em("vector", lambda E: E.tensor_scalar_add(stats[:, 12:16], stats[:, 12:16], EPS_LN),
           tag=f"ve{tag}")
        em("vector", lambda E: E.reciprocal(stats[:, 12:16], stats[:, 12:16]),
           inc=(sV, 1), tag=f"rv{tag}")
        rv_v = sV.n
        em("scalar", lambda E: E.activation(stats[:, 16:20], stats[:, 12:16], AF.Sqrt),
           waits=[(sV, rv_v)], inc=(sS, 1), tag=f"sqrt{tag}")
        sq2_s = sS.n
        em("vector", lambda E: E.tensor_scalar_mul(stats[:, 20:24], stats[:, 8:12], -1.0),
           tag=f"nm{tag}")
        em("vector", lambda E: E.tensor_mul(stats[:, 20:24], stats[:, 20:24], stats[:, 16:20]),
           waits=[(sS, sq2_s)], inc=(sV, 1), tag=f"nb{tag}")
        return sS.n, sV.n

    def transpose_batch3(blocks, dst3, wait_list, tag):
        s, ev = acquire_slot()
        bank_bf = pb[s * 4][:, 0:512].bitcast(BF16)
        n = len(blocks)
        first = True
        for i, blk in enumerate(blocks):
            w = (evw(ev) + list(wait_list)) if first else []
            em("tensor", (lambda i=i, blk=blk, bank_bf=bank_bf: (lambda E: E.transpose(
                bank_bf[:, i * 128:(i + 1) * 128], blk, idn[:])))(),
                waits=w, inc=((sMM, 1) if i == n - 1 else None),
                tag=f"T{tag}")
            first = False
        mmv = sMM.n
        bsrc = bank_bf.rearrange("p (i x) -> p i x", x=128)
        em("scalar", (lambda dst3=dst3, bsrc=bsrc, n=n:
            (lambda E: E.activation(dst3, bsrc[:, 0:n, :], AF.Copy)))(),
            waits=[(sMM, mmv)], inc=(sES, 1), tag=f"Te{tag}")
        release_slot(s)

    def transpose_batch(blocks, dst, dst_lo, nbf, wait_list, tag):
        s, ev = acquire_slot()
        bank_bf = pb[s * 4][:, 0:512].bitcast(BF16)
        first = True
        for i, blk in enumerate(blocks):
            w = (evw(ev) + list(wait_list)) if first else []
            em("tensor", (lambda i=i, blk=blk: (lambda E: E.transpose(
                bank_bf[:, i * 128:(i + 1) * 128], blk, idn[:])))(),
                waits=w, inc=((sMM, 1) if i == len(blocks) - 1 else None),
                tag=f"T{tag}")
            first = False
        mmv = sMM.n
        em("scalar", (lambda dst=dst, dst_lo=dst_lo, nbf=nbf, bank_bf=bank_bf:
            (lambda E: E.activation(dst[:, dst_lo:dst_lo + nbf],
                                    bank_bf[:, 0:nbf], AF.Copy)))(),
            waits=[(sMM, mmv)], inc=(sES, 1), tag=f"Te{tag}")
        release_slot(s)

    proj_cfg = [(fV, AF.Tanh), (siV, AF.Sigmoid), (svV, AF.Silu),
                (qV, AF.Copy), (uV, AF.Copy), (sgV, AF.Silu)]
    o2led = [0, 0]
    out_total = [()]

    def emit_round(r):
        x_tiles = [xV[:, tb * 2048:(tb + 1) * 2048] for tb in range(4)]
        pre = [(sX, xload_g[r]), (sEV, ffn2ev[r - 1])]
        ln1_disc = [out1V[:, tb * 2048:(tb + 1) * 2048] for tb in range(4)]
        s_ln1, v_ln1 = ln_stats(r, x_tiles, ln1_disc, pre, pre, f"L1.{r}")
        for tb in range(4):
            em("vector", (lambda tb=tb: (lambda E: E.tensor_scalar(
                yV[:, tb * 2048:(tb + 1) * 2048], x_tiles[tb],
                stats[:, 16 + tb:17 + tb], stats[:, 20 + tb:21 + tb],
                ALU.mult, ALU.add)))(),
                inc=((sV, 1) if tb == 3 else None), tag=f"y{r}.{tb}")
        y_s = sV.n

        for hb2 in range(8):
            blocks = [yV[:, tb * 2048 + (hb2 * 2 + hbl) * 128:
                         tb * 2048 + (hb2 * 2 + hbl) * 128 + 128]
                      for hbl in range(2) for tb in range(4)]
            transpose_batch(blocks, yTV, hb2 * 1024, 1024,
                            [(sV, y_s), (sK, g_consts)], f"y{r}.{hb2}")
        yT_e = sES.n

        proj_ev = {}
        g_v_box = [0]
        if r == 0:
            qgen = ffnq_quant_gen()
            qdone = [False]
        for m6 in range(6):
            dst, func = proj_cfg[m6]
            extra_ev = []
            if m6 in (0, 1):
                extra_ev = [(sEV, ffn2ev[r - 1])]
            if m6 in (2, 4):
                extra_ev = list(out_total[0])
            if m6 == 4:
                extra_ev.append((sV, g_v_box[0]))
            if m6 == 5 and r == 0:
                extra_ev = list(retile_done)
            for rbq in range(4):
                s, ev = acquire_slot()
                group_first = True
                for hbq in range(4):
                    si, wv = wchunk_load([(0, wall[rbq][m6, hbq][:, :])],
                                         extra_waits=[(sCC, ag_w_cc[rbq])],
                                         tag=f"pc{r}.{m6}.{rbq}.{hbq}")
                    for hbl in range(4):
                        hb = hbq * 4 + hbl
                        for rbi in range(4):
                            w = []
                            if group_first:
                                w = evw(ev) + [(sES, yT_e)]
                            if hbl == 0 and rbi == 0:
                                w.append((sW2[si], wv))
                            last_of_chunk = (hbl == 3 and rbi == 3)
                            em("tensor", (lambda s=s, rbi=rbi, si=si, hbl=hbl, hb=hb:
                                (lambda E: E.matmul(
                                    pb[s * 4 + rbi][:],
                                    wslot[:, si * 2048 + hbl * 512 + rbi * 128:
                                          si * 2048 + hbl * 512 + (rbi + 1) * 128],
                                    yTV[:, hb * 512:(hb + 1) * 512],
                                    start=(hb == 0), stop=(hb == 15))))(),
                                waits=w, inc=((sMM, 1) if last_of_chunk else None),
                                tag=f"P{r}.{m6}.{rbq}.{hbq}")
                            group_first = False
                    wchunk_done(si)
                mmv = sMM.n
                for rbi in range(4):
                    rb = rbq * 4 + rbi
                    em("scalar", (lambda dst=dst, func=func, rb=rb, s=s, rbi=rbi:
                        (lambda E: E.activation(
                            dst[:, rb * 512:(rb + 1) * 512], pb[s * 4 + rbi][:], func)
                         if func == AF.Copy else E.activation(
                            dst[:, rb * 512:(rb + 1) * 512], pb[s * 4 + rbi][:], func,
                            bias=zcol[:, 0:1])))(),
                        waits=[(sMM, mmv)] + extra_ev, inc=(sES, 1),
                        tag=f"Pe{r}.{m6}.{rbq}.{rbi}")
                release_slot(s)
                if r == 0 and m6 >= 3 and not qdone[0]:
                    for _ in range(3):
                        wk = next(qgen)
                        if wk is None:
                            qdone[0] = True
                            break
                        ffnq_emit_stripe(wk)
            proj_ev[m6] = sES.n
            if m6 == 2:
                em("vector", lambda E: E.tensor_mul(
                    siV[:, 0:8192], siV[:, 0:8192], svV[:, 0:8192]),
                    waits=[(sES, proj_ev[2])], inc=(sV, 1), tag=f"g{r}")
                g_v_box[0] = sV.n
                for rg in range(16):
                    em("vector", (lambda rg=rg: (lambda E: E.tensor_tensor_scan(
                        stV[:, rg * 512:(rg + 1) * 512], fV[:, rg * 512:(rg + 1) * 512],
                        siV[:, rg * 512:(rg + 1) * 512], 0.0, ALU.mult, ALU.add)))(),
                        waits=[(sV, g_v_box[0])], tag=f"scS{r}.{rg}")
                for rg in range(16):
                    em("vector", (lambda rg=rg: (lambda E: E.tensor_tensor_scan(
                        cV[:, rg * 512:(rg + 1) * 512], fV[:, rg * 512:(rg + 1) * 512],
                        fV[:, rg * 512:(rg + 1) * 512], 1.0, ALU.mult, ALU.bypass)))(),
                        tag=f"scC{r}.{rg}")
                cV3 = cV.rearrange("p (rg t) -> p rg t", t=512)
                sV3 = stV.rearrange("p (rg t) -> p rg t", t=512)
                em("vector", lambda E: E.tensor_copy(
                    summ[:, 0:16], cV3[:, :, 511:512].squeeze(2)), tag=f"sP{r}")
                em("vector", lambda E: E.tensor_copy(
                    summ[:, 16:32], sV3[:, :, 511:512].squeeze(2)),
                    inc=(sV, 1), tag=f"sQ{r}")
                summ_v = sV.n
                dma("gpsimd", cin[r].ap()[:, :], summ[:],
                    waits=[(sV, summ_v)], inc=(sCI, 16), tag=f"cin{r}")
                em("gpsimd", (lambda r=r: (lambda E: E.collective_compute(
                    "AllGather", ALU.bypass, replica_groups=[list(range(NCORES))],
                    ins=[cin[r].ap().opt()], outs=[cout[r].ap().opt()])))(),
                    waits=[(sCI, sCI.n)], inc=(sCC, 1), tag=f"AGc{r}")
                dma("gpsimd", s_all[:].rearrange("p (j f) -> p j f", f=32),
                    cout[r].ap().rearrange("(j p) f -> p j f", p=128),
                    waits=[(sCC, sCC.n)], inc=(sSA, 16), tag=f"sall{r}")
                sall_g = sSA.n
            if r == 0 and m6 == 0:
                ffnq_means()

        if r == 0:
            while not qdone[0]:
                wk = next(qgen)
                if wk is None:
                    qdone[0] = True
                    break
                ffnq_emit_stripe(wk)
            for mi in range(3):
                em("gpsimd", (lambda mi=mi: (lambda E: E.collective_compute(
                    "AllGather", ALU.bypass, replica_groups=[list(range(NCORES))],
                    ins=[wq_in[mi].ap().opt()], outs=[wq_ag[mi].ap().opt()])))(),
                    waits=list(fst[f"agq_ready{mi}"]), inc=(sCC, 1), tag=f"AGq{mi}")
                fst[f"ag{mi}_cc"] = sCC.n

        if r == 0:
            em("vector", lambda E: E.tensor_mul(
                uV[:, 0:8192], uV[:, 0:8192], sgV[:, 0:8192]),
                waits=[(sES, proj_ev[5])], inc=(sV, 1), tag=f"z{r}")
            z_v = sV.n

        em("vector", lambda E: E.tensor_copy(s_in[:], s_pref[:]),
           waits=[(sSA, sall_g)], tag=f"fi{r}")
        for j in range(8):
            Pj = s_all[:, j * 32:j * 32 + 16]
            Qj = s_all[:, j * 32 + 16:j * 32 + 32]
            em("vector", (lambda Pj=Pj, j=j: (lambda E: E.tensor_scalar(
                t16a[:], Pj, maskm[:, j:j + 1], maskn[:, j:j + 1], ALU.mult, ALU.add)))(),
                tag=f"f1.{j}")
            em("vector", lambda E: E.tensor_mul(s_in[:], s_in[:], t16a[:]), tag=f"f2.{j}")
            em("vector", (lambda Qj=Qj, j=j: (lambda E: E.tensor_scalar_mul(
                t16b[:], Qj, maskm[:, j:j + 1])))(), tag=f"f3.{j}")
            em("vector", lambda E: E.tensor_add(s_in[:], s_in[:], t16b[:]), tag=f"f4.{j}")
        for j in range(8):
            Pj = s_all[:, j * 32:j * 32 + 16]
            Qj = s_all[:, j * 32 + 16:j * 32 + 32]
            em("vector", (lambda Pj=Pj: (lambda E: E.tensor_mul(s_pref[:], s_pref[:], Pj)))(),
               tag=f"f5.{j}")
            em("vector", (lambda Qj=Qj: (lambda E: E.tensor_add(s_pref[:], s_pref[:], Qj)))(),
               tag=f"f6.{j}")
        for rg in range(16):
            em("vector", (lambda rg=rg: (lambda E: E.scalar_tensor_tensor(
                cV[:, rg * 512:(rg + 1) * 512], cV[:, rg * 512:(rg + 1) * 512],
                s_in[:, rg:rg + 1], stV[:, rg * 512:(rg + 1) * 512],
                ALU.mult, ALU.add)))(), tag=f"co{r}.{rg}")
        em("vector", lambda E: E.tensor_mul(stV[:, 0:8192], qV[:, 0:8192], cV[:, 0:8192]),
           waits=[(sES, proj_ev[3])], inc=(sV, 1), tag=f"tmp{r}")
        tmp_v = sV.n
        em("scalar", lambda E: E.activation(qV[:, 0:8192], stV[:, 0:8192], AF.Silu,
                                            bias=zcol[:, 0:1]),
           waits=[(sV, tmp_v)], inc=(sS, 1), tag=f"ro{r}")
        ro_s = sS.n
        if r > 0:
            em("vector", lambda E: E.tensor_mul(
                uV[:, 0:8192], uV[:, 0:8192], sgV[:, 0:8192]),
                waits=[(sES, proj_ev[5])], inc=(sV, 1), tag=f"z{r}")
            z_v = sV.n
        em("vector", lambda E: E.memset(stats[:, 0:8], 0.0), tag=f"l2z{r}")
        sqv_led = {}
        for ht in range(4):
            s, ev = acquire_slot()
            group_first = True
            for mseq, m2 in enumerate((1, 0)):
                stat_src = qV if m2 == 0 else zV
                for kq in range(4):
                    si, wv = wchunk_load([(0, wall[ht][6 + m2, kq][:, :])],
                                         extra_waits=[(sCC, ag_w_cc[ht])],
                                         tag=f"mc{r}.{ht}.{m2}.{kq}")
                    for kbl in range(4):
                        kb = kq * 4 + kbl
                        for tbi in range(4):
                            w = []
                            if group_first:
                                w = evw(ev) + [(sV, z_v)]
                            if mseq == 1 and kq == 0 and kbl == 0 and tbi == 0:
                                w.append((sS, ro_s))
                            if kbl == 0 and tbi == 0:
                                w.append((sW2[si], wv))
                            last_of_chunk = (kbl == 3 and tbi == 3)
                            em("tensor", (lambda s=s, tbi=tbi, si=si, kbl=kbl, kb=kb,
                                          stat_src=stat_src, mseq=mseq: (lambda E: E.matmul(
                                pb[s * 4 + tbi][:],
                                stat_src[:, kb * 512 + tbi * 128: kb * 512 + (tbi + 1) * 128],
                                wslot[:, si * 2048 + kbl * 512: si * 2048 + (kbl + 1) * 512],
                                start=(mseq == 0 and kb == 0), stop=(mseq == 1 and kb == 15))))(),
                                waits=w, inc=((sMM, 1) if last_of_chunk else None),
                                tag=f"M{r}.{ht}.{m2}.{kq}")
                            group_first = False
                    wchunk_done(si)
            mmv = sMM.n
            for tbi in range(4):
                em("vector", (lambda s=s, tbi=tbi, ht=ht: (lambda E: E.tensor_tensor(
                    out1V[:, tbi * 2048 + ht * 512: tbi * 2048 + (ht + 1) * 512],
                    pb[s * 4 + tbi][:],
                    xV[:, tbi * 2048 + ht * 512: tbi * 2048 + (ht + 1) * 512],
                    ALU.add)))(),
                    waits=[(sMM, mmv)], inc=(sEV, 1), tag=f"Me{r}.{ht}.{tbi}")
            evht = sEV.n
            for tbi in range(4):
                blk = out1V[:, tbi * 2048 + ht * 512: tbi * 2048 + (ht + 1) * 512]
                em("vector", (lambda blk=blk: (lambda E: E.reduce_sum(
                    stats[:, 59:60], blk, axis=AX.X)))(), tag=f"l2a{r}.{ht}.{tbi}")
                em("vector", (lambda tbi=tbi: (lambda E: E.tensor_add(
                    stats[:, tbi:tbi + 1], stats[:, tbi:tbi + 1],
                    stats[:, 59:60])))(), tag=f"l2b{r}.{ht}.{tbi}")
                sw = [(sEV, evht)]
                if (ht, tbi) in sqv_led:
                    pass
                if ht > 0:
                    sw.append((sV, sqv_led[(ht - 1, tbi)]))
                em("scalar", (lambda blk=blk, tbi=tbi: (lambda E: E.activation(
                    xV[:, tbi * 2048:tbi * 2048 + 512], blk, AF.Square,
                    accum_out=stats[:, 60 + tbi:61 + tbi])))(),
                    waits=sw, inc=(sS, 1), tag=f"l2s{r}.{ht}.{tbi}")
                sqn = sS.n
                em("vector", (lambda tbi=tbi: (lambda E: E.tensor_add(
                    stats[:, 4 + tbi:5 + tbi], stats[:, 4 + tbi:5 + tbi],
                    stats[:, 60 + tbi:61 + tbi])))(),
                    waits=[(sS, sqn)], inc=(sV, 1), tag=f"l2c{r}.{ht}.{tbi}")
                sqv_led[(ht, tbi)] = sV.n
            release_slot(s)
        mix_ev = sEV.n

        o1_tiles = [out1V[:, tb * 2048:(tb + 1) * 2048] for tb in range(4)]
        em("vector", lambda E: E.tensor_scalar_mul(stats[:, 8:12], stats[:, 0:4],
                                                   1.0 / H), tag=f"muL2{r}")
        em("vector", lambda E: E.tensor_scalar_mul(stats[:, 12:16], stats[:, 4:8],
                                                   1.0 / H), tag=f"e2L2{r}")
        em("vector", lambda E: E.tensor_mul(stats[:, 56:60], stats[:, 8:12],
                                            stats[:, 8:12]), tag=f"msqL2{r}")
        em("vector", lambda E: E.tensor_sub(stats[:, 12:16], stats[:, 12:16],
                                            stats[:, 56:60]), tag=f"varL2{r}")
        em("vector", lambda E: E.tensor_scalar_add(stats[:, 12:16], stats[:, 12:16],
                                                   EPS_LN), tag=f"veL2{r}")
        em("vector", lambda E: E.reciprocal(stats[:, 12:16], stats[:, 12:16]),
           inc=(sV, 1), tag=f"rvL2{r}")
        rv2_v = sV.n
        em("scalar", lambda E: E.activation(stats[:, 16:20], stats[:, 12:16], AF.Sqrt),
           waits=[(sV, rv2_v)], inc=(sS, 1), tag=f"sqrtL2{r}")
        s_ln2 = sS.n
        em("vector", lambda E: E.tensor_scalar_mul(stats[:, 20:24], stats[:, 8:12],
                                                   -1.0), tag=f"nmL2{r}")
        em("vector", lambda E: E.tensor_mul(stats[:, 20:24], stats[:, 20:24],
                                            stats[:, 16:20]),
           waits=[(sS, s_ln2)], inc=(sV, 1), tag=f"nbL2{r}")
        rq2_tb = [0, 0, 0, 0]
        for tb in range(4):
            em("vector", (lambda tb=tb: (lambda E: E.tensor_scalar(
                y2stage, o1_tiles[tb],
                stats[:, 16 + tb:17 + tb], stats[:, 20 + tb:21 + tb],
                ALU.mult, ALU.add)))(), tag=f"y2.{r}.{tb}")
            em("vector", (lambda tb=tb: (lambda E: E.reduce_max(
                stats[:, 24 + tb:25 + tb], y2stage, axis=AX.X,
                apply_absolute_value=True)))(), tag=f"qm{r}.{tb}")
            em("vector", (lambda tb=tb: (lambda E: E.tensor_scalar_max(
                stats[:, 24 + tb:25 + tb], stats[:, 24 + tb:25 + tb], EPS_Q)))(),
                tag=f"qc{r}.{tb}")
            em("vector", (lambda tb=tb: (lambda E: E.tensor_scalar_mul(
                stats[:, 32 + tb:33 + tb], stats[:, 24 + tb:25 + tb], 1.0 / 127.0)))(),
                tag=f"rs{r}.{tb}")
            em("vector", (lambda tb=tb: (lambda E: E.reciprocal(
                stats[:, 28 + tb:29 + tb], stats[:, 24 + tb:25 + tb])))(),
                tag=f"r1{r}.{tb}")
            em("vector", (lambda tb=tb: (lambda E: E.tensor_scalar_mul(
                stats[:, 28 + tb:29 + tb], stats[:, 28 + tb:29 + tb], 127.0)))(),
                tag=f"r2{r}.{tb}")
            em("vector", (lambda tb=tb: (lambda E: E.tensor_scalar(
                ln2scr, y2stage, stats[:, 28 + tb:29 + tb], MAGIC, ALU.mult, ALU.add)))(),
                tag=f"rq1{r}.{tb}")
            em("vector", (lambda tb=tb: (lambda E: E.tensor_scalar_add(
                y2qV[:, tb * 2048:(tb + 1) * 2048], ln2scr, -MAGIC)))(),
                inc=(sV, 1), tag=f"rq2{r}.{tb}")
            rq2_tb[tb] = sV.n
        y2q_v = sV.n
        em("vector", lambda E: E.tensor_scalar(
            stats[:, 44:48], stats[:, 32:36], mcl[:, 0:1], None, ALU.mult),
            waits=[(sV, fst.get("means_v", 0))], tag=f"sc1{r}")
        em("vector", lambda E: E.tensor_scalar(
            stats[:, 48:52], stats[:, 32:36], mcl[:, 1:2], None, ALU.mult),
            inc=(sV, 1), tag=f"sc3{r}")
        sc_v = sV.n

        y2qT3 = y2qTV.rearrange("p (dg x) -> p dg x", x=512)
        for tb in range(4):
            for grp in range(2):
                blocks = [y2qV[:, tb * 2048 + (grp * 8 + i) * 128:
                              tb * 2048 + (grp * 8 + i) * 128 + 128] for i in range(8)]
                transpose_batch3(blocks,
                                 y2qT3[:, grp * 8:(grp + 1) * 8,
                                       tb * 128:(tb + 1) * 128],
                                 [(sV, rq2_tb[tb])], f"q{r}.{tb}.{grp}")
        y2qT_e = sES.n

        for mi, (dstV, func, sccol) in enumerate(
                [(aV, AF.Silu, 44), (bV, AF.Copy, 48)]):
            agw = [(sCC, fst.get("ag2_cc", 0))]
            for ft in range(FT):
                s, ev = acquire_slot()
                group_first = True
                for q in range(4):
                    srcs = [(i * 512, wq_ag[mi][4 * q + i, ft][:, :]) for i in range(4)]
                    si, wv = wchunk_load(srcs, extra_waits=agw, tag=f"fc{r}.{mi}.{ft}.{q}")
                    for dgl in range(4):
                        dgb = q * 4 + dgl
                        for tbi in range(4):
                            w = []
                            if group_first:
                                w = evw(ev) + [(sES, y2qT_e), (sV, sc_v)]
                            if dgl == 0 and tbi == 0:
                                w.append((sW2[si], wv))
                            last_of_chunk = (dgl == 3 and tbi == 3)
                            em("tensor", (lambda s=s, tbi=tbi, si=si, dgl=dgl, dgb=dgb:
                                (lambda E: E.matmul(
                                    pb[s * 4 + tbi][:],
                                    y2qTV[:, dgb * 512 + tbi * 128: dgb * 512 + (tbi + 1) * 128],
                                    wslot[:, si * 2048 + dgl * 512: si * 2048 + (dgl + 1) * 512],
                                    start=(dgb == 0), stop=(dgb == 15))))(),
                                waits=w, inc=((sMM, 1) if last_of_chunk else None),
                                tag=f"F1.{r}.{mi}.{ft}.{q}")
                            group_first = False
                    wchunk_done(si)
                mmv = sMM.n
                for tbi in range(4):
                    em("scalar", (lambda dstV=dstV, func=func, tbi=tbi, ft=ft, s=s,
                                  sccol=sccol: (lambda E: E.activation(
                        dstV[:, tbi * FH + ft * 512: tbi * FH + ft * 512 + 512],
                        pb[s * 4 + tbi][:], func,
                        scale=stats[:, sccol + tbi:sccol + tbi + 1])
                        if func == AF.Copy else E.activation(
                        dstV[:, tbi * FH + ft * 512: tbi * FH + ft * 512 + 512],
                        pb[s * 4 + tbi][:], func,
                        bias=zcol[:, 0:1], scale=stats[:, sccol + tbi:sccol + tbi + 1])))(),
                        waits=[(sMM, mmv)], inc=(sES, 1), tag=f"F1e{r}.{mi}.{ft}.{tbi}")
                release_slot(s)
                if mi == 1:
                    if ft == 0:
                        em("vector", lambda E: E.memset(stats[:, 24:28], 0.0),
                           tag=f"hz{r}")
                    ev_g = sES.n
                    for tbi in range(4):
                        lo = tbi * FH + ft * 512
                        em("vector", (lambda lo=lo: (lambda E: E.tensor_mul(
                            aV[:, lo:lo + 512], aV[:, lo:lo + 512],
                            bV[:, lo:lo + 512])))(),
                            waits=([(sES, ev_g)] if tbi == 0 else []),
                            tag=f"h{r}.{ft}.{tbi}")
                        em("vector", (lambda lo=lo: (lambda E: E.reduce_max(
                            stats[:, 60:61], aV[:, lo:lo + 512], axis=AX.X,
                            apply_absolute_value=True)))(), tag=f"hm{r}.{ft}.{tbi}")
                        em("vector", (lambda tbi=tbi: (lambda E: E.tensor_max(
                            stats[:, 24 + tbi:25 + tbi], stats[:, 24 + tbi:25 + tbi],
                            stats[:, 60:61])))(), tag=f"hx{r}.{ft}.{tbi}")
        ffn1_ev = sES.n

        hq_tb = [0, 0, 0, 0]
        st_v = [0, 0, 0, 0]
        for tb in range(4):
            em("vector", (lambda tb=tb: (lambda E: E.tensor_scalar_max(
                stats[:, 24 + tb:25 + tb], stats[:, 24 + tb:25 + tb], EPS_Q)))(),
                tag=f"hc{r}.{tb}")
            em("vector", (lambda tb=tb: (lambda E: E.tensor_scalar_mul(
                stats[:, 40 + tb:41 + tb], stats[:, 24 + tb:25 + tb], 1.0 / 127.0)))(),
                tag=f"hr{r}.{tb}")
            em("vector", (lambda tb=tb: (lambda E: E.reciprocal(
                stats[:, 36 + tb:37 + tb], stats[:, 24 + tb:25 + tb])))(),
                tag=f"h1{r}.{tb}")
            em("vector", (lambda tb=tb: (lambda E: E.tensor_scalar_mul(
                stats[:, 36 + tb:37 + tb], stats[:, 36 + tb:37 + tb], 127.0)))(),
                inc=(sV, 1), tag=f"h2{r}.{tb}")
            st_v[tb] = sV.n
        hqst2 = [hqstage, eqv(14080, 2816)]
        hq1_led = {}
        hq2_led = {}
        for tb in range(4):
            for hh in range(2):
                k = tb * 2 + hh
                w = [(sV, st_v[tb])]
                if k >= 2:
                    w.append((sV, hq2_led[k - 2]))
                em("scalar", (lambda tb=tb, hh=hh, k=k: (lambda E: E.activation(
                    hqst2[k % 2],
                    aV[:, tb * FH + hh * 2816: tb * FH + (hh + 1) * 2816],
                    AF.Copy, bias=MAGIC,
                    scale=stats[:, 36 + tb:37 + tb])))(),
                    waits=w, inc=(sES, 1), tag=f"hq1{r}.{tb}.{hh}")
                hq1_led[k] = sES.n
                em("vector", (lambda tb=tb, hh=hh, k=k: (lambda E: E.tensor_scalar_add(
                    bV[:, tb * FH + hh * 2816: tb * FH + (hh + 1) * 2816],
                    hqst2[k % 2], -MAGIC)))(),
                    waits=[(sES, hq1_led[k])], inc=(sV, 1), tag=f"hq2{r}.{tb}.{hh}")
                hq2_led[k] = sV.n
            hq_tb[tb] = hq2_led[tb * 2 + 1]
        em("vector", lambda E: E.tensor_scalar(
            stats[:, 52:56], stats[:, 40:44], mcl[:, 2:3], None, ALU.mult),
            inc=(sV, 1), tag=f"sc2{r}")
        hq_v = sV.n

        hqT3 = aV.rearrange("p (fb x) -> p fb x", x=512)
        for tb in range(4):
            for grp in range(6):
                fb0 = grp * 8
                nb = min(8, FB - fb0)
                blocks = [bV[:, tb * FH + (fb0 + i) * 128:
                             tb * FH + (fb0 + i) * 128 + 128] for i in range(nb)]
                transpose_batch3(blocks,
                                 hqT3[:, fb0:fb0 + nb, tb * 128:(tb + 1) * 128],
                                 [(sV, hq_tb[tb])], f"hq{r}.{tb}.{grp}")
        hqT_e = sES.n

        agw2 = [(sCC, fst.get("ag2_cc", 0))]
        for ht in range(4):
            s, ev = acquire_slot()
            group_first = True
            for q in range(11):
                srcs = [(i * 512, wq_ag[2][4 * q + i, ht][:, :]) for i in range(4)]
                si, wv = wchunk_load(srcs, extra_waits=agw2, tag=f"gc{r}.{ht}.{q}")
                for fbl in range(4):
                    fb = q * 4 + fbl
                    for tbi in range(4):
                        w = []
                        if group_first:
                            w = evw(ev) + [(sES, hqT_e)]
                        if fbl == 0 and tbi == 0:
                            w.append((sW2[si], wv))
                        last_of_chunk = (fbl == 3 and tbi == 3)
                        em("tensor", (lambda s=s, tbi=tbi, si=si, fbl=fbl, fb=fb:
                            (lambda E: E.matmul(
                                pb[s * 4 + tbi][:],
                                aV[:, fb * 512 + tbi * 128: fb * 512 + (tbi + 1) * 128],
                                wslot[:, si * 2048 + fbl * 512: si * 2048 + (fbl + 1) * 512],
                                start=(fb == 0), stop=(fb == FB - 1))))(),
                            waits=w, inc=((sMM, 1) if last_of_chunk else None),
                            tag=f"F2.{r}.{ht}.{q}")
                        group_first = False
                wchunk_done(si)
            mmv = sMM.n
            for tbi in range(4):
                oi = (ht * 4 + tbi) % 2
                em("vector", (lambda s=s, tbi=tbi, ht=ht, oi=oi: (lambda E: E.scalar_tensor_tensor(
                    o2st[oi], pb[s * 4 + tbi][:], stats[:, 52 + tbi:53 + tbi],
                    out1V[:, tbi * 2048 + ht * 512: tbi * 2048 + (ht + 1) * 512],
                    ALU.mult, ALU.add)))(),
                    waits=[(sMM, mmv), (sO2[oi], o2led[oi])], inc=(sEV, 1),
                    tag=f"F2e{r}.{ht}.{tbi}")
                evn = sEV.n
                dma("gpsimd",
                    out[r * TS + tbi * 128: r * TS + (tbi + 1) * 128,
                        ht * 512:(ht + 1) * 512],
                    o2st[oi], waits=[(sEV, evn)], inc=(sO2[oi], 16), tag=f"o{r}.{ht}.{tbi}")
                o2led[oi] = sO2[oi].n
            release_slot(s)
        ffn2ev[r] = sEV.n
        xfree_mm[r + 1] = sMM.n
        out_total[0] = ((sO2[0], sO2[0].n), (sO2[1], sO2[1].n))

    for r in range(ROUNDS):
        emit_round(r)
        if r + 1 < ROUNDS:
            load_x(r + 1)

    em("gpsimd", None, waits=[(sO2[0], sO2[0].n), (sO2[1], sO2[1].n)], tag="fin")

    P.check()

    for s in sems:
        s.h = nc.alloc_semaphore(s.name)
    with nc.Block() as block:
        P.emit(block)
    return nc


_nc_cache = {}


def _get_nc():
    if "nc" not in _nc_cache:
        _nc_cache["nc"] = build_nc()
    return _nc_cache["nc"]


def kernel(L0_attn_x, L0_attn_w_f, L0_attn_w_i, L0_attn_w_v, L0_attn_w_q,
           L0_attn_w_ro, L0_attn_w_aux, L0_attn_w_up, L0_attn_w_gate,
           L0_attn_w_down, L0_attn_init_state, L0_ffn_w1, L0_ffn_w2, L0_ffn_w3):
    x = np.ascontiguousarray(np.asarray(L0_attn_x, dtype=np.float32).reshape(16384, 2048))
    wlist = [np.ascontiguousarray(np.asarray(a, dtype=np.float32)) for a in [
        L0_attn_w_f, L0_attn_w_i, L0_attn_w_v, L0_attn_w_q, L0_attn_w_up,
        L0_attn_w_gate, L0_attn_w_ro, L0_attn_w_down]]
    w1 = np.asarray(L0_ffn_w1, dtype=np.float32)
    w2 = np.asarray(L0_ffn_w2, dtype=np.float32)
    w3 = np.asarray(L0_ffn_w3, dtype=np.float32)
    init = np.asarray(L0_attn_init_state, dtype=np.float32)
    init_s = np.ascontiguousarray(init.reshape(16, 128).T)
    idn = np.eye(128, dtype=ml_dtypes.bfloat16)

    nc = _get_nc()
    in_maps = []
    for c in range(NCORES):
        xin = np.ascontiguousarray(np.concatenate(
            [x[(8 * r + c) * TS:(8 * r + c + 1) * TS] for r in range(ROUNDS)], axis=0))
        mask_m = np.zeros((128, 8), dtype=np.float32)
        mask_m[:, :c] = 1.0
        mask_n = np.ascontiguousarray(1.0 - mask_m)
        w1sh = np.ascontiguousarray(w1[:, c * 256:(c + 1) * 256])
        w3sh = np.ascontiguousarray(w3[:, c * 256:(c + 1) * 256])
        w2sh = np.zeros((2048, 768), dtype=np.float32)
        lo = c * 768
        hi = min(FH, lo + 768)
        if lo < FH:
            w2sh[:, :hi - lo] = w2[:, lo:hi]
        im = dict(xin=xin, init_s=init_s, mask_m=mask_m, mask_n=mask_n,
                  idn=idn, w1s=w1sh, w3s=w3sh, w2s=w2sh, wmine=wlist[c])
        in_maps.append(im)

    res = run_bass_kernel_spmd(nc, in_maps, core_ids=list(range(NCORES)))

    full = np.empty((16384, 2048), dtype=np.float32)
    for c in range(NCORES):
        oc = np.asarray(res.results[c]["out"])
        for r in range(ROUNDS):
            full[(8 * r + c) * TS:(8 * r + c + 1) * TS] = oc[r * TS:(r + 1) * TS]
    return full.reshape(1, 16384, 2048)


# version 18 (final)
# speedup vs baseline: 1.0298x; 1.0010x over previous
import numpy as np
import ml_dtypes
import concourse.bass as bass
import concourse.mybir as mybir
from concourse.bass_utils import run_bass_kernel_spmd

F32 = mybir.dt.float32
BF16 = mybir.dt.bfloat16
AF = mybir.ActivationFunctionType
ALU = mybir.AluOpType
AX = mybir.AxisListType

NCORES = 8
ROUNDS = 4
TS = 512
H = 2048
FH = 5632
FT = 11
FB = 44
HT = 4
MAGIC = float(np.float32(1.5 * 2 ** 23))
EPS_LN = 1e-5
EPS_Q = 1e-5
NW = float(FH * H)

A_X = 0
A_X2 = 8192
A_YT = 16384
A_F = 20480
A_SI = 24576
A_ST = 28672
A_C = 32768
A_Q = 36864
A_SV = 40960
A_END = 45056
P_STRIPE = 8192
P_QBF = 9728
P_TST = 10496
A_SLAB = 12288
A_SG = 12288


class Sem:
    def __init__(self, name):
        self.name = name
        self.n = 0
        self.h = None


class Plan:
    ENGS = ("sync", "gpsimd", "tensor", "vector", "scalar")

    def __init__(self):
        self.items = {e: [] for e in self.ENGS}
        self.lastwait = {e: {} for e in self.ENGS}

    def em(self, eng, fn, waits=(), inc=None, tag=""):
        w = []
        for sem, val in waits:
            if val <= 0:
                continue
            if self.lastwait[eng].get(sem.name, -1) >= val:
                continue
            self.lastwait[eng][sem.name] = val
            w.append((sem, val))
        self.items[eng].append((tuple(w), fn, inc, tag))
        if inc is not None:
            inc[0].n += inc[1]

    def check(self):
        pos = {e: 0 for e in self.ENGS}
        cnt = {}
        progress = True
        while progress:
            progress = False
            for e in self.ENGS:
                while pos[e] < len(self.items[e]):
                    waits, _, inc, _ = self.items[e][pos[e]]
                    if all(cnt.get(s.name, 0) >= v for s, v in waits):
                        if inc is not None:
                            s, k = inc
                            cnt[s.name] = cnt.get(s.name, 0) + k
                        pos[e] += 1
                        progress = True
                    else:
                        break
        stuck = {e: pos[e] for e in self.ENGS if pos[e] < len(self.items[e])}
        if stuck:
            msg = []
            for e, p in stuck.items():
                waits, _, _, tag = self.items[e][p]
                wd = [(s.name, v, cnt.get(s.name, 0)) for s, v in waits]
                msg.append(f"{e}@{p}/{len(self.items[e])} tag={tag} waits={wd}")
            raise RuntimeError("plan deadlock: " + " | ".join(msg))

    def emit(self, block):
        def mk(eng_name):
            drainy = eng_name in ("vector", "scalar")
            def body(eng):
                for waits, fn, inc, _ in self.items[eng_name]:
                    for s, v in waits:
                        eng.wait_ge(s.h, v)
                    ins = fn(eng) if fn is not None else None
                    if inc is not None:
                        s, k = inc
                        if ins is not None:
                            ins.then_inc(s.h, k)
                        else:
                            eng.sem_inc(s.h, k)
                    if drainy and ins is not None:
                        eng.drain()
            return body
        block.sync(mk("sync"))
        block.gpsimd(mk("gpsimd"))
        block.tensor(mk("tensor"))
        block.vector(mk("vector"))
        block.scalar(mk("scalar"))


def build_nc():
    nc = bass.Bass("TRN2", target_bir_lowering=False)

    xin = nc.dram_tensor("xin", [ROUNDS * TS, H], F32, kind="ExternalInput").ap()
    wmine = nc.dram_tensor("wmine", [H, H], F32, kind="ExternalInput").ap()
    w1s = nc.dram_tensor("w1s", [FH, 256], F32, kind="ExternalInput").ap()
    w3s = nc.dram_tensor("w3s", [FH, 256], F32, kind="ExternalInput").ap()
    w2s = nc.dram_tensor("w2s", [H, 768], F32, kind="ExternalInput").ap()
    init_in = nc.dram_tensor("init_s", [128, 16], F32, kind="ExternalInput").ap()
    mm_in = nc.dram_tensor("mask_m", [128, 8], F32, kind="ExternalInput").ap()
    mn_in = nc.dram_tensor("mask_n", [128, 8], F32, kind="ExternalInput").ap()
    idn_in = nc.dram_tensor("idn", [128, 128], BF16, kind="ExternalInput").ap()
    out = nc.dram_tensor("out", [ROUNDS * TS, H], F32, kind="ExternalOutput").ap()

    wri = [nc.dram_tensor(f"wri{j}", [4, 128, 2048], BF16) for j in range(4)]
    wall = [nc.dram_tensor(f"wall{j}", [8, 4, 128, 2048], BF16, addr_space="Shared") for j in range(4)]
    wq_in = [nc.dram_tensor("w1q_in", [2, FT, 128, 512], BF16),
             nc.dram_tensor("w3q_in", [2, FT, 128, 512], BF16),
             nc.dram_tensor("w2q_in", [6, HT, 128, 512], BF16)]
    wq_ag = [nc.dram_tensor("w1q_ag", [16, FT, 128, 512], BF16, addr_space="Shared"),
             nc.dram_tensor("w3q_ag", [16, FT, 128, 512], BF16, addr_space="Shared"),
             nc.dram_tensor("w2q_ag", [48, HT, 128, 512], BF16, addr_space="Shared")]
    ar_in = nc.dram_tensor("ar_in", [128, 4], F32)
    ar_out = nc.dram_tensor("ar_out", [128, 4], F32, addr_space="Shared")
    cin = [nc.dram_tensor(f"cin{r}", [128, 32], F32) for r in range(ROUNDS)]
    cout = [nc.dram_tensor(f"cout{r}", [1024, 32], F32, addr_space="Shared") for r in range(ROUNDS)]

    arena = nc.alloc_sbuf_tensor("arena", [128, A_END], F32)
    wslot = nc.alloc_sbuf_tensor("wslot", [128, 6144], BF16)
    stats = nc.alloc_sbuf_tensor("stats", [128, 64], F32)
    summ = nc.alloc_sbuf_tensor("summ", [128, 32], F32)
    s_all = nc.alloc_sbuf_tensor("s_all", [128, 256], F32)
    s_pref = nc.alloc_sbuf_tensor("s_pref", [128, 16], F32)
    s_in = nc.alloc_sbuf_tensor("s_in", [128, 16], F32)
    t16a = nc.alloc_sbuf_tensor("t16a", [128, 16], F32)
    t16b = nc.alloc_sbuf_tensor("t16b", [128, 16], F32)
    maskm = nc.alloc_sbuf_tensor("maskm", [128, 8], F32)
    maskn = nc.alloc_sbuf_tensor("maskn", [128, 8], F32)
    initb = nc.alloc_sbuf_tensor("initb", [128, 16], F32)
    partials = nc.alloc_sbuf_tensor("partials", [128, 4], F32)
    ones_f32 = nc.alloc_sbuf_tensor("ones_f32", [128, 128], F32)
    means = nc.alloc_sbuf_tensor("means", [128, 4], F32)
    mcl = nc.alloc_sbuf_tensor("mcl", [128, 4], F32)
    swq = nc.alloc_sbuf_tensor("swq", [128, 4], F32)
    zcol = nc.alloc_sbuf_tensor("zcol", [128, 1], F32)
    ones_bf = nc.alloc_sbuf_tensor("ones_bf", [128, 128], BF16)
    idn = nc.alloc_sbuf_tensor("idn_s", [128, 128], BF16)

    pb = [nc.alloc_psum_tensor(f"pb{i}", [128, 512], F32) for i in range(8)]

    sMM = Sem("sMM")
    sEV = Sem("sEV")
    sES = Sem("sES")
    sV = Sem("sV")
    sS = Sem("sS")
    sCC = Sem("sCC")
    sK = Sem("sK")
    sX = Sem("sX")
    sP = [Sem("sP0"), Sem("sP1")]
    sSC = [Sem(f"sSC{m}") for m in range(2)]
    sQW3 = [Sem(f"sQW{i}") for i in range(3)]
    sW2 = [Sem("sW0"), Sem("sW1"), Sem("sW2s")]
    sST = Sem("sST")
    sGQ = Sem("sGQ")
    sAR = Sem("sAR")
    sSA = Sem("sSA")
    sCI = Sem("sCI")
    sO2 = [Sem("sO0"), Sem("sO1")]
    sems = [sMM, sEV, sES, sV, sS, sCC, sK, sX] + sP + sSC + sW2 + sQW3 + [
        sST, sGQ, sAR, sSA, sCI] + sO2

    def eqv(off, n):
        return arena[:, off:off + n]

    def bfv(off, neq):
        return arena[:, off:off + neq].bitcast(BF16)

    xV = eqv(A_X, 8192)
    y2stage = eqv(A_X, 2048)
    ln2scr = eqv(A_X + 2048, 2048)
    aV = bfv(A_X, 11264)
    hqstage = eqv(11264, 2816)
    sgV = bfv(A_SG, 4096)
    yTV = bfv(A_YT, 4096)
    y2qTV = bfv(A_YT, 4096)
    fV = bfv(A_F, 4096)
    ln1scr = eqv(A_F, 2048)
    out1V = eqv(A_F, 8192)
    yV = bfv(A_SI, 4096)
    siV = bfv(A_SI, 4096)
    stV = bfv(A_ST, 4096)
    cV = bfv(A_C, 4096)
    qV = bfv(A_Q, 4096)
    y2qV = bfv(A_Q, 4096)
    bV = bfv(A_ST, 11264)
    svV = bfv(A_SV, 4096)
    uV = bfv(A_SV, 4096)
    zV = bfv(A_SV, 4096)
    o2st = [eqv(A_SV + i * 512, 512) for i in range(2)]
    pstripe = eqv(P_STRIPE, 1536)
    qbf = bfv(P_QBF, 768)
    tst = [bfv(P_TST + i * 256, 256) for i in range(3)]
    slabv = bfv(A_SLAB, 2048)

    P = Plan()
    em = P.em

    def dma(eng, o, i, waits=(), inc=None, tag="", **kw):
        em(eng, (lambda o=o, i=i, kw=kw: (lambda E: E.dma_start(out=o, in_=i, **kw)))(),
           waits=waits, inc=inc, tag=tag)

    slot_ev = [(0, 0), (0, 0)]
    slot_ctr = [0]

    def acquire_slot():
        s = slot_ctr[0] % 2
        slot_ctr[0] += 1
        return s, slot_ev[s]

    def release_slot(s):
        slot_ev[s] = (sES.n, sEV.n)

    def evw(ev):
        return [(sES, ev[0]), (sEV, ev[1])]

    wfree = [0, 0, 0]
    wseq = [0]

    def wchunk_load(srcs, extra_waits=(), tag=""):
        si = wseq[0] % 3
        wseq[0] += 1
        first = True
        n = len(srcs)
        for idx, (lo, src) in enumerate(srcs):
            w = list(extra_waits) + ([(sMM, wfree[si])] if first else [])
            width = 2048 if n == 1 else 512
            dma("sync", wslot[:, si * 2048 + lo: si * 2048 + lo + width],
                src, waits=w, inc=(sW2[si], 16), tag=tag)
            first = False
        return si, sW2[si].n

    def wchunk_done(si):
        wfree[si] = sMM.n

    em("vector", lambda E: E.memset(zcol[:], 0.0), tag="z0")
    em("vector", lambda E: E.memset(ones_bf[:], 1.0), tag="ones")
    em("vector", lambda E: E.memset(ones_f32[:], 1.0), tag="ones32")
    dma("gpsimd", idn[:], idn_in[:, :], inc=(sK, 16), tag="idn")
    dma("gpsimd", maskm[:], mm_in[:, :], inc=(sK, 16), tag="mm")
    dma("gpsimd", maskn[:], mn_in[:, :], inc=(sK, 16), tag="mn")
    dma("gpsimd", initb[:], init_in[:, :], inc=(sK, 16), tag="init")
    g_consts = sK.n
    em("vector", lambda E: E.tensor_copy(s_pref[:], initb[:]),
       waits=[(sK, g_consts)], inc=(sV, 1), tag="spref0")

    ffn2ev = {-1: 0}
    xfree_mm = [0] * (ROUNDS + 1)
    xload_g = [0] * ROUNDS

    def load_x(r, eng="gpsimd"):
        dma(eng, xV.rearrange("p (t d) -> p t d", t=4),
            xin[r * TS:(r + 1) * TS, :].rearrange("(t p) d -> p t d", p=128),
            waits=[(sMM, xfree_mm[r])], inc=(sX, 16), tag=f"x{r}")
        xload_g[r] = sX.n

    load_x(0, eng="scalar")

    slabh = [bfv(A_SLAB, 2048).rearrange("p (t d) -> p t d", d=512),
             bfv(A_SLAB + 2048, 2048).rearrange("p (t d) -> p t d", d=512)]
    prev_writes_w = [None, None]
    ag_w_cc = {}
    for j in range(4):
        src_j = wmine[:, j * 512:(j + 1) * 512].rearrange("(t p) d -> p t d", p=128)
        for h in range(2):
            dma("gpsimd", slabh[h][:, :, :], src_j[:, 8 * h:8 * h + 8, :],
                waits=[prev_writes_w[h]] if prev_writes_w[h] else [],
                inc=(sP[h], 16), tag=f"slab{j}.{h}")
            lg = sP[h].n
            for q in (2 * h, 2 * h + 1):
                dma("sync", wri[j][q][:, :].rearrange("p (t d) -> p t d", d=512),
                    slabh[h][:, (q % 2) * 4:(q % 2) * 4 + 4, :],
                    waits=[(sP[h], lg)], inc=(sSC[h], 16), tag=f"scw{j}.{q}")
            prev_writes_w[h] = (sSC[h], sSC[h].n)
        em("gpsimd", (lambda j=j: (lambda E: E.collective_compute(
            "AllGather", ALU.bypass, replica_groups=[list(range(NCORES))],
            ins=[wri[j].ap().opt()], outs=[wall[j].ap().opt()])))(),
            waits=[(s, s.n) for s in sSC], inc=(sCC, 1), tag=f"AGw{j}")
        ag_w_cc[j] = sCC.n
    retile_done = [(s, s.n) for s in sSC]
    _FFNQ_PROLOGUE = [True]

    fst = {"qv": 0}
    tstseq = [0]

    def ffnq_stripes(mi):
        wsrc = [w1s, w3s, w2s][mi]
        if mi < 2:
            return [(ft, wsrc[ft * 512:(ft + 1) * 512, :]
                     .rearrange("(g p) d -> p g d", p=128), 1024) for ft in range(FT)]
        res = []
        for st in range(4):
            for hh in range(2):
                res.append(((st, hh),
                            wsrc[st * 512 + hh * 256: st * 512 + (hh + 1) * 256, :]
                            .rearrange("(g p) d -> p g d", p=128), 1536))
        return res

    phbuf = [eqv(P_STRIPE, 1536), eqv(A_SLAB, 1536)]
    phdisc = eqv(P_QBF, 1536)
    phsum = eqv(11264, 32)
    phred = {}

    def phA_works():
        k = 0
        for mi in range(3):
            for sid, src, ncols in ffnq_stripes(mi):
                yield (k, mi, src, ncols)
                k += 1

    def phA_emit(work):
        k, mi, src, ncols = work
        b = k % 2
        gdim = ncols // 256 if mi == 2 else 4
        w = []
        if k >= 2:
            w.append((sS, phred[k - 2]))
        if k == 1:
            w += list(retile_done)
        dma("scalar", phbuf[b][:, 0:ncols].rearrange("p (g d) -> p g d", g=gdim),
            src, waits=w, inc=(sST, 16), tag=f"pha{k}")
        g = sST.n
        em("scalar", (lambda ncols=ncols, b=b, k=k: (lambda E: E.activation(
            phdisc[:, 0:ncols], phbuf[b][:, 0:ncols], AF.Abs,
            accum_out=phsum[:, k:k + 1])))(),
            waits=[(sST, g)], inc=(sS, 1), tag=f"phr{k}")
        phred[k] = sS.n
        if k == 29:
            fst["phs"] = sS.n

    def phA_finale():
        em("vector", lambda E: E.memset(partials[:, 3:4], 0.0), tag="pz3")
        for mi, (c0, c1) in enumerate([(0, 11), (11, 22), (22, 30)]):
            em("vector", (lambda mi=mi, c0=c0, c1=c1: (lambda E: E.reduce_sum(
                partials[:, mi:mi + 1], phsum[:, c0:c1], axis=AX.X)))(),
                waits=([(sS, fst["phs"])] if mi == 0 else []),
                inc=((sV, 1) if mi == 2 else None), tag=f"phf{mi}")
        fin_v = sV.n
        dma("gpsimd", ar_in.ap()[:, :], partials[:], waits=[(sV, fin_v)],
            inc=(sAR, 16), tag="arin")
        em("gpsimd", lambda E: E.collective_compute(
            "AllReduce", ALU.add, replica_groups=[list(range(NCORES))],
            ins=[ar_in.ap().opt()], outs=[ar_out.ap().opt()]),
            waits=[(sAR, sAR.n)], inc=(sCC, 1), tag="AR")
        dma("gpsimd", partials[:], ar_out.ap()[:, :], waits=[(sCC, sCC.n)],
            inc=(sAR, 16), tag="arout")
        fst["ar_g"] = sAR.n

    def ffnq_means():
        em("vector", lambda E: E.engine_nop(),
           waits=[(sAR, fst["ar_g"])], inc=(sV, 1), tag="pcast")
        vv = sV.n
        s, ev = acquire_slot()
        em("tensor", (lambda s=s: (lambda E: E.matmul(
            pb[s * 4][:, 0:4], ones_f32[:], partials[:], start=True, stop=True)))(),
            waits=evw(ev) + [(sV, vv)], inc=(sMM, 1), tag="meanMM")
        mmv = sMM.n
        em("vector", (lambda s=s: (lambda E: E.tensor_scalar_mul(
            means[:], pb[s * 4][:, 0:4], 1.0 / NW)))(),
            waits=[(sMM, mmv)], inc=(sEV, 1), tag="means")
        release_slot(s)
        em("vector", lambda E: E.tensor_scalar_max(mcl[:], means[:], EPS_Q), tag="mcl")
        em("vector", lambda E: E.reciprocal(swq[:], mcl[:]), inc=(sV, 1), tag="swq")
        fst["means_v"] = sV.n

    def ffnq_quant_gen():
        for mi in range(3):
            for item in ffnq_quant_stripes(mi):
                yield item
        yield None

    def ffnq_quant_stripes(mi):
        mv = fst["means_v"]
        dst = wq_in[mi]
        for sid, src, ncols in ffnq_stripes(mi):
            yield (mi, sid, src, ncols, dst, mv)

    def ffnq_emit_stripe(work):
        mi, sid, src, ncols, dst, mv = work
        if True:
            gdim = ncols // 256 if mi == 2 else 4
            dma("sync", pstripe[:, 0:ncols].rearrange("p (g d) -> p g d", g=gdim),
                src, waits=[(sV, fst["qv"]), (sS, fst.get("phs", 0))],
                inc=(sST, 16), tag=f"qb{mi}")
            g = sST.n
            em("vector", (lambda ncols=ncols, mi=mi: (lambda E: E.tensor_scalar(
                pstripe[:, 0:ncols], pstripe[:, 0:ncols], swq[:, mi:mi + 1], MAGIC,
                ALU.mult, ALU.add)))(),
                waits=[(sST, g), (sV, mv), (sES, fst.get("qes", 0))],
                tag=f"qr1{mi}")
            em("vector", (lambda ncols=ncols: (lambda E: E.tensor_scalar(
                pstripe[:, 0:ncols], pstripe[:, 0:ncols], -MAGIC, -1.0,
                ALU.add, ALU.max)))(), tag=f"qr2{mi}")
            em("vector", (lambda ncols=ncols: (lambda E: E.tensor_scalar_min(
                qbf[:, 0:ncols], pstripe[:, 0:ncols], 1.0)))(),
                inc=(sV, 1), tag=f"qr3{mi}")
            fst["qv"] = sV.n
            qv = sV.n
            if mi < 2:
                ft = sid
                groups = [(db, [qbf[:, g4 * 256 + db * 128: g4 * 256 + (db + 1) * 128]
                                for g4 in range(4)], 512, dst[db, ft][:, :]) for db in range(2)]
            else:
                st, hh = sid
                groups = [(fb, [qbf[:, g2 * 768 + fb * 128: g2 * 768 + (fb + 1) * 128]
                                for g2 in range(2)], 256,
                           dst[fb, st][:, hh * 256:(hh + 1) * 256]) for fb in range(6)]
            for _, blocks, nbf, dstap in groups:
                s, ev = acquire_slot()
                bank_bf = pb[s * 4][:, 0:512].bitcast(BF16)
                first = True
                for i, blk in enumerate(blocks):
                    w = (evw(ev) + [(sV, qv)]) if first else []
                    em("tensor", (lambda i=i, blk=blk, bank_bf=bank_bf:
                        (lambda E: E.transpose(
                            bank_bf[:, i * 128:(i + 1) * 128], blk, idn[:])))(),
                        waits=w, inc=((sMM, 1) if i == len(blocks) - 1 else None),
                        tag=f"qT{mi}")
                    first = False
                mmv = sMM.n
                ti = tstseq[0] % 3
                tstseq[0] += 1
                em("scalar", (lambda nbf=nbf, bank_bf=bank_bf, ti=ti: (lambda E: E.activation(
                    tst[ti][:, 0:nbf], bank_bf[:, 0:nbf], AF.Copy)))(),
                    waits=[(sMM, mmv), (sQW3[ti], fst.get(f"tw{ti}", 0))],
                    inc=(sES, 1), tag=f"qTe{mi}")
                release_slot(s)
                ev2 = sES.n
                dma("gpsimd", dstap, tst[ti][:, 0:nbf],
                    waits=[(sES, ev2)], inc=(sQW3[ti], 16), tag=f"qw{mi}")
                fst[f"tw{ti}"] = sQW3[ti].n
        fst["qes"] = sES.n
        nst = FT if mi < 2 else 8
        fst[f"nst{mi}"] = fst.get(f"nst{mi}", 0) + 1
        if fst[f"nst{mi}"] == nst:
            fst[f"agq_ready{mi}"] = [(q, q.n) for q in sQW3]


    def ln_stats(r, tiles, discards, pre_s, pre_v, tag):
        for tb in range(4):
            em("vector", (lambda tb=tb: (lambda E: E.reduce_sum(
                stats[:, tb:tb + 1], tiles[tb], axis=AX.X)))(),
                waits=list(pre_v), tag=f"s1{tag}.{tb}")
            em("scalar", (lambda tb=tb: (lambda E: E.activation(
                discards[tb], tiles[tb], AF.Square,
                accum_out=stats[:, 4 + tb:5 + tb])))(),
                waits=list(pre_s), inc=(sS, 1), tag=f"s2{tag}.{tb}")
        sq_s = sS.n
        em("vector", lambda E: E.tensor_scalar_mul(stats[:, 8:12], stats[:, 0:4], 1.0 / H),
           tag=f"mu{tag}")
        em("vector", lambda E: E.tensor_scalar_mul(stats[:, 12:16], stats[:, 4:8], 1.0 / H),
           waits=[(sS, sq_s)], tag=f"e2{tag}")
        em("vector", lambda E: E.tensor_mul(stats[:, 56:60], stats[:, 8:12], stats[:, 8:12]),
           tag=f"msq{tag}")
        em("vector", lambda E: E.tensor_sub(stats[:, 12:16], stats[:, 12:16], stats[:, 56:60]),
           tag=f"var{tag}")
        em("vector", lambda E: E.tensor_scalar_add(stats[:, 12:16], stats[:, 12:16], EPS_LN),
           tag=f"ve{tag}")
        em("vector", lambda E: E.reciprocal(stats[:, 12:16], stats[:, 12:16]),
           inc=(sV, 1), tag=f"rv{tag}")
        rv_v = sV.n
        em("scalar", lambda E: E.activation(stats[:, 16:20], stats[:, 12:16], AF.Sqrt),
           waits=[(sV, rv_v)], inc=(sS, 1), tag=f"sqrt{tag}")
        sq2_s = sS.n
        em("vector", lambda E: E.tensor_scalar_mul(stats[:, 20:24], stats[:, 8:12], -1.0),
           tag=f"nm{tag}")
        em("vector", lambda E: E.tensor_mul(stats[:, 20:24], stats[:, 20:24], stats[:, 16:20]),
           waits=[(sS, sq2_s)], inc=(sV, 1), tag=f"nb{tag}")
        return sS.n, sV.n

    def transpose_batch3(blocks, dst3, wait_list, tag):
        s, ev = acquire_slot()
        bank_bf = pb[s * 4][:, 0:512].bitcast(BF16)
        n = len(blocks)
        first = True
        for i, blk in enumerate(blocks):
            w = (evw(ev) + list(wait_list)) if first else []
            em("tensor", (lambda i=i, blk=blk, bank_bf=bank_bf: (lambda E: E.transpose(
                bank_bf[:, i * 128:(i + 1) * 128], blk, idn[:])))(),
                waits=w, inc=((sMM, 1) if i == n - 1 else None),
                tag=f"T{tag}")
            first = False
        mmv = sMM.n
        bsrc = bank_bf.rearrange("p (i x) -> p i x", x=128)
        em("scalar", (lambda dst3=dst3, bsrc=bsrc, n=n:
            (lambda E: E.activation(dst3, bsrc[:, 0:n, :], AF.Copy)))(),
            waits=[(sMM, mmv)], inc=(sES, 1), tag=f"Te{tag}")
        release_slot(s)

    def transpose_batch(blocks, dst, dst_lo, nbf, wait_list, tag):
        s, ev = acquire_slot()
        bank_bf = pb[s * 4][:, 0:512].bitcast(BF16)
        first = True
        for i, blk in enumerate(blocks):
            w = (evw(ev) + list(wait_list)) if first else []
            em("tensor", (lambda i=i, blk=blk: (lambda E: E.transpose(
                bank_bf[:, i * 128:(i + 1) * 128], blk, idn[:])))(),
                waits=w, inc=((sMM, 1) if i == len(blocks) - 1 else None),
                tag=f"T{tag}")
            first = False
        mmv = sMM.n
        em("scalar", (lambda dst=dst, dst_lo=dst_lo, nbf=nbf, bank_bf=bank_bf:
            (lambda E: E.activation(dst[:, dst_lo:dst_lo + nbf],
                                    bank_bf[:, 0:nbf], AF.Copy)))(),
            waits=[(sMM, mmv)], inc=(sES, 1), tag=f"Te{tag}")
        release_slot(s)

    proj_cfg = [(fV, AF.Tanh), (siV, AF.Sigmoid), (svV, AF.Silu),
                (qV, AF.Copy), (uV, AF.Copy), (sgV, AF.Silu)]
    o2led = [0, 0]
    out_total = [()]

    def emit_round(r):
        x_tiles = [xV[:, tb * 2048:(tb + 1) * 2048] for tb in range(4)]
        pre = [(sX, xload_g[r]), (sEV, ffn2ev[r - 1])]
        ln1_disc = [out1V[:, tb * 2048:(tb + 1) * 2048] for tb in range(4)]
        s_ln1, v_ln1 = ln_stats(r, x_tiles, ln1_disc, pre, pre, f"L1.{r}")
        for tb in range(4):
            em("vector", (lambda tb=tb: (lambda E: E.tensor_scalar(
                yV[:, tb * 2048:(tb + 1) * 2048], x_tiles[tb],
                stats[:, 16 + tb:17 + tb], stats[:, 20 + tb:21 + tb],
                ALU.mult, ALU.add)))(),
                inc=((sV, 1) if tb == 3 else None), tag=f"y{r}.{tb}")
        y_s = sV.n

        for hb2 in range(8):
            blocks = [yV[:, tb * 2048 + (hb2 * 2 + hbl) * 128:
                         tb * 2048 + (hb2 * 2 + hbl) * 128 + 128]
                      for hbl in range(2) for tb in range(4)]
            transpose_batch(blocks, yTV, hb2 * 1024, 1024,
                            [(sV, y_s), (sK, g_consts)], f"y{r}.{hb2}")
        yT_e = sES.n

        proj_ev = {}
        g_v_box = [0]
        if r == 0:
            qgen = ffnq_quant_gen()
            qdone = [False]
            phgen = phA_works()
            phdone = [False]
        for m6 in range(6):
            dst, func = proj_cfg[m6]
            extra_ev = []
            if m6 in (0, 1):
                extra_ev = [(sEV, ffn2ev[r - 1])]
            if m6 in (2, 4):
                extra_ev = list(out_total[0])
            if m6 == 4:
                extra_ev.append((sV, g_v_box[0]))
            if m6 == 5 and r == 0:
                extra_ev = list(retile_done)
            for rbq in range(4):
                s, ev = acquire_slot()
                group_first = True
                for hbq in range(4):
                    si, wv = wchunk_load([(0, wall[rbq][m6, hbq][:, :])],
                                         extra_waits=[(sCC, ag_w_cc[rbq])],
                                         tag=f"pc{r}.{m6}.{rbq}.{hbq}")
                    for hbl in range(4):
                        hb = hbq * 4 + hbl
                        for rbi in range(4):
                            w = []
                            if group_first:
                                w = evw(ev) + [(sES, yT_e)]
                            if hbl == 0 and rbi == 0:
                                w.append((sW2[si], wv))
                            last_of_chunk = (hbl == 3 and rbi == 3)
                            em("tensor", (lambda s=s, rbi=rbi, si=si, hbl=hbl, hb=hb:
                                (lambda E: E.matmul(
                                    pb[s * 4 + rbi][:],
                                    wslot[:, si * 2048 + hbl * 512 + rbi * 128:
                                          si * 2048 + hbl * 512 + (rbi + 1) * 128],
                                    yTV[:, hb * 512:(hb + 1) * 512],
                                    start=(hb == 0), stop=(hb == 15))))(),
                                waits=w, inc=((sMM, 1) if last_of_chunk else None),
                                tag=f"P{r}.{m6}.{rbq}.{hbq}")
                            group_first = False
                    wchunk_done(si)
                mmv = sMM.n
                for rbi in range(4):
                    rb = rbq * 4 + rbi
                    em("scalar", (lambda dst=dst, func=func, rb=rb, s=s, rbi=rbi:
                        (lambda E: E.activation(
                            dst[:, rb * 512:(rb + 1) * 512], pb[s * 4 + rbi][:], func)
                         if func == AF.Copy else E.activation(
                            dst[:, rb * 512:(rb + 1) * 512], pb[s * 4 + rbi][:], func,
                            bias=zcol[:, 0:1])))(),
                        waits=[(sMM, mmv)] + extra_ev, inc=(sES, 1),
                        tag=f"Pe{r}.{m6}.{rbq}.{rbi}")
                release_slot(s)
                if r == 0 and m6 <= 1 and not phdone[0]:
                    for _ in range(4):
                        pw = next(phgen, None)
                        if pw is None:
                            phdone[0] = True
                            break
                        phA_emit(pw)
                if r == 0 and m6 >= 3 and not qdone[0]:
                    for _ in range(3):
                        wk = next(qgen)
                        if wk is None:
                            qdone[0] = True
                            break
                        ffnq_emit_stripe(wk)
            proj_ev[m6] = sES.n
            if m6 == 2:
                em("vector", lambda E: E.tensor_mul(
                    siV[:, 0:8192], siV[:, 0:8192], svV[:, 0:8192]),
                    waits=[(sES, proj_ev[2])], inc=(sV, 1), tag=f"g{r}")
                g_v_box[0] = sV.n
                for rg in range(16):
                    em("vector", (lambda rg=rg: (lambda E: E.tensor_tensor_scan(
                        stV[:, rg * 512:(rg + 1) * 512], fV[:, rg * 512:(rg + 1) * 512],
                        siV[:, rg * 512:(rg + 1) * 512], 0.0, ALU.mult, ALU.add)))(),
                        waits=[(sV, g_v_box[0])], tag=f"scS{r}.{rg}")
                for rg in range(16):
                    em("vector", (lambda rg=rg: (lambda E: E.tensor_tensor_scan(
                        cV[:, rg * 512:(rg + 1) * 512], fV[:, rg * 512:(rg + 1) * 512],
                        fV[:, rg * 512:(rg + 1) * 512], 1.0, ALU.mult, ALU.bypass)))(),
                        tag=f"scC{r}.{rg}")
                cV3 = cV.rearrange("p (rg t) -> p rg t", t=512)
                sV3 = stV.rearrange("p (rg t) -> p rg t", t=512)
                em("vector", lambda E: E.tensor_copy(
                    summ[:, 0:16], cV3[:, :, 511:512].squeeze(2)), tag=f"sP{r}")
                em("vector", lambda E: E.tensor_copy(
                    summ[:, 16:32], sV3[:, :, 511:512].squeeze(2)),
                    inc=(sV, 1), tag=f"sQ{r}")
                summ_v = sV.n
                dma("gpsimd", cin[r].ap()[:, :], summ[:],
                    waits=[(sV, summ_v)], inc=(sCI, 16), tag=f"cin{r}")
                em("gpsimd", (lambda r=r: (lambda E: E.collective_compute(
                    "AllGather", ALU.bypass, replica_groups=[list(range(NCORES))],
                    ins=[cin[r].ap().opt()], outs=[cout[r].ap().opt()])))(),
                    waits=[(sCI, sCI.n)], inc=(sCC, 1), tag=f"AGc{r}")
                dma("gpsimd", s_all[:].rearrange("p (j f) -> p j f", f=32),
                    cout[r].ap().rearrange("(j p) f -> p j f", p=128),
                    waits=[(sCC, sCC.n)], inc=(sSA, 16), tag=f"sall{r}")
                sall_g = sSA.n
            if r == 0 and m6 == 1:
                while not phdone[0]:
                    pw = next(phgen, None)
                    if pw is None:
                        phdone[0] = True
                        break
                    phA_emit(pw)
                phA_finale()
                ffnq_means()

        if r == 0:
            while not qdone[0]:
                wk = next(qgen)
                if wk is None:
                    qdone[0] = True
                    break
                ffnq_emit_stripe(wk)
            for mi in range(3):
                em("gpsimd", (lambda mi=mi: (lambda E: E.collective_compute(
                    "AllGather", ALU.bypass, replica_groups=[list(range(NCORES))],
                    ins=[wq_in[mi].ap().opt()], outs=[wq_ag[mi].ap().opt()])))(),
                    waits=list(fst[f"agq_ready{mi}"]), inc=(sCC, 1), tag=f"AGq{mi}")
                fst[f"ag{mi}_cc"] = sCC.n

        em("vector", lambda E: E.tensor_copy(s_in[:], s_pref[:]),
           waits=[(sSA, sall_g)], tag=f"fi{r}")
        for j in range(8):
            Pj = s_all[:, j * 32:j * 32 + 16]
            Qj = s_all[:, j * 32 + 16:j * 32 + 32]
            em("vector", (lambda Pj=Pj, j=j: (lambda E: E.tensor_scalar(
                t16a[:], Pj, maskm[:, j:j + 1], maskn[:, j:j + 1], ALU.mult, ALU.add)))(),
                tag=f"f1.{j}")
            em("vector", lambda E: E.tensor_mul(s_in[:], s_in[:], t16a[:]), tag=f"f2.{j}")
            em("vector", (lambda Qj=Qj, j=j: (lambda E: E.tensor_scalar_mul(
                t16b[:], Qj, maskm[:, j:j + 1])))(), tag=f"f3.{j}")
            em("vector", lambda E: E.tensor_add(s_in[:], s_in[:], t16b[:]), tag=f"f4.{j}")
        for j in range(8):
            Pj = s_all[:, j * 32:j * 32 + 16]
            Qj = s_all[:, j * 32 + 16:j * 32 + 32]
            em("vector", (lambda Pj=Pj: (lambda E: E.tensor_mul(s_pref[:], s_pref[:], Pj)))(),
               tag=f"f5.{j}")
            em("vector", (lambda Qj=Qj: (lambda E: E.tensor_add(s_pref[:], s_pref[:], Qj)))(),
               tag=f"f6.{j}")
        for rg in range(16):
            em("vector", (lambda rg=rg: (lambda E: E.scalar_tensor_tensor(
                cV[:, rg * 512:(rg + 1) * 512], cV[:, rg * 512:(rg + 1) * 512],
                s_in[:, rg:rg + 1], stV[:, rg * 512:(rg + 1) * 512],
                ALU.mult, ALU.add)))(), tag=f"co{r}.{rg}")
        em("vector", lambda E: E.tensor_mul(stV[:, 0:8192], qV[:, 0:8192], cV[:, 0:8192]),
           waits=[(sES, proj_ev[3])], inc=(sV, 1), tag=f"tmp{r}")
        tmp_v = sV.n
        em("scalar", lambda E: E.activation(qV[:, 0:8192], stV[:, 0:8192], AF.Silu,
                                            bias=zcol[:, 0:1]),
           waits=[(sV, tmp_v)], inc=(sS, 1), tag=f"ro{r}")
        ro_s = sS.n
        em("vector", lambda E: E.tensor_mul(uV[:, 0:8192], uV[:, 0:8192], sgV[:, 0:8192]),
           waits=[(sES, proj_ev[5])], inc=(sV, 1), tag=f"z{r}")
        z_v = sV.n

        em("vector", lambda E: E.memset(stats[:, 0:8], 0.0), tag=f"l2z{r}")
        sqv_led = {}
        for ht in range(4):
            s, ev = acquire_slot()
            group_first = True
            for mseq, m2 in enumerate((1, 0)):
                stat_src = qV if m2 == 0 else zV
                for kq in range(4):
                    si, wv = wchunk_load([(0, wall[ht][6 + m2, kq][:, :])],
                                         extra_waits=[(sCC, ag_w_cc[ht])],
                                         tag=f"mc{r}.{ht}.{m2}.{kq}")
                    for kbl in range(4):
                        kb = kq * 4 + kbl
                        for tbi in range(4):
                            w = []
                            if group_first:
                                w = evw(ev) + [(sV, z_v)]
                            if mseq == 1 and kq == 0 and kbl == 0 and tbi == 0:
                                w.append((sS, ro_s))
                            if kbl == 0 and tbi == 0:
                                w.append((sW2[si], wv))
                            last_of_chunk = (kbl == 3 and tbi == 3)
                            em("tensor", (lambda s=s, tbi=tbi, si=si, kbl=kbl, kb=kb,
                                          stat_src=stat_src, mseq=mseq: (lambda E: E.matmul(
                                pb[s * 4 + tbi][:],
                                stat_src[:, kb * 512 + tbi * 128: kb * 512 + (tbi + 1) * 128],
                                wslot[:, si * 2048 + kbl * 512: si * 2048 + (kbl + 1) * 512],
                                start=(mseq == 0 and kb == 0), stop=(mseq == 1 and kb == 15))))(),
                                waits=w, inc=((sMM, 1) if last_of_chunk else None),
                                tag=f"M{r}.{ht}.{m2}.{kq}")
                            group_first = False
                    wchunk_done(si)
            mmv = sMM.n
            for tbi in range(4):
                em("vector", (lambda s=s, tbi=tbi, ht=ht: (lambda E: E.tensor_tensor(
                    out1V[:, tbi * 2048 + ht * 512: tbi * 2048 + (ht + 1) * 512],
                    pb[s * 4 + tbi][:],
                    xV[:, tbi * 2048 + ht * 512: tbi * 2048 + (ht + 1) * 512],
                    ALU.add)))(),
                    waits=[(sMM, mmv)], inc=(sEV, 1), tag=f"Me{r}.{ht}.{tbi}")
            evht = sEV.n
            for tbi in range(4):
                blk = out1V[:, tbi * 2048 + ht * 512: tbi * 2048 + (ht + 1) * 512]
                em("vector", (lambda blk=blk: (lambda E: E.reduce_sum(
                    stats[:, 59:60], blk, axis=AX.X)))(), tag=f"l2a{r}.{ht}.{tbi}")
                em("vector", (lambda tbi=tbi: (lambda E: E.tensor_add(
                    stats[:, tbi:tbi + 1], stats[:, tbi:tbi + 1],
                    stats[:, 59:60])))(), tag=f"l2b{r}.{ht}.{tbi}")
                sw = [(sEV, evht)]
                if (ht, tbi) in sqv_led:
                    pass
                if ht > 0:
                    sw.append((sV, sqv_led[(ht - 1, tbi)]))
                em("scalar", (lambda blk=blk, tbi=tbi: (lambda E: E.activation(
                    xV[:, tbi * 2048:tbi * 2048 + 512], blk, AF.Square,
                    accum_out=stats[:, 60 + tbi:61 + tbi])))(),
                    waits=sw, inc=(sS, 1), tag=f"l2s{r}.{ht}.{tbi}")
                sqn = sS.n
                em("vector", (lambda tbi=tbi: (lambda E: E.tensor_add(
                    stats[:, 4 + tbi:5 + tbi], stats[:, 4 + tbi:5 + tbi],
                    stats[:, 60 + tbi:61 + tbi])))(),
                    waits=[(sS, sqn)], inc=(sV, 1), tag=f"l2c{r}.{ht}.{tbi}")
                sqv_led[(ht, tbi)] = sV.n
            release_slot(s)
        mix_ev = sEV.n

        o1_tiles = [out1V[:, tb * 2048:(tb + 1) * 2048] for tb in range(4)]
        em("vector", lambda E: E.tensor_scalar_mul(stats[:, 8:12], stats[:, 0:4],
                                                   1.0 / H), tag=f"muL2{r}")
        em("vector", lambda E: E.tensor_scalar_mul(stats[:, 12:16], stats[:, 4:8],
                                                   1.0 / H), tag=f"e2L2{r}")
        em("vector", lambda E: E.tensor_mul(stats[:, 56:60], stats[:, 8:12],
                                            stats[:, 8:12]), tag=f"msqL2{r}")
        em("vector", lambda E: E.tensor_sub(stats[:, 12:16], stats[:, 12:16],
                                            stats[:, 56:60]), tag=f"varL2{r}")
        em("vector", lambda E: E.tensor_scalar_add(stats[:, 12:16], stats[:, 12:16],
                                                   EPS_LN), tag=f"veL2{r}")
        em("vector", lambda E: E.reciprocal(stats[:, 12:16], stats[:, 12:16]),
           inc=(sV, 1), tag=f"rvL2{r}")
        rv2_v = sV.n
        em("scalar", lambda E: E.activation(stats[:, 16:20], stats[:, 12:16], AF.Sqrt),
           waits=[(sV, rv2_v)], inc=(sS, 1), tag=f"sqrtL2{r}")
        s_ln2 = sS.n
        em("vector", lambda E: E.tensor_scalar_mul(stats[:, 20:24], stats[:, 8:12],
                                                   -1.0), tag=f"nmL2{r}")
        em("vector", lambda E: E.tensor_mul(stats[:, 20:24], stats[:, 20:24],
                                            stats[:, 16:20]),
           waits=[(sS, s_ln2)], inc=(sV, 1), tag=f"nbL2{r}")
        rq2_tb = [0, 0, 0, 0]
        for tb in range(4):
            em("vector", (lambda tb=tb: (lambda E: E.tensor_scalar(
                y2stage, o1_tiles[tb],
                stats[:, 16 + tb:17 + tb], stats[:, 20 + tb:21 + tb],
                ALU.mult, ALU.add)))(), tag=f"y2.{r}.{tb}")
            em("vector", (lambda tb=tb: (lambda E: E.reduce_max(
                stats[:, 24 + tb:25 + tb], y2stage, axis=AX.X,
                apply_absolute_value=True)))(), tag=f"qm{r}.{tb}")
            em("vector", (lambda tb=tb: (lambda E: E.tensor_scalar_max(
                stats[:, 24 + tb:25 + tb], stats[:, 24 + tb:25 + tb], EPS_Q)))(),
                tag=f"qc{r}.{tb}")
            em("vector", (lambda tb=tb: (lambda E: E.tensor_scalar_mul(
                stats[:, 32 + tb:33 + tb], stats[:, 24 + tb:25 + tb], 1.0 / 127.0)))(),
                tag=f"rs{r}.{tb}")
            em("vector", (lambda tb=tb: (lambda E: E.reciprocal(
                stats[:, 28 + tb:29 + tb], stats[:, 24 + tb:25 + tb])))(),
                tag=f"r1{r}.{tb}")
            em("vector", (lambda tb=tb: (lambda E: E.tensor_scalar_mul(
                stats[:, 28 + tb:29 + tb], stats[:, 28 + tb:29 + tb], 127.0)))(),
                tag=f"r2{r}.{tb}")
            em("vector", (lambda tb=tb: (lambda E: E.tensor_scalar(
                ln2scr, y2stage, stats[:, 28 + tb:29 + tb], MAGIC, ALU.mult, ALU.add)))(),
                tag=f"rq1{r}.{tb}")
            em("vector", (lambda tb=tb: (lambda E: E.tensor_scalar_add(
                y2qV[:, tb * 2048:(tb + 1) * 2048], ln2scr, -MAGIC)))(),
                inc=(sV, 1), tag=f"rq2{r}.{tb}")
            rq2_tb[tb] = sV.n
        y2q_v = sV.n
        em("vector", lambda E: E.tensor_scalar(
            stats[:, 44:48], stats[:, 32:36], mcl[:, 0:1], None, ALU.mult),
            waits=[(sV, fst.get("means_v", 0))], tag=f"sc1{r}")
        em("vector", lambda E: E.tensor_scalar(
            stats[:, 48:52], stats[:, 32:36], mcl[:, 1:2], None, ALU.mult),
            inc=(sV, 1), tag=f"sc3{r}")
        sc_v = sV.n

        y2qT3 = y2qTV.rearrange("p (dg x) -> p dg x", x=512)
        for tb in range(4):
            for grp in range(2):
                blocks = [y2qV[:, tb * 2048 + (grp * 8 + i) * 128:
                              tb * 2048 + (grp * 8 + i) * 128 + 128] for i in range(8)]
                transpose_batch3(blocks,
                                 y2qT3[:, grp * 8:(grp + 1) * 8,
                                       tb * 128:(tb + 1) * 128],
                                 [(sV, rq2_tb[tb])], f"q{r}.{tb}.{grp}")
        y2qT_e = sES.n

        for mi, (dstV, func, sccol) in enumerate(
                [(aV, AF.Silu, 44), (bV, AF.Copy, 48)]):
            agw = [(sCC, fst.get("ag2_cc", 0))]
            for ft in range(FT):
                s, ev = acquire_slot()
                group_first = True
                for q in range(4):
                    srcs = [(i * 512, wq_ag[mi][4 * q + i, ft][:, :]) for i in range(4)]
                    si, wv = wchunk_load(srcs, extra_waits=agw, tag=f"fc{r}.{mi}.{ft}.{q}")
                    for dgl in range(4):
                        dgb = q * 4 + dgl
                        for tbi in range(4):
                            w = []
                            if group_first:
                                w = evw(ev) + [(sES, y2qT_e), (sV, sc_v)]
                            if dgl == 0 and tbi == 0:
                                w.append((sW2[si], wv))
                            last_of_chunk = (dgl == 3 and tbi == 3)
                            em("tensor", (lambda s=s, tbi=tbi, si=si, dgl=dgl, dgb=dgb:
                                (lambda E: E.matmul(
                                    pb[s * 4 + tbi][:],
                                    y2qTV[:, dgb * 512 + tbi * 128: dgb * 512 + (tbi + 1) * 128],
                                    wslot[:, si * 2048 + dgl * 512: si * 2048 + (dgl + 1) * 512],
                                    start=(dgb == 0), stop=(dgb == 15))))(),
                                waits=w, inc=((sMM, 1) if last_of_chunk else None),
                                tag=f"F1.{r}.{mi}.{ft}.{q}")
                            group_first = False
                    wchunk_done(si)
                mmv = sMM.n
                for tbi in range(4):
                    em("scalar", (lambda dstV=dstV, func=func, tbi=tbi, ft=ft, s=s,
                                  sccol=sccol: (lambda E: E.activation(
                        dstV[:, tbi * FH + ft * 512: tbi * FH + ft * 512 + 512],
                        pb[s * 4 + tbi][:], func,
                        scale=stats[:, sccol + tbi:sccol + tbi + 1])
                        if func == AF.Copy else E.activation(
                        dstV[:, tbi * FH + ft * 512: tbi * FH + ft * 512 + 512],
                        pb[s * 4 + tbi][:], func,
                        bias=zcol[:, 0:1], scale=stats[:, sccol + tbi:sccol + tbi + 1])))(),
                        waits=[(sMM, mmv)], inc=(sES, 1), tag=f"F1e{r}.{mi}.{ft}.{tbi}")
                release_slot(s)
                if mi == 1:
                    if ft == 0:
                        em("vector", lambda E: E.memset(stats[:, 24:28], 0.0),
                           tag=f"hz{r}")
                    ev_g = sES.n
                    for tbi in range(4):
                        lo = tbi * FH + ft * 512
                        em("vector", (lambda lo=lo: (lambda E: E.tensor_mul(
                            aV[:, lo:lo + 512], aV[:, lo:lo + 512],
                            bV[:, lo:lo + 512])))(),
                            waits=([(sES, ev_g)] if tbi == 0 else []),
                            tag=f"h{r}.{ft}.{tbi}")
                        em("vector", (lambda lo=lo: (lambda E: E.reduce_max(
                            stats[:, 60:61], aV[:, lo:lo + 512], axis=AX.X,
                            apply_absolute_value=True)))(), tag=f"hm{r}.{ft}.{tbi}")
                        em("vector", (lambda tbi=tbi: (lambda E: E.tensor_max(
                            stats[:, 24 + tbi:25 + tbi], stats[:, 24 + tbi:25 + tbi],
                            stats[:, 60:61])))(), tag=f"hx{r}.{ft}.{tbi}")
        ffn1_ev = sES.n

        hq_tb = [0, 0, 0, 0]
        st_v = [0, 0, 0, 0]
        for tb in range(4):
            em("vector", (lambda tb=tb: (lambda E: E.tensor_scalar_max(
                stats[:, 24 + tb:25 + tb], stats[:, 24 + tb:25 + tb], EPS_Q)))(),
                tag=f"hc{r}.{tb}")
            em("vector", (lambda tb=tb: (lambda E: E.tensor_scalar_mul(
                stats[:, 40 + tb:41 + tb], stats[:, 24 + tb:25 + tb], 1.0 / 127.0)))(),
                tag=f"hr{r}.{tb}")
            em("vector", (lambda tb=tb: (lambda E: E.reciprocal(
                stats[:, 36 + tb:37 + tb], stats[:, 24 + tb:25 + tb])))(),
                tag=f"h1{r}.{tb}")
            em("vector", (lambda tb=tb: (lambda E: E.tensor_scalar_mul(
                stats[:, 36 + tb:37 + tb], stats[:, 36 + tb:37 + tb], 127.0)))(),
                inc=(sV, 1), tag=f"h2{r}.{tb}")
            st_v[tb] = sV.n
        hqst2 = [hqstage, eqv(14080, 2816)]
        hq1_led = {}
        hq2_led = {}
        for tb in range(4):
            for hh in range(2):
                k = tb * 2 + hh
                w = [(sV, st_v[tb])]
                if k >= 2:
                    w.append((sV, hq2_led[k - 2]))
                em("scalar", (lambda tb=tb, hh=hh, k=k: (lambda E: E.activation(
                    hqst2[k % 2],
                    aV[:, tb * FH + hh * 2816: tb * FH + (hh + 1) * 2816],
                    AF.Copy, bias=MAGIC,
                    scale=stats[:, 36 + tb:37 + tb])))(),
                    waits=w, inc=(sES, 1), tag=f"hq1{r}.{tb}.{hh}")
                hq1_led[k] = sES.n
                em("vector", (lambda tb=tb, hh=hh, k=k: (lambda E: E.tensor_scalar_add(
                    bV[:, tb * FH + hh * 2816: tb * FH + (hh + 1) * 2816],
                    hqst2[k % 2], -MAGIC)))(),
                    waits=[(sES, hq1_led[k])], inc=(sV, 1), tag=f"hq2{r}.{tb}.{hh}")
                hq2_led[k] = sV.n
            hq_tb[tb] = hq2_led[tb * 2 + 1]
        em("vector", lambda E: E.tensor_scalar(
            stats[:, 52:56], stats[:, 40:44], mcl[:, 2:3], None, ALU.mult),
            inc=(sV, 1), tag=f"sc2{r}")
        hq_v = sV.n

        hqT3 = aV.rearrange("p (fb x) -> p fb x", x=512)
        for tb in range(4):
            for grp in range(6):
                fb0 = grp * 8
                nb = min(8, FB - fb0)
                blocks = [bV[:, tb * FH + (fb0 + i) * 128:
                             tb * FH + (fb0 + i) * 128 + 128] for i in range(nb)]
                transpose_batch3(blocks,
                                 hqT3[:, fb0:fb0 + nb, tb * 128:(tb + 1) * 128],
                                 [(sV, hq_tb[tb])], f"hq{r}.{tb}.{grp}")
        hqT_e = sES.n

        agw2 = [(sCC, fst.get("ag2_cc", 0))]
        for ht in range(4):
            s, ev = acquire_slot()
            group_first = True
            for q in range(11):
                srcs = [(i * 512, wq_ag[2][4 * q + i, ht][:, :]) for i in range(4)]
                si, wv = wchunk_load(srcs, extra_waits=agw2, tag=f"gc{r}.{ht}.{q}")
                for fbl in range(4):
                    fb = q * 4 + fbl
                    for tbi in range(4):
                        w = []
                        if group_first:
                            w = evw(ev) + [(sES, hqT_e)]
                        if fbl == 0 and tbi == 0:
                            w.append((sW2[si], wv))
                        last_of_chunk = (fbl == 3 and tbi == 3)
                        em("tensor", (lambda s=s, tbi=tbi, si=si, fbl=fbl, fb=fb:
                            (lambda E: E.matmul(
                                pb[s * 4 + tbi][:],
                                aV[:, fb * 512 + tbi * 128: fb * 512 + (tbi + 1) * 128],
                                wslot[:, si * 2048 + fbl * 512: si * 2048 + (fbl + 1) * 512],
                                start=(fb == 0), stop=(fb == FB - 1))))(),
                            waits=w, inc=((sMM, 1) if last_of_chunk else None),
                            tag=f"F2.{r}.{ht}.{q}")
                        group_first = False
                wchunk_done(si)
            mmv = sMM.n
            for tbi in range(4):
                oi = (ht * 4 + tbi) % 2
                em("vector", (lambda s=s, tbi=tbi, ht=ht, oi=oi: (lambda E: E.scalar_tensor_tensor(
                    o2st[oi], pb[s * 4 + tbi][:], stats[:, 52 + tbi:53 + tbi],
                    out1V[:, tbi * 2048 + ht * 512: tbi * 2048 + (ht + 1) * 512],
                    ALU.mult, ALU.add)))(),
                    waits=[(sMM, mmv), (sO2[oi], o2led[oi])], inc=(sEV, 1),
                    tag=f"F2e{r}.{ht}.{tbi}")
                evn = sEV.n
                dma("gpsimd",
                    out[r * TS + tbi * 128: r * TS + (tbi + 1) * 128,
                        ht * 512:(ht + 1) * 512],
                    o2st[oi], waits=[(sEV, evn)], inc=(sO2[oi], 16), tag=f"o{r}.{ht}.{tbi}")
                o2led[oi] = sO2[oi].n
            release_slot(s)
        ffn2ev[r] = sEV.n
        xfree_mm[r + 1] = sMM.n
        out_total[0] = ((sO2[0], sO2[0].n), (sO2[1], sO2[1].n))

    for r in range(ROUNDS):
        emit_round(r)
        if r + 1 < ROUNDS:
            load_x(r + 1, eng="scalar")

    em("gpsimd", None, waits=[(sO2[0], sO2[0].n), (sO2[1], sO2[1].n)], tag="fin")

    P.check()

    for s in sems:
        s.h = nc.alloc_semaphore(s.name)
    with nc.Block() as block:
        P.emit(block)
    return nc


_nc_cache = {}


def _get_nc():
    if "nc" not in _nc_cache:
        _nc_cache["nc"] = build_nc()
    return _nc_cache["nc"]


def kernel(L0_attn_x, L0_attn_w_f, L0_attn_w_i, L0_attn_w_v, L0_attn_w_q,
           L0_attn_w_ro, L0_attn_w_aux, L0_attn_w_up, L0_attn_w_gate,
           L0_attn_w_down, L0_attn_init_state, L0_ffn_w1, L0_ffn_w2, L0_ffn_w3):
    x = np.ascontiguousarray(np.asarray(L0_attn_x, dtype=np.float32).reshape(16384, 2048))
    wlist = [np.ascontiguousarray(np.asarray(a, dtype=np.float32)) for a in [
        L0_attn_w_f, L0_attn_w_i, L0_attn_w_v, L0_attn_w_q, L0_attn_w_up,
        L0_attn_w_gate, L0_attn_w_ro, L0_attn_w_down]]
    w1 = np.asarray(L0_ffn_w1, dtype=np.float32)
    w2 = np.asarray(L0_ffn_w2, dtype=np.float32)
    w3 = np.asarray(L0_ffn_w3, dtype=np.float32)
    init = np.asarray(L0_attn_init_state, dtype=np.float32)
    init_s = np.ascontiguousarray(init.reshape(16, 128).T)
    idn = np.eye(128, dtype=ml_dtypes.bfloat16)

    nc = _get_nc()
    in_maps = []
    for c in range(NCORES):
        xin = np.ascontiguousarray(np.concatenate(
            [x[(8 * r + c) * TS:(8 * r + c + 1) * TS] for r in range(ROUNDS)], axis=0))
        mask_m = np.zeros((128, 8), dtype=np.float32)
        mask_m[:, :c] = 1.0
        mask_n = np.ascontiguousarray(1.0 - mask_m)
        w1sh = np.ascontiguousarray(w1[:, c * 256:(c + 1) * 256])
        w3sh = np.ascontiguousarray(w3[:, c * 256:(c + 1) * 256])
        w2sh = np.zeros((2048, 768), dtype=np.float32)
        lo = c * 768
        hi = min(FH, lo + 768)
        if lo < FH:
            w2sh[:, :hi - lo] = w2[:, lo:hi]
        im = dict(xin=xin, init_s=init_s, mask_m=mask_m, mask_n=mask_n,
                  idn=idn, w1s=w1sh, w3s=w3sh, w2s=w2sh, wmine=wlist[c])
        in_maps.append(im)

    res = run_bass_kernel_spmd(nc, in_maps, core_ids=list(range(NCORES)))

    full = np.empty((16384, 2048), dtype=np.float32)
    for c in range(NCORES):
        oc = np.asarray(res.results[c]["out"])
        for r in range(ROUNDS):
            full[(8 * r + c) * TS:(8 * r + c + 1) * TS] = oc[r * TS:(r + 1) * TS]
    return full.reshape(1, 16384, 2048)


# version 19
# speedup vs baseline: 1.0368x; 1.0068x over previous
import numpy as np
import ml_dtypes
import concourse.bass as bass
import concourse.mybir as mybir
from concourse.bass_utils import run_bass_kernel_spmd

F32 = mybir.dt.float32
BF16 = mybir.dt.bfloat16
AF = mybir.ActivationFunctionType
ALU = mybir.AluOpType
AX = mybir.AxisListType

NCORES = 8
ROUNDS = 4
TS = 512
H = 2048
FH = 5632
FT = 11
FB = 44
HT = 4
MAGIC = float(np.float32(1.5 * 2 ** 23))
EPS_LN = 1e-5
EPS_Q = 1e-5
NW = float(FH * H)

A_X = 0
A_X2 = 8192
A_YT = 16384
A_F = 20480
A_SI = 24576
A_ST = 28672
A_C = 32768
A_Q = 36864
A_SV = 40960
A_END = 45056
P_STRIPE = 8192
P_QBF = 9728
P_TST = 10496
A_SLAB = 12288
A_SG = 12288


class Sem:
    def __init__(self, name):
        self.name = name
        self.n = 0
        self.h = None


class Plan:
    ENGS = ("sync", "gpsimd", "tensor", "vector", "scalar")

    def __init__(self):
        self.items = {e: [] for e in self.ENGS}
        self.lastwait = {e: {} for e in self.ENGS}

    def em(self, eng, fn, waits=(), inc=None, tag=""):
        w = []
        for sem, val in waits:
            if val <= 0:
                continue
            if self.lastwait[eng].get(sem.name, -1) >= val:
                continue
            self.lastwait[eng][sem.name] = val
            w.append((sem, val))
        self.items[eng].append((tuple(w), fn, inc, tag))
        if inc is not None:
            inc[0].n += inc[1]

    def check(self):
        pos = {e: 0 for e in self.ENGS}
        cnt = {}
        progress = True
        while progress:
            progress = False
            for e in self.ENGS:
                while pos[e] < len(self.items[e]):
                    waits, _, inc, _ = self.items[e][pos[e]]
                    if all(cnt.get(s.name, 0) >= v for s, v in waits):
                        if inc is not None:
                            s, k = inc
                            cnt[s.name] = cnt.get(s.name, 0) + k
                        pos[e] += 1
                        progress = True
                    else:
                        break
        stuck = {e: pos[e] for e in self.ENGS if pos[e] < len(self.items[e])}
        if stuck:
            msg = []
            for e, p in stuck.items():
                waits, _, _, tag = self.items[e][p]
                wd = [(s.name, v, cnt.get(s.name, 0)) for s, v in waits]
                msg.append(f"{e}@{p}/{len(self.items[e])} tag={tag} waits={wd}")
            raise RuntimeError("plan deadlock: " + " | ".join(msg))

    def emit(self, block):
        def mk(eng_name):
            drainy = eng_name in ("vector", "scalar")
            def body(eng):
                for waits, fn, inc, _ in self.items[eng_name]:
                    for s, v in waits:
                        eng.wait_ge(s.h, v)
                    ins = fn(eng) if fn is not None else None
                    if inc is not None:
                        s, k = inc
                        if ins is not None:
                            ins.then_inc(s.h, k)
                        else:
                            eng.sem_inc(s.h, k)
                    if drainy and ins is not None:
                        eng.drain()
            return body
        block.sync(mk("sync"))
        block.gpsimd(mk("gpsimd"))
        block.tensor(mk("tensor"))
        block.vector(mk("vector"))
        block.scalar(mk("scalar"))


def build_nc():
    nc = bass.Bass("TRN2", target_bir_lowering=False)

    xin = nc.dram_tensor("xin", [ROUNDS * TS, H], F32, kind="ExternalInput").ap()
    wmine = nc.dram_tensor("wmine", [H, H], F32, kind="ExternalInput").ap()
    w1s = nc.dram_tensor("w1s", [FH, 256], F32, kind="ExternalInput").ap()
    w3s = nc.dram_tensor("w3s", [FH, 256], F32, kind="ExternalInput").ap()
    w2s = nc.dram_tensor("w2s", [H, 768], F32, kind="ExternalInput").ap()
    init_in = nc.dram_tensor("init_s", [128, 16], F32, kind="ExternalInput").ap()
    mm_in = nc.dram_tensor("mask_m", [128, 8], F32, kind="ExternalInput").ap()
    mn_in = nc.dram_tensor("mask_n", [128, 8], F32, kind="ExternalInput").ap()
    idn_in = nc.dram_tensor("idn", [128, 128], BF16, kind="ExternalInput").ap()
    out = nc.dram_tensor("out", [ROUNDS * TS, H], F32, kind="ExternalOutput").ap()

    wri = [nc.dram_tensor(f"wri{j}", [4, 128, 2048], BF16) for j in range(4)]
    wall = [nc.dram_tensor(f"wall{j}", [8, 4, 128, 2048], BF16, addr_space="Shared") for j in range(4)]
    wq_in = [nc.dram_tensor("w1q_in", [2, FT, 128, 512], BF16),
             nc.dram_tensor("w3q_in", [2, FT, 128, 512], BF16),
             nc.dram_tensor("w2q_in", [6, HT, 128, 512], BF16)]
    wq_ag = [nc.dram_tensor("w1q_ag", [16, FT, 128, 512], BF16, addr_space="Shared"),
             nc.dram_tensor("w3q_ag", [16, FT, 128, 512], BF16, addr_space="Shared"),
             nc.dram_tensor("w2q_ag", [48, HT, 128, 512], BF16, addr_space="Shared")]
    ar_in = nc.dram_tensor("ar_in", [128, 4], F32)
    ar_out = nc.dram_tensor("ar_out", [128, 4], F32, addr_space="Shared")
    cin = [nc.dram_tensor(f"cin{r}", [128, 32], F32) for r in range(ROUNDS)]
    cout = [nc.dram_tensor(f"cout{r}", [1024, 32], F32, addr_space="Shared") for r in range(ROUNDS)]

    arena = nc.alloc_sbuf_tensor("arena", [128, A_END], F32)
    wslot = nc.alloc_sbuf_tensor("wslot", [128, 6144], BF16)
    stats = nc.alloc_sbuf_tensor("stats", [128, 64], F32)
    summ = nc.alloc_sbuf_tensor("summ", [128, 32], F32)
    s_all = nc.alloc_sbuf_tensor("s_all", [128, 256], F32)
    s_pref = nc.alloc_sbuf_tensor("s_pref", [128, 16], F32)
    s_in = nc.alloc_sbuf_tensor("s_in", [128, 16], F32)
    t16a = nc.alloc_sbuf_tensor("t16a", [128, 16], F32)
    t16b = nc.alloc_sbuf_tensor("t16b", [128, 16], F32)
    maskm = nc.alloc_sbuf_tensor("maskm", [128, 8], F32)
    maskn = nc.alloc_sbuf_tensor("maskn", [128, 8], F32)
    initb = nc.alloc_sbuf_tensor("initb", [128, 16], F32)
    partials = nc.alloc_sbuf_tensor("partials", [128, 4], F32)
    ones_f32 = nc.alloc_sbuf_tensor("ones_f32", [128, 128], F32)
    means = nc.alloc_sbuf_tensor("means", [128, 4], F32)
    mcl = nc.alloc_sbuf_tensor("mcl", [128, 4], F32)
    swq = nc.alloc_sbuf_tensor("swq", [128, 4], F32)
    zcol = nc.alloc_sbuf_tensor("zcol", [128, 1], F32)
    ones_bf = nc.alloc_sbuf_tensor("ones_bf", [128, 128], BF16)
    idn = nc.alloc_sbuf_tensor("idn_s", [128, 128], BF16)

    pb = [nc.alloc_psum_tensor(f"pb{i}", [128, 512], F32) for i in range(8)]

    sMM = Sem("sMM")
    sEV = Sem("sEV")
    sES = Sem("sES")
    sV = Sem("sV")
    sS = Sem("sS")
    sCC = Sem("sCC")
    sK = Sem("sK")
    sX = Sem("sX")
    sP = [Sem("sP0"), Sem("sP1")]
    sSC = [Sem(f"sSC{m}") for m in range(2)]
    sQW3 = [Sem(f"sQW{i}") for i in range(3)]
    sW2 = [Sem("sW0"), Sem("sW1"), Sem("sW2s")]
    sST = Sem("sST")
    sGQ = Sem("sGQ")
    sAR = Sem("sAR")
    sSA = Sem("sSA")
    sCI = Sem("sCI")
    sO2 = [Sem("sO0"), Sem("sO1")]
    sems = [sMM, sEV, sES, sV, sS, sCC, sK, sX] + sP + sSC + sW2 + sQW3 + [
        sST, sGQ, sAR, sSA, sCI] + sO2

    def eqv(off, n):
        return arena[:, off:off + n]

    def bfv(off, neq):
        return arena[:, off:off + neq].bitcast(BF16)

    xV = eqv(A_X, 8192)
    y2stage = eqv(A_X, 2048)
    ln2scr = eqv(A_X + 2048, 2048)
    aV = bfv(A_X, 11264)
    hqstage = eqv(11264, 2816)
    sgV = bfv(A_SG, 4096)
    yTV = bfv(A_YT, 4096)
    y2qTV = bfv(A_YT, 4096)
    fV = bfv(A_F, 4096)
    ln1scr = eqv(A_F, 2048)
    out1V = eqv(A_F, 8192)
    yV = bfv(A_SI, 4096)
    siV = bfv(A_SI, 4096)
    stV = bfv(A_ST, 4096)
    cV = bfv(A_C, 4096)
    qV = bfv(A_Q, 4096)
    y2qV = bfv(A_Q, 4096)
    bV = bfv(A_ST, 11264)
    svV = bfv(A_SV, 4096)
    uV = bfv(A_SV, 4096)
    zV = bfv(A_SV, 4096)
    o2st = [eqv(A_SV + i * 512, 512) for i in range(2)]
    pstripe = eqv(P_STRIPE, 1536)
    qbf = bfv(P_QBF, 768)
    tst = [bfv(P_TST + i * 256, 256) for i in range(3)]
    slabv = bfv(A_SLAB, 2048)

    P = Plan()
    em = P.em

    def dma(eng, o, i, waits=(), inc=None, tag="", **kw):
        em(eng, (lambda o=o, i=i, kw=kw: (lambda E: E.dma_start(out=o, in_=i, **kw)))(),
           waits=waits, inc=inc, tag=tag)

    slot_ev = [(0, 0), (0, 0)]
    slot_ctr = [0]

    def acquire_slot():
        s = slot_ctr[0] % 2
        slot_ctr[0] += 1
        return s, slot_ev[s]

    def release_slot(s):
        slot_ev[s] = (sES.n, sEV.n)

    def evw(ev):
        return [(sES, ev[0]), (sEV, ev[1])]

    wfree = [0, 0, 0]
    wseq = [0]

    def wchunk_load(srcs, extra_waits=(), tag=""):
        si = wseq[0] % 3
        wseq[0] += 1
        first = True
        n = len(srcs)
        for idx, (lo, src) in enumerate(srcs):
            w = list(extra_waits) + ([(sMM, wfree[si])] if first else [])
            width = 2048 if n == 1 else 512
            dma("sync", wslot[:, si * 2048 + lo: si * 2048 + lo + width],
                src, waits=w, inc=(sW2[si], 16), tag=tag)
            first = False
        return si, sW2[si].n

    def wchunk_done(si):
        wfree[si] = sMM.n

    em("vector", lambda E: E.memset(zcol[:], 0.0), tag="z0")
    em("vector", lambda E: E.memset(ones_bf[:], 1.0), tag="ones")
    em("vector", lambda E: E.memset(ones_f32[:], 1.0), tag="ones32")
    dma("gpsimd", idn[:], idn_in[:, :], inc=(sK, 16), tag="idn")
    dma("gpsimd", maskm[:], mm_in[:, :], inc=(sK, 16), tag="mm")
    dma("gpsimd", maskn[:], mn_in[:, :], inc=(sK, 16), tag="mn")
    dma("gpsimd", initb[:], init_in[:, :], inc=(sK, 16), tag="init")
    g_consts = sK.n
    em("vector", lambda E: E.tensor_copy(s_pref[:], initb[:]),
       waits=[(sK, g_consts)], inc=(sV, 1), tag="spref0")

    ffn2ev = {-1: 0}
    xfree_mm = [0] * (ROUNDS + 1)
    xload_g = [0] * ROUNDS

    def load_x(r, eng="gpsimd"):
        dma(eng, xV.rearrange("p (t d) -> p t d", t=4),
            xin[r * TS:(r + 1) * TS, :].rearrange("(t p) d -> p t d", p=128),
            waits=[(sMM, xfree_mm[r])], inc=(sX, 16), tag=f"x{r}")
        xload_g[r] = sX.n

    load_x(0, eng="scalar")

    slabh = [bfv(A_SLAB, 2048).rearrange("p (t d) -> p t d", d=512),
             bfv(A_SLAB + 2048, 2048).rearrange("p (t d) -> p t d", d=512)]
    prev_writes_w = [None, None]
    ag_w_cc = {}
    for j in range(4):
        src_j = wmine[:, j * 512:(j + 1) * 512].rearrange("(t p) d -> p t d", p=128)
        for h in range(2):
            dma("gpsimd", slabh[h][:, :, :], src_j[:, 8 * h:8 * h + 8, :],
                waits=[prev_writes_w[h]] if prev_writes_w[h] else [],
                inc=(sP[h], 16), tag=f"slab{j}.{h}")
            lg = sP[h].n
            for q in (2 * h, 2 * h + 1):
                dma("sync", wri[j][q][:, :].rearrange("p (t d) -> p t d", d=512),
                    slabh[h][:, (q % 2) * 4:(q % 2) * 4 + 4, :],
                    waits=[(sP[h], lg)], inc=(sSC[h], 16), tag=f"scw{j}.{q}")
            prev_writes_w[h] = (sSC[h], sSC[h].n)
        em("gpsimd", (lambda j=j: (lambda E: E.collective_compute(
            "AllGather", ALU.bypass, replica_groups=[list(range(NCORES))],
            ins=[wri[j].ap().opt()], outs=[wall[j].ap().opt()])))(),
            waits=[(s, s.n) for s in sSC], inc=(sCC, 1), tag=f"AGw{j}")
        ag_w_cc[j] = sCC.n
    retile_done = [(s, s.n) for s in sSC]
    _FFNQ_PROLOGUE = [True]

    fst = {"qv": 0}
    tstseq = [0]

    def ffnq_stripes(mi):
        wsrc = [w1s, w3s, w2s][mi]
        if mi < 2:
            return [(ft, wsrc[ft * 512:(ft + 1) * 512, :]
                     .rearrange("(g p) d -> p g d", p=128), 1024) for ft in range(FT)]
        res = []
        for st in range(4):
            for hh in range(2):
                res.append(((st, hh),
                            wsrc[st * 512 + hh * 256: st * 512 + (hh + 1) * 256, :]
                            .rearrange("(g p) d -> p g d", p=128), 1536))
        return res

    phbuf = [eqv(P_STRIPE, 1536), eqv(A_SLAB, 1536)]
    phdisc = eqv(P_QBF, 1536)
    phsum = eqv(11264, 32)
    phred = {}

    def phA_works():
        k = 0
        for mi in range(3):
            for sid, src, ncols in ffnq_stripes(mi):
                yield (k, mi, src, ncols)
                k += 1

    def phA_emit(work):
        k, mi, src, ncols = work
        b = k % 2
        gdim = ncols // 256 if mi == 2 else 4
        w = []
        if k >= 2:
            w.append((sS, phred[k - 2]))
        if k == 1:
            w += list(retile_done)
        dma("scalar", phbuf[b][:, 0:ncols].rearrange("p (g d) -> p g d", g=gdim),
            src, waits=w, inc=(sST, 16), tag=f"pha{k}")
        g = sST.n
        em("scalar", (lambda ncols=ncols, b=b, k=k: (lambda E: E.activation(
            phdisc[:, 0:ncols], phbuf[b][:, 0:ncols], AF.Abs,
            accum_out=phsum[:, k:k + 1])))(),
            waits=[(sST, g)], inc=(sS, 1), tag=f"phr{k}")
        phred[k] = sS.n
        if k == 29:
            fst["phs"] = sS.n

    def phA_finale():
        em("vector", lambda E: E.memset(partials[:, 3:4], 0.0), tag="pz3")
        for mi, (c0, c1) in enumerate([(0, 11), (11, 22), (22, 30)]):
            em("vector", (lambda mi=mi, c0=c0, c1=c1: (lambda E: E.reduce_sum(
                partials[:, mi:mi + 1], phsum[:, c0:c1], axis=AX.X)))(),
                waits=([(sS, fst["phs"])] if mi == 0 else []),
                inc=((sV, 1) if mi == 2 else None), tag=f"phf{mi}")
        fin_v = sV.n
        dma("gpsimd", ar_in.ap()[:, :], partials[:], waits=[(sV, fin_v)],
            inc=(sAR, 16), tag="arin")
        em("gpsimd", lambda E: E.collective_compute(
            "AllReduce", ALU.add, replica_groups=[list(range(NCORES))],
            ins=[ar_in.ap().opt()], outs=[ar_out.ap().opt()]),
            waits=[(sAR, sAR.n)], inc=(sCC, 1), tag="AR")
        dma("gpsimd", partials[:], ar_out.ap()[:, :], waits=[(sCC, sCC.n)],
            inc=(sAR, 16), tag="arout")
        fst["ar_g"] = sAR.n

    def ffnq_means():
        em("vector", lambda E: E.engine_nop(),
           waits=[(sAR, fst["ar_g"])], inc=(sV, 1), tag="pcast")
        vv = sV.n
        s, ev = acquire_slot()
        em("tensor", (lambda s=s: (lambda E: E.matmul(
            pb[s * 4][:, 0:4], ones_f32[:], partials[:], start=True, stop=True)))(),
            waits=evw(ev) + [(sV, vv)], inc=(sMM, 1), tag="meanMM")
        mmv = sMM.n
        em("vector", (lambda s=s: (lambda E: E.tensor_scalar_mul(
            means[:], pb[s * 4][:, 0:4], 1.0 / NW)))(),
            waits=[(sMM, mmv)], inc=(sEV, 1), tag="means")
        release_slot(s)
        em("vector", lambda E: E.tensor_scalar_max(mcl[:], means[:], EPS_Q), tag="mcl")
        em("vector", lambda E: E.reciprocal(swq[:], mcl[:]), inc=(sV, 1), tag="swq")
        fst["means_v"] = sV.n

    def ffnq_quant_gen():
        for mi in range(3):
            for item in ffnq_quant_stripes(mi):
                yield item
        yield None

    def ffnq_quant_stripes(mi):
        mv = fst["means_v"]
        dst = wq_in[mi]
        for sid, src, ncols in ffnq_stripes(mi):
            yield (mi, sid, src, ncols, dst, mv)

    def ffnq_emit_stripe(work):
        mi, sid, src, ncols, dst, mv = work
        if True:
            gdim = ncols // 256 if mi == 2 else 4
            dma("sync", pstripe[:, 0:ncols].rearrange("p (g d) -> p g d", g=gdim),
                src, waits=[(sV, fst["qv"]), (sS, fst.get("phs", 0))],
                inc=(sST, 16), tag=f"qb{mi}")
            g = sST.n
            em("vector", (lambda ncols=ncols, mi=mi: (lambda E: E.tensor_scalar(
                pstripe[:, 0:ncols], pstripe[:, 0:ncols], swq[:, mi:mi + 1], MAGIC,
                ALU.mult, ALU.add)))(),
                waits=[(sST, g), (sV, mv), (sES, fst.get("qes", 0))],
                tag=f"qr1{mi}")
            em("vector", (lambda ncols=ncols: (lambda E: E.tensor_scalar(
                pstripe[:, 0:ncols], pstripe[:, 0:ncols], -MAGIC, -1.0,
                ALU.add, ALU.max)))(), tag=f"qr2{mi}")
            em("vector", (lambda ncols=ncols: (lambda E: E.tensor_scalar_min(
                qbf[:, 0:ncols], pstripe[:, 0:ncols], 1.0)))(),
                inc=(sV, 1), tag=f"qr3{mi}")
            fst["qv"] = sV.n
            qv = sV.n
            if mi < 2:
                ft = sid
                groups = [(db, [qbf[:, g4 * 256 + db * 128: g4 * 256 + (db + 1) * 128]
                                for g4 in range(4)], 512, dst[db, ft][:, :]) for db in range(2)]
            else:
                st, hh = sid
                groups = [(fb, [qbf[:, g2 * 768 + fb * 128: g2 * 768 + (fb + 1) * 128]
                                for g2 in range(2)], 256,
                           dst[fb, st][:, hh * 256:(hh + 1) * 256]) for fb in range(6)]
            for _, blocks, nbf, dstap in groups:
                s, ev = acquire_slot()
                bank_bf = pb[s * 4][:, 0:512].bitcast(BF16)
                first = True
                for i, blk in enumerate(blocks):
                    w = (evw(ev) + [(sV, qv)]) if first else []
                    em("tensor", (lambda i=i, blk=blk, bank_bf=bank_bf:
                        (lambda E: E.transpose(
                            bank_bf[:, i * 128:(i + 1) * 128], blk, idn[:])))(),
                        waits=w, inc=((sMM, 1) if i == len(blocks) - 1 else None),
                        tag=f"qT{mi}")
                    first = False
                mmv = sMM.n
                ti = tstseq[0] % 3
                tstseq[0] += 1
                em("scalar", (lambda nbf=nbf, bank_bf=bank_bf, ti=ti: (lambda E: E.activation(
                    tst[ti][:, 0:nbf], bank_bf[:, 0:nbf], AF.Copy)))(),
                    waits=[(sMM, mmv), (sQW3[ti], fst.get(f"tw{ti}", 0))],
                    inc=(sES, 1), tag=f"qTe{mi}")
                release_slot(s)
                ev2 = sES.n
                dma("gpsimd", dstap, tst[ti][:, 0:nbf],
                    waits=[(sES, ev2)], inc=(sQW3[ti], 16), tag=f"qw{mi}")
                fst[f"tw{ti}"] = sQW3[ti].n
        fst["qes"] = sES.n
        nst = FT if mi < 2 else 8
        fst[f"nst{mi}"] = fst.get(f"nst{mi}", 0) + 1
        if fst[f"nst{mi}"] == nst:
            fst[f"agq_ready{mi}"] = [(q, q.n) for q in sQW3]


    def ln_stats(r, tiles, discards, pre_s, pre_v, tag):
        for tb in range(4):
            em("vector", (lambda tb=tb: (lambda E: E.reduce_sum(
                stats[:, tb:tb + 1], tiles[tb], axis=AX.X)))(),
                waits=list(pre_v), tag=f"s1{tag}.{tb}")
            em("scalar", (lambda tb=tb: (lambda E: E.activation(
                discards[tb], tiles[tb], AF.Square,
                accum_out=stats[:, 4 + tb:5 + tb])))(),
                waits=list(pre_s), inc=(sS, 1), tag=f"s2{tag}.{tb}")
        sq_s = sS.n
        em("vector", lambda E: E.tensor_scalar_mul(stats[:, 8:12], stats[:, 0:4], 1.0 / H),
           tag=f"mu{tag}")
        em("vector", lambda E: E.tensor_scalar_mul(stats[:, 12:16], stats[:, 4:8], 1.0 / H),
           waits=[(sS, sq_s)], tag=f"e2{tag}")
        em("vector", lambda E: E.tensor_mul(stats[:, 56:60], stats[:, 8:12], stats[:, 8:12]),
           tag=f"msq{tag}")
        em("vector", lambda E: E.tensor_sub(stats[:, 12:16], stats[:, 12:16], stats[:, 56:60]),
           tag=f"var{tag}")
        em("vector", lambda E: E.tensor_scalar_add(stats[:, 12:16], stats[:, 12:16], EPS_LN),
           tag=f"ve{tag}")
        em("vector", lambda E: E.reciprocal(stats[:, 12:16], stats[:, 12:16]),
           inc=(sV, 1), tag=f"rv{tag}")
        rv_v = sV.n
        em("scalar", lambda E: E.activation(stats[:, 16:20], stats[:, 12:16], AF.Sqrt),
           waits=[(sV, rv_v)], inc=(sS, 1), tag=f"sqrt{tag}")
        sq2_s = sS.n
        em("vector", lambda E: E.tensor_scalar_mul(stats[:, 20:24], stats[:, 8:12], -1.0),
           tag=f"nm{tag}")
        em("vector", lambda E: E.tensor_mul(stats[:, 20:24], stats[:, 20:24], stats[:, 16:20]),
           waits=[(sS, sq2_s)], inc=(sV, 1), tag=f"nb{tag}")
        return sS.n, sV.n

    def transpose_batch3(blocks, dst3, wait_list, tag):
        s, ev = acquire_slot()
        bank_bf = pb[s * 4][:, 0:512].bitcast(BF16)
        n = len(blocks)
        first = True
        for i, blk in enumerate(blocks):
            w = (evw(ev) + list(wait_list)) if first else []
            em("tensor", (lambda i=i, blk=blk, bank_bf=bank_bf: (lambda E: E.transpose(
                bank_bf[:, i * 128:(i + 1) * 128], blk, idn[:])))(),
                waits=w, inc=((sMM, 1) if i == n - 1 else None),
                tag=f"T{tag}")
            first = False
        mmv = sMM.n
        bsrc = bank_bf.rearrange("p (i x) -> p i x", x=128)
        em("scalar", (lambda dst3=dst3, bsrc=bsrc, n=n:
            (lambda E: E.activation(dst3, bsrc[:, 0:n, :], AF.Copy)))(),
            waits=[(sMM, mmv)], inc=(sES, 1), tag=f"Te{tag}")
        release_slot(s)

    def transpose_batch(blocks, dst, dst_lo, nbf, wait_list, tag):
        s, ev = acquire_slot()
        bank_bf = pb[s * 4][:, 0:512].bitcast(BF16)
        first = True
        for i, blk in enumerate(blocks):
            w = (evw(ev) + list(wait_list)) if first else []
            em("tensor", (lambda i=i, blk=blk: (lambda E: E.transpose(
                bank_bf[:, i * 128:(i + 1) * 128], blk, idn[:])))(),
                waits=w, inc=((sMM, 1) if i == len(blocks) - 1 else None),
                tag=f"T{tag}")
            first = False
        mmv = sMM.n
        em("scalar", (lambda dst=dst, dst_lo=dst_lo, nbf=nbf, bank_bf=bank_bf:
            (lambda E: E.activation(dst[:, dst_lo:dst_lo + nbf],
                                    bank_bf[:, 0:nbf], AF.Copy)))(),
            waits=[(sMM, mmv)], inc=(sES, 1), tag=f"Te{tag}")
        release_slot(s)

    proj_cfg = [(fV, AF.Tanh), (siV, AF.Sigmoid), (svV, AF.Silu),
                (qV, AF.Copy), (uV, AF.Copy), (sgV, AF.Silu)]
    o2led = [0, 0]
    out_total = [()]

    def emit_round(r):
        x_tiles = [xV[:, tb * 2048:(tb + 1) * 2048] for tb in range(4)]
        pre = [(sX, xload_g[r]), (sEV, ffn2ev[r - 1])]
        ln1_disc = [out1V[:, tb * 2048:(tb + 1) * 2048] for tb in range(4)]
        s_ln1, v_ln1 = ln_stats(r, x_tiles, ln1_disc, pre, pre, f"L1.{r}")
        for tb in range(4):
            em("vector", (lambda tb=tb: (lambda E: E.tensor_scalar(
                yV[:, tb * 2048:(tb + 1) * 2048], x_tiles[tb],
                stats[:, 16 + tb:17 + tb], stats[:, 20 + tb:21 + tb],
                ALU.mult, ALU.add)))(),
                inc=((sV, 1) if tb == 3 else None), tag=f"y{r}.{tb}")
        y_s = sV.n

        for hb2 in range(8):
            blocks = [yV[:, tb * 2048 + (hb2 * 2 + hbl) * 128:
                         tb * 2048 + (hb2 * 2 + hbl) * 128 + 128]
                      for hbl in range(2) for tb in range(4)]
            transpose_batch(blocks, yTV, hb2 * 1024, 1024,
                            [(sV, y_s), (sK, g_consts)], f"y{r}.{hb2}")
        yT_e = sES.n

        proj_ev = {}
        g_v_box = [0]
        if r == 0:
            qgen = ffnq_quant_gen()
            qdone = [False]
            phgen = phA_works()
            phdone = [False]
        for m6 in range(6):
            dst, func = proj_cfg[m6]
            extra_ev = []
            if m6 in (0, 1):
                extra_ev = [(sEV, ffn2ev[r - 1])]
            if m6 in (2, 4):
                extra_ev = list(out_total[0])
            if m6 == 4:
                extra_ev.append((sV, g_v_box[0]))
            if m6 == 5 and r == 0:
                extra_ev = list(retile_done)
            for rbq in range(4):
                s, ev = acquire_slot()
                group_first = True
                for hbq in range(4):
                    si, wv = wchunk_load([(0, wall[rbq][m6, hbq][:, :])],
                                         extra_waits=[(sCC, ag_w_cc[rbq])],
                                         tag=f"pc{r}.{m6}.{rbq}.{hbq}")
                    for hbl in range(4):
                        hb = hbq * 4 + hbl
                        for rbi in range(4):
                            w = []
                            if group_first:
                                w = evw(ev) + [(sES, yT_e)]
                            if hbl == 0 and rbi == 0:
                                w.append((sW2[si], wv))
                            last_of_chunk = (hbl == 3 and rbi == 3)
                            em("tensor", (lambda s=s, rbi=rbi, si=si, hbl=hbl, hb=hb:
                                (lambda E: E.matmul(
                                    pb[s * 4 + rbi][:],
                                    wslot[:, si * 2048 + hbl * 512 + rbi * 128:
                                          si * 2048 + hbl * 512 + (rbi + 1) * 128],
                                    yTV[:, hb * 512:(hb + 1) * 512],
                                    start=(hb == 0), stop=(hb == 15))))(),
                                waits=w, inc=((sMM, 1) if last_of_chunk else None),
                                tag=f"P{r}.{m6}.{rbq}.{hbq}")
                            group_first = False
                    wchunk_done(si)
                mmv = sMM.n
                for rbi in range(4):
                    rb = rbq * 4 + rbi
                    em("scalar", (lambda dst=dst, func=func, rb=rb, s=s, rbi=rbi:
                        (lambda E: E.activation(
                            dst[:, rb * 512:(rb + 1) * 512], pb[s * 4 + rbi][:], func)
                         if func == AF.Copy else E.activation(
                            dst[:, rb * 512:(rb + 1) * 512], pb[s * 4 + rbi][:], func,
                            bias=zcol[:, 0:1])))(),
                        waits=[(sMM, mmv)] + extra_ev, inc=(sES, 1),
                        tag=f"Pe{r}.{m6}.{rbq}.{rbi}")
                release_slot(s)
                if r == 0 and m6 <= 1 and not phdone[0]:
                    for _ in range(4):
                        pw = next(phgen, None)
                        if pw is None:
                            phdone[0] = True
                            break
                        phA_emit(pw)
                if r == 0 and m6 >= 3 and not qdone[0]:
                    for _ in range(3):
                        wk = next(qgen)
                        if wk is None:
                            qdone[0] = True
                            break
                        ffnq_emit_stripe(wk)
            proj_ev[m6] = sES.n
            if m6 == 2:
                em("vector", lambda E: E.tensor_mul(
                    siV[:, 0:8192], siV[:, 0:8192], svV[:, 0:8192]),
                    waits=[(sES, proj_ev[2])], inc=(sV, 1), tag=f"g{r}")
                g_v_box[0] = sV.n
                for rg in range(16):
                    em("vector", (lambda rg=rg: (lambda E: E.tensor_tensor_scan(
                        stV[:, rg * 512:(rg + 1) * 512], fV[:, rg * 512:(rg + 1) * 512],
                        siV[:, rg * 512:(rg + 1) * 512], 0.0, ALU.mult, ALU.add)))(),
                        waits=[(sV, g_v_box[0])], tag=f"scS{r}.{rg}")
                for rg in range(16):
                    em("vector", (lambda rg=rg: (lambda E: E.tensor_tensor_scan(
                        cV[:, rg * 512:(rg + 1) * 512], fV[:, rg * 512:(rg + 1) * 512],
                        fV[:, rg * 512:(rg + 1) * 512], 1.0, ALU.mult, ALU.bypass)))(),
                        tag=f"scC{r}.{rg}")
                cV3 = cV.rearrange("p (rg t) -> p rg t", t=512)
                sV3 = stV.rearrange("p (rg t) -> p rg t", t=512)
                em("vector", lambda E: E.tensor_copy(
                    summ[:, 0:16], cV3[:, :, 511:512].squeeze(2)), tag=f"sP{r}")
                em("vector", lambda E: E.tensor_copy(
                    summ[:, 16:32], sV3[:, :, 511:512].squeeze(2)),
                    inc=(sV, 1), tag=f"sQ{r}")
                summ_v = sV.n
                dma("gpsimd", cin[r].ap()[:, :], summ[:],
                    waits=[(sV, summ_v)], inc=(sCI, 16), tag=f"cin{r}")
                em("gpsimd", (lambda r=r: (lambda E: E.collective_compute(
                    "AllGather", ALU.bypass, replica_groups=[list(range(NCORES))],
                    ins=[cin[r].ap().opt()], outs=[cout[r].ap().opt()])))(),
                    waits=[(sCI, sCI.n)], inc=(sCC, 1), tag=f"AGc{r}")
                dma("gpsimd", s_all[:].rearrange("p (j f) -> p j f", f=32),
                    cout[r].ap().rearrange("(j p) f -> p j f", p=128),
                    waits=[(sCC, sCC.n)], inc=(sSA, 16), tag=f"sall{r}")
                sall_g = sSA.n
            if r == 0 and m6 == 1:
                while not phdone[0]:
                    pw = next(phgen, None)
                    if pw is None:
                        phdone[0] = True
                        break
                    phA_emit(pw)
                phA_finale()
                ffnq_means()

        if r == 0:
            while not qdone[0]:
                wk = next(qgen)
                if wk is None:
                    qdone[0] = True
                    break
                ffnq_emit_stripe(wk)
            for mi in range(3):
                em("gpsimd", (lambda mi=mi: (lambda E: E.collective_compute(
                    "AllGather", ALU.bypass, replica_groups=[list(range(NCORES))],
                    ins=[wq_in[mi].ap().opt()], outs=[wq_ag[mi].ap().opt()])))(),
                    waits=list(fst[f"agq_ready{mi}"]), inc=(sCC, 1), tag=f"AGq{mi}")
                fst[f"ag{mi}_cc"] = sCC.n

        em("vector", lambda E: E.tensor_copy(s_in[:], s_pref[:]),
           waits=[(sSA, sall_g)], tag=f"fi{r}")
        for j in range(8):
            Pj = s_all[:, j * 32:j * 32 + 16]
            Qj = s_all[:, j * 32 + 16:j * 32 + 32]
            em("vector", (lambda Pj=Pj, j=j: (lambda E: E.tensor_scalar(
                t16a[:], Pj, maskm[:, j:j + 1], maskn[:, j:j + 1], ALU.mult, ALU.add)))(),
                tag=f"f1.{j}")
            em("vector", lambda E: E.tensor_mul(s_in[:], s_in[:], t16a[:]), tag=f"f2.{j}")
            em("vector", (lambda Qj=Qj, j=j: (lambda E: E.tensor_scalar_mul(
                t16b[:], Qj, maskm[:, j:j + 1])))(), tag=f"f3.{j}")
            em("vector", lambda E: E.tensor_add(s_in[:], s_in[:], t16b[:]), tag=f"f4.{j}")
        for j in range(8):
            Pj = s_all[:, j * 32:j * 32 + 16]
            Qj = s_all[:, j * 32 + 16:j * 32 + 32]
            em("vector", (lambda Pj=Pj: (lambda E: E.tensor_mul(s_pref[:], s_pref[:], Pj)))(),
               tag=f"f5.{j}")
            em("vector", (lambda Qj=Qj: (lambda E: E.tensor_add(s_pref[:], s_pref[:], Qj)))(),
               tag=f"f6.{j}")
        for rg in range(16):
            em("vector", (lambda rg=rg: (lambda E: E.scalar_tensor_tensor(
                cV[:, rg * 512:(rg + 1) * 512], cV[:, rg * 512:(rg + 1) * 512],
                s_in[:, rg:rg + 1], stV[:, rg * 512:(rg + 1) * 512],
                ALU.mult, ALU.add)))(), tag=f"co{r}.{rg}")
        em("vector", lambda E: E.tensor_mul(stV[:, 0:8192], qV[:, 0:8192], cV[:, 0:8192]),
           waits=[(sES, proj_ev[3])], inc=(sV, 1), tag=f"tmp{r}")
        tmp_v = sV.n
        em("scalar", lambda E: E.activation(qV[:, 0:8192], stV[:, 0:8192], AF.Silu,
                                            bias=zcol[:, 0:1]),
           waits=[(sV, tmp_v)], inc=(sS, 1), tag=f"ro{r}")
        ro_s = sS.n
        em("vector", lambda E: E.tensor_mul(uV[:, 0:8192], uV[:, 0:8192], sgV[:, 0:8192]),
           waits=[(sES, proj_ev[5])], inc=(sV, 1), tag=f"z{r}")
        z_v = sV.n

        em("vector", lambda E: E.memset(stats[:, 0:8], 0.0), tag=f"l2z{r}")
        sqv_led = {}
        for ht in range(4):
            s, ev = acquire_slot()
            group_first = True
            for mseq, m2 in enumerate((1, 0)):
                stat_src = qV if m2 == 0 else zV
                for kq in range(4):
                    si, wv = wchunk_load([(0, wall[ht][6 + m2, kq][:, :])],
                                         extra_waits=[(sCC, ag_w_cc[ht])],
                                         tag=f"mc{r}.{ht}.{m2}.{kq}")
                    for kbl in range(4):
                        kb = kq * 4 + kbl
                        for tbi in range(4):
                            w = []
                            if group_first:
                                w = evw(ev) + [(sV, z_v)]
                            if mseq == 1 and kq == 0 and kbl == 0 and tbi == 0:
                                w.append((sS, ro_s))
                            if kbl == 0 and tbi == 0:
                                w.append((sW2[si], wv))
                            last_of_chunk = (kbl == 3 and tbi == 3)
                            em("tensor", (lambda s=s, tbi=tbi, si=si, kbl=kbl, kb=kb,
                                          stat_src=stat_src, mseq=mseq: (lambda E: E.matmul(
                                pb[s * 4 + tbi][:],
                                stat_src[:, kb * 512 + tbi * 128: kb * 512 + (tbi + 1) * 128],
                                wslot[:, si * 2048 + kbl * 512: si * 2048 + (kbl + 1) * 512],
                                start=(mseq == 0 and kb == 0), stop=(mseq == 1 and kb == 15))))(),
                                waits=w, inc=((sMM, 1) if last_of_chunk else None),
                                tag=f"M{r}.{ht}.{m2}.{kq}")
                            group_first = False
                    wchunk_done(si)
            mmv = sMM.n
            for tbi in range(4):
                em("vector", (lambda s=s, tbi=tbi, ht=ht: (lambda E: E.tensor_tensor(
                    out1V[:, tbi * 2048 + ht * 512: tbi * 2048 + (ht + 1) * 512],
                    pb[s * 4 + tbi][:],
                    xV[:, tbi * 2048 + ht * 512: tbi * 2048 + (ht + 1) * 512],
                    ALU.add)))(),
                    waits=[(sMM, mmv)], inc=(sEV, 1), tag=f"Me{r}.{ht}.{tbi}")
            evht = sEV.n
            for tbi in range(4):
                blk = out1V[:, tbi * 2048 + ht * 512: tbi * 2048 + (ht + 1) * 512]
                em("vector", (lambda blk=blk: (lambda E: E.reduce_sum(
                    stats[:, 59:60], blk, axis=AX.X)))(), tag=f"l2a{r}.{ht}.{tbi}")
                em("vector", (lambda tbi=tbi: (lambda E: E.tensor_add(
                    stats[:, tbi:tbi + 1], stats[:, tbi:tbi + 1],
                    stats[:, 59:60])))(), tag=f"l2b{r}.{ht}.{tbi}")
                sw = [(sEV, evht)]
                if (ht, tbi) in sqv_led:
                    pass
                if ht > 0:
                    sw.append((sV, sqv_led[(ht - 1, tbi)]))
                em("scalar", (lambda blk=blk, tbi=tbi: (lambda E: E.activation(
                    xV[:, tbi * 2048:tbi * 2048 + 512], blk, AF.Square,
                    accum_out=stats[:, 60 + tbi:61 + tbi])))(),
                    waits=sw, inc=(sS, 1), tag=f"l2s{r}.{ht}.{tbi}")
                sqn = sS.n
                em("vector", (lambda tbi=tbi: (lambda E: E.tensor_add(
                    stats[:, 4 + tbi:5 + tbi], stats[:, 4 + tbi:5 + tbi],
                    stats[:, 60 + tbi:61 + tbi])))(),
                    waits=[(sS, sqn)], inc=(sV, 1), tag=f"l2c{r}.{ht}.{tbi}")
                sqv_led[(ht, tbi)] = sV.n
            release_slot(s)
        mix_ev = sEV.n

        o1_tiles = [out1V[:, tb * 2048:(tb + 1) * 2048] for tb in range(4)]
        em("vector", lambda E: E.tensor_scalar_mul(stats[:, 8:12], stats[:, 0:4],
                                                   1.0 / H), tag=f"muL2{r}")
        em("vector", lambda E: E.tensor_scalar_mul(stats[:, 12:16], stats[:, 4:8],
                                                   1.0 / H), tag=f"e2L2{r}")
        em("vector", lambda E: E.tensor_mul(stats[:, 56:60], stats[:, 8:12],
                                            stats[:, 8:12]), tag=f"msqL2{r}")
        em("vector", lambda E: E.tensor_sub(stats[:, 12:16], stats[:, 12:16],
                                            stats[:, 56:60]), tag=f"varL2{r}")
        em("vector", lambda E: E.tensor_scalar_add(stats[:, 12:16], stats[:, 12:16],
                                                   EPS_LN), tag=f"veL2{r}")
        em("vector", lambda E: E.reciprocal(stats[:, 12:16], stats[:, 12:16]),
           inc=(sV, 1), tag=f"rvL2{r}")
        rv2_v = sV.n
        em("scalar", lambda E: E.activation(stats[:, 16:20], stats[:, 12:16], AF.Sqrt),
           waits=[(sV, rv2_v)], inc=(sS, 1), tag=f"sqrtL2{r}")
        s_ln2 = sS.n
        em("vector", lambda E: E.tensor_scalar_mul(stats[:, 20:24], stats[:, 8:12],
                                                   -1.0), tag=f"nmL2{r}")
        em("vector", lambda E: E.tensor_mul(stats[:, 20:24], stats[:, 20:24],
                                            stats[:, 16:20]),
           waits=[(sS, s_ln2)], inc=(sV, 1), tag=f"nbL2{r}")
        rq2_tb = [0, 0, 0, 0]
        for tb in range(4):
            em("vector", (lambda tb=tb: (lambda E: E.tensor_scalar(
                y2stage, o1_tiles[tb],
                stats[:, 16 + tb:17 + tb], stats[:, 20 + tb:21 + tb],
                ALU.mult, ALU.add)))(), tag=f"y2.{r}.{tb}")
            em("vector", (lambda tb=tb: (lambda E: E.reduce_max(
                stats[:, 24 + tb:25 + tb], y2stage, axis=AX.X,
                apply_absolute_value=True)))(), tag=f"qm{r}.{tb}")
            em("vector", (lambda tb=tb: (lambda E: E.tensor_scalar_max(
                stats[:, 24 + tb:25 + tb], stats[:, 24 + tb:25 + tb], EPS_Q)))(),
                tag=f"qc{r}.{tb}")
            em("vector", (lambda tb=tb: (lambda E: E.tensor_scalar_mul(
                stats[:, 32 + tb:33 + tb], stats[:, 24 + tb:25 + tb], 1.0 / 127.0)))(),
                tag=f"rs{r}.{tb}")
            em("vector", (lambda tb=tb: (lambda E: E.reciprocal(
                stats[:, 28 + tb:29 + tb], stats[:, 24 + tb:25 + tb])))(),
                tag=f"r1{r}.{tb}")
            em("vector", (lambda tb=tb: (lambda E: E.tensor_scalar_mul(
                stats[:, 28 + tb:29 + tb], stats[:, 28 + tb:29 + tb], 127.0)))(),
                tag=f"r2{r}.{tb}")
            em("vector", (lambda tb=tb: (lambda E: E.tensor_scalar(
                ln2scr, y2stage, stats[:, 28 + tb:29 + tb], MAGIC, ALU.mult, ALU.add)))(),
                tag=f"rq1{r}.{tb}")
            em("vector", (lambda tb=tb: (lambda E: E.tensor_scalar_add(
                y2qV[:, tb * 2048:(tb + 1) * 2048], ln2scr, -MAGIC)))(),
                inc=(sV, 1), tag=f"rq2{r}.{tb}")
            rq2_tb[tb] = sV.n
        y2q_v = sV.n
        em("vector", lambda E: E.tensor_scalar(
            stats[:, 44:48], stats[:, 32:36], mcl[:, 0:1], None, ALU.mult),
            waits=[(sV, fst.get("means_v", 0))], tag=f"sc1{r}")
        em("vector", lambda E: E.tensor_scalar(
            stats[:, 48:52], stats[:, 32:36], mcl[:, 1:2], None, ALU.mult),
            inc=(sV, 1), tag=f"sc3{r}")
        sc_v = sV.n

        y2qT3 = y2qTV.rearrange("p (dg x) -> p dg x", x=512)
        for tb in range(4):
            for grp in range(2):
                blocks = [y2qV[:, tb * 2048 + (grp * 8 + i) * 128:
                              tb * 2048 + (grp * 8 + i) * 128 + 128] for i in range(8)]
                transpose_batch3(blocks,
                                 y2qT3[:, grp * 8:(grp + 1) * 8,
                                       tb * 128:(tb + 1) * 128],
                                 [(sV, rq2_tb[tb])], f"q{r}.{tb}.{grp}")
        y2qT_e = sES.n

        for mi, (dstV, func, sccol) in enumerate(
                [(aV, AF.Silu, 44), (bV, AF.Copy, 48)]):
            agw = [(sCC, fst.get(f"ag{mi}_cc", 0))]
            ev_extra = []
            if r == 0 and mi == 0:
                ev_extra = [(sV, fst["qv"])] + list(fst.get("agq_ready2", []))
            for ft in range(FT):
                s, ev = acquire_slot()
                group_first = True
                for q in range(4):
                    srcs = [(i * 512, wq_ag[mi][4 * q + i, ft][:, :]) for i in range(4)]
                    si, wv = wchunk_load(srcs, extra_waits=agw, tag=f"fc{r}.{mi}.{ft}.{q}")
                    for dgl in range(4):
                        dgb = q * 4 + dgl
                        for tbi in range(4):
                            w = []
                            if group_first:
                                w = evw(ev) + [(sES, y2qT_e), (sV, sc_v)]
                            if dgl == 0 and tbi == 0:
                                w.append((sW2[si], wv))
                            last_of_chunk = (dgl == 3 and tbi == 3)
                            em("tensor", (lambda s=s, tbi=tbi, si=si, dgl=dgl, dgb=dgb:
                                (lambda E: E.matmul(
                                    pb[s * 4 + tbi][:],
                                    y2qTV[:, dgb * 512 + tbi * 128: dgb * 512 + (tbi + 1) * 128],
                                    wslot[:, si * 2048 + dgl * 512: si * 2048 + (dgl + 1) * 512],
                                    start=(dgb == 0), stop=(dgb == 15))))(),
                                waits=w, inc=((sMM, 1) if last_of_chunk else None),
                                tag=f"F1.{r}.{mi}.{ft}.{q}")
                            group_first = False
                    wchunk_done(si)
                mmv = sMM.n
                for tbi in range(4):
                    em("scalar", (lambda dstV=dstV, func=func, tbi=tbi, ft=ft, s=s,
                                  sccol=sccol: (lambda E: E.activation(
                        dstV[:, tbi * FH + ft * 512: tbi * FH + ft * 512 + 512],
                        pb[s * 4 + tbi][:], func,
                        scale=stats[:, sccol + tbi:sccol + tbi + 1])
                        if func == AF.Copy else E.activation(
                        dstV[:, tbi * FH + ft * 512: tbi * FH + ft * 512 + 512],
                        pb[s * 4 + tbi][:], func,
                        bias=zcol[:, 0:1], scale=stats[:, sccol + tbi:sccol + tbi + 1])))(),
                        waits=[(sMM, mmv)] + (ev_extra if ft == 0 and tbi == 0
                                              else []),
                        inc=(sES, 1), tag=f"F1e{r}.{mi}.{ft}.{tbi}")
                release_slot(s)
                if mi == 1:
                    if ft == 0:
                        em("vector", lambda E: E.memset(stats[:, 24:28], 0.0),
                           tag=f"hz{r}")
                    ev_g = sES.n
                    for tbi in range(4):
                        lo = tbi * FH + ft * 512
                        em("vector", (lambda lo=lo: (lambda E: E.tensor_mul(
                            aV[:, lo:lo + 512], aV[:, lo:lo + 512],
                            bV[:, lo:lo + 512])))(),
                            waits=([(sES, ev_g)] if tbi == 0 else []),
                            tag=f"h{r}.{ft}.{tbi}")
                        em("vector", (lambda lo=lo: (lambda E: E.reduce_max(
                            stats[:, 60:61], aV[:, lo:lo + 512], axis=AX.X,
                            apply_absolute_value=True)))(), tag=f"hm{r}.{ft}.{tbi}")
                        em("vector", (lambda tbi=tbi: (lambda E: E.tensor_max(
                            stats[:, 24 + tbi:25 + tbi], stats[:, 24 + tbi:25 + tbi],
                            stats[:, 60:61])))(), tag=f"hx{r}.{ft}.{tbi}")
        ffn1_ev = sES.n

        hq_tb = [0, 0, 0, 0]
        st_v = [0, 0, 0, 0]
        for tb in range(4):
            em("vector", (lambda tb=tb: (lambda E: E.tensor_scalar_max(
                stats[:, 24 + tb:25 + tb], stats[:, 24 + tb:25 + tb], EPS_Q)))(),
                tag=f"hc{r}.{tb}")
            em("vector", (lambda tb=tb: (lambda E: E.tensor_scalar_mul(
                stats[:, 40 + tb:41 + tb], stats[:, 24 + tb:25 + tb], 1.0 / 127.0)))(),
                tag=f"hr{r}.{tb}")
            em("vector", (lambda tb=tb: (lambda E: E.reciprocal(
                stats[:, 36 + tb:37 + tb], stats[:, 24 + tb:25 + tb])))(),
                tag=f"h1{r}.{tb}")
            em("vector", (lambda tb=tb: (lambda E: E.tensor_scalar_mul(
                stats[:, 36 + tb:37 + tb], stats[:, 36 + tb:37 + tb], 127.0)))(),
                inc=(sV, 1), tag=f"h2{r}.{tb}")
            st_v[tb] = sV.n
        hqst2 = [hqstage, eqv(14080, 2816)]
        hq1_led = {}
        hq2_led = {}
        for tb in range(4):
            for hh in range(2):
                k = tb * 2 + hh
                w = [(sV, st_v[tb])]
                if k >= 2:
                    w.append((sV, hq2_led[k - 2]))
                em("scalar", (lambda tb=tb, hh=hh, k=k: (lambda E: E.activation(
                    hqst2[k % 2],
                    aV[:, tb * FH + hh * 2816: tb * FH + (hh + 1) * 2816],
                    AF.Copy, bias=MAGIC,
                    scale=stats[:, 36 + tb:37 + tb])))(),
                    waits=w, inc=(sES, 1), tag=f"hq1{r}.{tb}.{hh}")
                hq1_led[k] = sES.n
                em("vector", (lambda tb=tb, hh=hh, k=k: (lambda E: E.tensor_scalar_add(
                    bV[:, tb * FH + hh * 2816: tb * FH + (hh + 1) * 2816],
                    hqst2[k % 2], -MAGIC)))(),
                    waits=[(sES, hq1_led[k])], inc=(sV, 1), tag=f"hq2{r}.{tb}.{hh}")
                hq2_led[k] = sV.n
            hq_tb[tb] = hq2_led[tb * 2 + 1]
        em("vector", lambda E: E.tensor_scalar(
            stats[:, 52:56], stats[:, 40:44], mcl[:, 2:3], None, ALU.mult),
            inc=(sV, 1), tag=f"sc2{r}")
        hq_v = sV.n

        hqT3 = aV.rearrange("p (fb x) -> p fb x", x=512)
        for tb in range(4):
            for grp in range(6):
                fb0 = grp * 8
                nb = min(8, FB - fb0)
                blocks = [bV[:, tb * FH + (fb0 + i) * 128:
                             tb * FH + (fb0 + i) * 128 + 128] for i in range(nb)]
                transpose_batch3(blocks,
                                 hqT3[:, fb0:fb0 + nb, tb * 128:(tb + 1) * 128],
                                 [(sV, hq_tb[tb])], f"hq{r}.{tb}.{grp}")
        hqT_e = sES.n

        agw2 = [(sCC, fst.get("ag2_cc", 0))]
        for ht in range(4):
            s, ev = acquire_slot()
            group_first = True
            for q in range(11):
                srcs = [(i * 512, wq_ag[2][4 * q + i, ht][:, :]) for i in range(4)]
                si, wv = wchunk_load(srcs, extra_waits=agw2, tag=f"gc{r}.{ht}.{q}")
                for fbl in range(4):
                    fb = q * 4 + fbl
                    for tbi in range(4):
                        w = []
                        if group_first:
                            w = evw(ev) + [(sES, hqT_e)]
                        if fbl == 0 and tbi == 0:
                            w.append((sW2[si], wv))
                        last_of_chunk = (fbl == 3 and tbi == 3)
                        em("tensor", (lambda s=s, tbi=tbi, si=si, fbl=fbl, fb=fb:
                            (lambda E: E.matmul(
                                pb[s * 4 + tbi][:],
                                aV[:, fb * 512 + tbi * 128: fb * 512 + (tbi + 1) * 128],
                                wslot[:, si * 2048 + fbl * 512: si * 2048 + (fbl + 1) * 512],
                                start=(fb == 0), stop=(fb == FB - 1))))(),
                            waits=w, inc=((sMM, 1) if last_of_chunk else None),
                            tag=f"F2.{r}.{ht}.{q}")
                        group_first = False
                wchunk_done(si)
            mmv = sMM.n
            for tbi in range(4):
                oi = (ht * 4 + tbi) % 2
                em("vector", (lambda s=s, tbi=tbi, ht=ht, oi=oi: (lambda E: E.scalar_tensor_tensor(
                    o2st[oi], pb[s * 4 + tbi][:], stats[:, 52 + tbi:53 + tbi],
                    out1V[:, tbi * 2048 + ht * 512: tbi * 2048 + (ht + 1) * 512],
                    ALU.mult, ALU.add)))(),
                    waits=[(sMM, mmv), (sO2[oi], o2led[oi])], inc=(sEV, 1),
                    tag=f"F2e{r}.{ht}.{tbi}")
                evn = sEV.n
                dma("gpsimd",
                    out[r * TS + tbi * 128: r * TS + (tbi + 1) * 128,
                        ht * 512:(ht + 1) * 512],
                    o2st[oi], waits=[(sEV, evn)], inc=(sO2[oi], 16), tag=f"o{r}.{ht}.{tbi}")
                o2led[oi] = sO2[oi].n
            release_slot(s)
        ffn2ev[r] = sEV.n
        xfree_mm[r + 1] = sMM.n
        out_total[0] = ((sO2[0], sO2[0].n), (sO2[1], sO2[1].n))

    for r in range(ROUNDS):
        emit_round(r)
        if r + 1 < ROUNDS:
            load_x(r + 1, eng="scalar")

    em("gpsimd", None, waits=[(sO2[0], sO2[0].n), (sO2[1], sO2[1].n)], tag="fin")

    P.check()

    for s in sems:
        s.h = nc.alloc_semaphore(s.name)
    with nc.Block() as block:
        P.emit(block)
    return nc


_nc_cache = {}


def _get_nc():
    if "nc" not in _nc_cache:
        _nc_cache["nc"] = build_nc()
    return _nc_cache["nc"]


def kernel(L0_attn_x, L0_attn_w_f, L0_attn_w_i, L0_attn_w_v, L0_attn_w_q,
           L0_attn_w_ro, L0_attn_w_aux, L0_attn_w_up, L0_attn_w_gate,
           L0_attn_w_down, L0_attn_init_state, L0_ffn_w1, L0_ffn_w2, L0_ffn_w3):
    x = np.ascontiguousarray(np.asarray(L0_attn_x, dtype=np.float32).reshape(16384, 2048))
    wlist = [np.ascontiguousarray(np.asarray(a, dtype=np.float32)) for a in [
        L0_attn_w_f, L0_attn_w_i, L0_attn_w_v, L0_attn_w_q, L0_attn_w_up,
        L0_attn_w_gate, L0_attn_w_ro, L0_attn_w_down]]
    w1 = np.asarray(L0_ffn_w1, dtype=np.float32)
    w2 = np.asarray(L0_ffn_w2, dtype=np.float32)
    w3 = np.asarray(L0_ffn_w3, dtype=np.float32)
    init = np.asarray(L0_attn_init_state, dtype=np.float32)
    init_s = np.ascontiguousarray(init.reshape(16, 128).T)
    idn = np.eye(128, dtype=ml_dtypes.bfloat16)

    nc = _get_nc()
    in_maps = []
    for c in range(NCORES):
        xin = np.ascontiguousarray(np.concatenate(
            [x[(8 * r + c) * TS:(8 * r + c + 1) * TS] for r in range(ROUNDS)], axis=0))
        mask_m = np.zeros((128, 8), dtype=np.float32)
        mask_m[:, :c] = 1.0
        mask_n = np.ascontiguousarray(1.0 - mask_m)
        w1sh = np.ascontiguousarray(w1[:, c * 256:(c + 1) * 256])
        w3sh = np.ascontiguousarray(w3[:, c * 256:(c + 1) * 256])
        w2sh = np.zeros((2048, 768), dtype=np.float32)
        lo = c * 768
        hi = min(FH, lo + 768)
        if lo < FH:
            w2sh[:, :hi - lo] = w2[:, lo:hi]
        im = dict(xin=xin, init_s=init_s, mask_m=mask_m, mask_n=mask_n,
                  idn=idn, w1s=w1sh, w3s=w3sh, w2s=w2sh, wmine=wlist[c])
        in_maps.append(im)

    res = run_bass_kernel_spmd(nc, in_maps, core_ids=list(range(NCORES)))

    full = np.empty((16384, 2048), dtype=np.float32)
    for c in range(NCORES):
        oc = np.asarray(res.results[c]["out"])
        for r in range(ROUNDS):
            full[(8 * r + c) * TS:(8 * r + c + 1) * TS] = oc[r * TS:(r + 1) * TS]
    return full.reshape(1, 16384, 2048)


# version 20
# speedup vs baseline: 1.0377x; 1.0008x over previous
import numpy as np
import ml_dtypes
import concourse.bass as bass
import concourse.mybir as mybir
from concourse.bass_utils import run_bass_kernel_spmd

F32 = mybir.dt.float32
BF16 = mybir.dt.bfloat16
AF = mybir.ActivationFunctionType
ALU = mybir.AluOpType
AX = mybir.AxisListType

NCORES = 8
ROUNDS = 4
TS = 512
H = 2048
FH = 5632
FT = 11
FB = 44
HT = 4
MAGIC = float(np.float32(1.5 * 2 ** 23))
EPS_LN = 1e-5
EPS_Q = 1e-5
NW = float(FH * H)

A_X = 0
A_X2 = 8192
A_YT = 16384
A_F = 20480
A_SI = 24576
A_ST = 28672
A_C = 32768
A_Q = 36864
A_SV = 40960
A_END = 45056
P_STRIPE = 8192
P_QBF = 9728
P_TST = 10496
A_SLAB = 12288
A_SG = 12288


class Sem:
    def __init__(self, name):
        self.name = name
        self.n = 0
        self.h = None


class Plan:
    ENGS = ("sync", "gpsimd", "tensor", "vector", "scalar")

    def __init__(self):
        self.items = {e: [] for e in self.ENGS}
        self.lastwait = {e: {} for e in self.ENGS}

    def em(self, eng, fn, waits=(), inc=None, tag=""):
        w = []
        for sem, val in waits:
            if val <= 0:
                continue
            if self.lastwait[eng].get(sem.name, -1) >= val:
                continue
            self.lastwait[eng][sem.name] = val
            w.append((sem, val))
        self.items[eng].append((tuple(w), fn, inc, tag))
        if inc is not None:
            inc[0].n += inc[1]

    def check(self):
        pos = {e: 0 for e in self.ENGS}
        cnt = {}
        progress = True
        while progress:
            progress = False
            for e in self.ENGS:
                while pos[e] < len(self.items[e]):
                    waits, _, inc, _ = self.items[e][pos[e]]
                    if all(cnt.get(s.name, 0) >= v for s, v in waits):
                        if inc is not None:
                            s, k = inc
                            cnt[s.name] = cnt.get(s.name, 0) + k
                        pos[e] += 1
                        progress = True
                    else:
                        break
        stuck = {e: pos[e] for e in self.ENGS if pos[e] < len(self.items[e])}
        if stuck:
            msg = []
            for e, p in stuck.items():
                waits, _, _, tag = self.items[e][p]
                wd = [(s.name, v, cnt.get(s.name, 0)) for s, v in waits]
                msg.append(f"{e}@{p}/{len(self.items[e])} tag={tag} waits={wd}")
            raise RuntimeError("plan deadlock: " + " | ".join(msg))

    def emit(self, block):
        def mk(eng_name):
            drainy = eng_name in ("vector", "scalar")
            def body(eng):
                for waits, fn, inc, _ in self.items[eng_name]:
                    for s, v in waits:
                        eng.wait_ge(s.h, v)
                    ins = fn(eng) if fn is not None else None
                    if inc is not None:
                        s, k = inc
                        if ins is not None:
                            ins.then_inc(s.h, k)
                        else:
                            eng.sem_inc(s.h, k)
                    if drainy and ins is not None:
                        eng.drain()
            return body
        block.sync(mk("sync"))
        block.gpsimd(mk("gpsimd"))
        block.tensor(mk("tensor"))
        block.vector(mk("vector"))
        block.scalar(mk("scalar"))


def build_nc():
    nc = bass.Bass("TRN2", target_bir_lowering=False)

    xin = nc.dram_tensor("xin", [ROUNDS * TS, H], F32, kind="ExternalInput").ap()
    wmine = nc.dram_tensor("wmine", [H, H], F32, kind="ExternalInput").ap()
    w1s = nc.dram_tensor("w1s", [FH, 256], F32, kind="ExternalInput").ap()
    w3s = nc.dram_tensor("w3s", [FH, 256], F32, kind="ExternalInput").ap()
    w2s = nc.dram_tensor("w2s", [H, 768], F32, kind="ExternalInput").ap()
    init_in = nc.dram_tensor("init_s", [128, 16], F32, kind="ExternalInput").ap()
    mm_in = nc.dram_tensor("mask_m", [128, 8], F32, kind="ExternalInput").ap()
    mn_in = nc.dram_tensor("mask_n", [128, 8], F32, kind="ExternalInput").ap()
    idn_in = nc.dram_tensor("idn", [128, 128], BF16, kind="ExternalInput").ap()
    out = nc.dram_tensor("out", [ROUNDS * TS, H], F32, kind="ExternalOutput").ap()

    wri = [nc.dram_tensor(f"wri{j}", [4, 128, 2048], BF16) for j in range(4)]
    wall = [nc.dram_tensor(f"wall{j}", [8, 4, 128, 2048], BF16, addr_space="Shared") for j in range(4)]
    wq_in = [nc.dram_tensor("w1q_in", [2, FT, 128, 512], BF16),
             nc.dram_tensor("w3q_in", [2, FT, 128, 512], BF16),
             nc.dram_tensor("w2q_in", [6, HT, 128, 512], BF16)]
    wq_ag = [nc.dram_tensor("w1q_ag", [16, FT, 128, 512], BF16, addr_space="Shared"),
             nc.dram_tensor("w3q_ag", [16, FT, 128, 512], BF16, addr_space="Shared"),
             nc.dram_tensor("w2q_ag", [48, HT, 128, 512], BF16, addr_space="Shared")]
    ar_in = nc.dram_tensor("ar_in", [128, 4], F32)
    ar_out = nc.dram_tensor("ar_out", [128, 4], F32, addr_space="Shared")
    cin = [nc.dram_tensor(f"cin{r}", [128, 32], F32) for r in range(ROUNDS)]
    cout = [nc.dram_tensor(f"cout{r}", [1024, 32], F32, addr_space="Shared") for r in range(ROUNDS)]

    arena = nc.alloc_sbuf_tensor("arena", [128, A_END], F32)
    wslot = nc.alloc_sbuf_tensor("wslot", [128, 6144], BF16)
    stats = nc.alloc_sbuf_tensor("stats", [128, 64], F32)
    summ = nc.alloc_sbuf_tensor("summ", [128, 32], F32)
    s_all = nc.alloc_sbuf_tensor("s_all", [128, 256], F32)
    s_pref = nc.alloc_sbuf_tensor("s_pref", [128, 16], F32)
    s_in = nc.alloc_sbuf_tensor("s_in", [128, 16], F32)
    t16a = nc.alloc_sbuf_tensor("t16a", [128, 16], F32)
    t16b = nc.alloc_sbuf_tensor("t16b", [128, 16], F32)
    maskm = nc.alloc_sbuf_tensor("maskm", [128, 8], F32)
    maskn = nc.alloc_sbuf_tensor("maskn", [128, 8], F32)
    initb = nc.alloc_sbuf_tensor("initb", [128, 16], F32)
    partials = nc.alloc_sbuf_tensor("partials", [128, 4], F32)
    ones_f32 = nc.alloc_sbuf_tensor("ones_f32", [128, 128], F32)
    means = nc.alloc_sbuf_tensor("means", [128, 4], F32)
    mcl = nc.alloc_sbuf_tensor("mcl", [128, 4], F32)
    swq = nc.alloc_sbuf_tensor("swq", [128, 4], F32)
    zcol = nc.alloc_sbuf_tensor("zcol", [128, 1], F32)
    ones_bf = nc.alloc_sbuf_tensor("ones_bf", [128, 128], BF16)
    idn = nc.alloc_sbuf_tensor("idn_s", [128, 128], BF16)

    pb = [nc.alloc_psum_tensor(f"pb{i}", [128, 512], F32) for i in range(8)]

    sMM = Sem("sMM")
    sEV = Sem("sEV")
    sES = Sem("sES")
    sV = Sem("sV")
    sS = Sem("sS")
    sCC = Sem("sCC")
    sK = Sem("sK")
    sX = Sem("sX")
    sP = [Sem("sP0"), Sem("sP1")]
    sSC = [Sem(f"sSC{m}") for m in range(2)]
    sQW3 = [Sem(f"sQW{i}") for i in range(3)]
    sW2 = [Sem("sW0"), Sem("sW1"), Sem("sW2s")]
    sST = Sem("sST")
    sGQ = Sem("sGQ")
    sAR = Sem("sAR")
    sSA = Sem("sSA")
    sCI = Sem("sCI")
    sO2 = [Sem("sO0"), Sem("sO1")]
    sems = [sMM, sEV, sES, sV, sS, sCC, sK, sX] + sP + sSC + sW2 + sQW3 + [
        sST, sGQ, sAR, sSA, sCI] + sO2

    def eqv(off, n):
        return arena[:, off:off + n]

    def bfv(off, neq):
        return arena[:, off:off + neq].bitcast(BF16)

    xV = eqv(A_X, 8192)
    y2stage = eqv(A_X, 2048)
    ln2scr = eqv(A_X + 2048, 2048)
    aV = bfv(A_X, 11264)
    hqstage = eqv(11264, 2816)
    sgV = bfv(A_SG, 4096)
    yTV = bfv(A_YT, 4096)
    y2qTV = bfv(A_YT, 4096)
    fV = bfv(A_F, 4096)
    ln1scr = eqv(A_F, 2048)
    out1V = eqv(A_F, 8192)
    yV = bfv(A_SI, 4096)
    siV = bfv(A_SI, 4096)
    stV = bfv(A_ST, 4096)
    cV = bfv(A_C, 4096)
    qV = bfv(A_Q, 4096)
    y2qV = bfv(A_Q, 4096)
    bV = bfv(A_ST, 11264)
    svV = bfv(A_SV, 4096)
    uV = bfv(A_SV, 4096)
    zV = bfv(A_SV, 4096)
    o2st = [eqv(A_SV + i * 512, 512) for i in range(2)]
    pstripe = eqv(P_STRIPE, 1536)
    qbf = bfv(P_QBF, 768)
    tst = [bfv(P_TST + i * 256, 256) for i in range(3)]
    slabv = bfv(A_SLAB, 2048)

    P = Plan()
    em = P.em

    def dma(eng, o, i, waits=(), inc=None, tag="", **kw):
        em(eng, (lambda o=o, i=i, kw=kw: (lambda E: E.dma_start(out=o, in_=i, **kw)))(),
           waits=waits, inc=inc, tag=tag)

    slot_ev = [(0, 0), (0, 0)]
    slot_ctr = [0]

    def acquire_slot():
        s = slot_ctr[0] % 2
        slot_ctr[0] += 1
        return s, slot_ev[s]

    def release_slot(s):
        slot_ev[s] = (sES.n, sEV.n)

    def evw(ev):
        return [(sES, ev[0]), (sEV, ev[1])]

    wfree = [0, 0, 0]
    wseq = [0]

    def wchunk_load(srcs, extra_waits=(), tag=""):
        si = wseq[0] % 3
        wseq[0] += 1
        first = True
        n = len(srcs)
        for idx, (lo, src) in enumerate(srcs):
            w = list(extra_waits) + ([(sMM, wfree[si])] if first else [])
            width = 2048 if n == 1 else 512
            dma("sync", wslot[:, si * 2048 + lo: si * 2048 + lo + width],
                src, waits=w, inc=(sW2[si], 16), tag=tag)
            first = False
        return si, sW2[si].n

    def wchunk_done(si):
        wfree[si] = sMM.n

    em("vector", lambda E: E.memset(zcol[:], 0.0), tag="z0")
    em("vector", lambda E: E.memset(ones_bf[:], 1.0), tag="ones")
    em("vector", lambda E: E.memset(ones_f32[:], 1.0), tag="ones32")
    dma("gpsimd", idn[:], idn_in[:, :], inc=(sK, 16), tag="idn")
    dma("gpsimd", maskm[:], mm_in[:, :], inc=(sK, 16), tag="mm")
    dma("gpsimd", maskn[:], mn_in[:, :], inc=(sK, 16), tag="mn")
    dma("gpsimd", initb[:], init_in[:, :], inc=(sK, 16), tag="init")
    g_consts = sK.n
    em("vector", lambda E: E.tensor_copy(s_pref[:], initb[:]),
       waits=[(sK, g_consts)], inc=(sV, 1), tag="spref0")

    ffn2ev = {-1: 0}
    xfree_mm = [0] * (ROUNDS + 1)
    xload_g = [0] * ROUNDS

    def load_x(r, eng="gpsimd"):
        dma(eng, xV.rearrange("p (t d) -> p t d", t=4),
            xin[r * TS:(r + 1) * TS, :].rearrange("(t p) d -> p t d", p=128),
            waits=[(sMM, xfree_mm[r])], inc=(sX, 16), tag=f"x{r}")
        xload_g[r] = sX.n

    load_x(0, eng="scalar")

    slabh = [bfv(A_SLAB, 2048).rearrange("p (t d) -> p t d", d=512),
             bfv(A_SLAB + 2048, 2048).rearrange("p (t d) -> p t d", d=512)]
    prev_writes_w = [None, None]
    ag_w_cc = {}
    for j in range(4):
        src_j = wmine[:, j * 512:(j + 1) * 512].rearrange("(t p) d -> p t d", p=128)
        for h in range(2):
            dma("gpsimd", slabh[h][:, :, :], src_j[:, 8 * h:8 * h + 8, :],
                waits=[prev_writes_w[h]] if prev_writes_w[h] else [],
                inc=(sP[h], 16), tag=f"slab{j}.{h}")
            lg = sP[h].n
            for q in (2 * h, 2 * h + 1):
                dma("sync", wri[j][q][:, :].rearrange("p (t d) -> p t d", d=512),
                    slabh[h][:, (q % 2) * 4:(q % 2) * 4 + 4, :],
                    waits=[(sP[h], lg)], inc=(sSC[h], 16), tag=f"scw{j}.{q}")
            prev_writes_w[h] = (sSC[h], sSC[h].n)
        em("gpsimd", (lambda j=j: (lambda E: E.collective_compute(
            "AllGather", ALU.bypass, replica_groups=[list(range(NCORES))],
            ins=[wri[j].ap().opt()], outs=[wall[j].ap().opt()])))(),
            waits=[(s, s.n) for s in sSC], inc=(sCC, 1), tag=f"AGw{j}")
        ag_w_cc[j] = sCC.n
    retile_done = [(s, s.n) for s in sSC]
    _FFNQ_PROLOGUE = [True]

    fst = {"qv": 0}
    tstseq = [0]

    def ffnq_stripes(mi):
        wsrc = [w1s, w3s, w2s][mi]
        if mi < 2:
            return [(ft, wsrc[ft * 512:(ft + 1) * 512, :]
                     .rearrange("(g p) d -> p g d", p=128), 1024) for ft in range(FT)]
        res = []
        for st in range(4):
            for hh in range(2):
                res.append(((st, hh),
                            wsrc[st * 512 + hh * 256: st * 512 + (hh + 1) * 256, :]
                            .rearrange("(g p) d -> p g d", p=128), 1536))
        return res

    phbuf = [eqv(P_STRIPE, 1536), eqv(A_SLAB, 1536)]
    phdisc = eqv(P_QBF, 1536)
    phsum = eqv(11264, 32)
    phred = {}

    def phA_works():
        k = 0
        for mi in range(3):
            for sid, src, ncols in ffnq_stripes(mi):
                yield (k, mi, src, ncols)
                k += 1

    def phA_emit(work):
        k, mi, src, ncols = work
        b = k % 2
        gdim = ncols // 256 if mi == 2 else 4
        w = []
        if k >= 2:
            w.append((sS, phred[k - 2]))
        if k == 1:
            w += list(retile_done)
        dma("scalar", phbuf[b][:, 0:ncols].rearrange("p (g d) -> p g d", g=gdim),
            src, waits=w, inc=(sST, 16), tag=f"pha{k}")
        g = sST.n
        em("scalar", (lambda ncols=ncols, b=b, k=k: (lambda E: E.activation(
            phdisc[:, 0:ncols], phbuf[b][:, 0:ncols], AF.Abs,
            accum_out=phsum[:, k:k + 1])))(),
            waits=[(sST, g)], inc=(sS, 1), tag=f"phr{k}")
        phred[k] = sS.n
        if k == 29:
            fst["phs"] = sS.n

    def phA_finale():
        em("vector", lambda E: E.memset(partials[:, 3:4], 0.0), tag="pz3")
        for mi, (c0, c1) in enumerate([(0, 11), (11, 22), (22, 30)]):
            em("vector", (lambda mi=mi, c0=c0, c1=c1: (lambda E: E.reduce_sum(
                partials[:, mi:mi + 1], phsum[:, c0:c1], axis=AX.X)))(),
                waits=([(sS, fst["phs"])] if mi == 0 else []),
                inc=((sV, 1) if mi == 2 else None), tag=f"phf{mi}")
        fin_v = sV.n
        dma("gpsimd", ar_in.ap()[:, :], partials[:], waits=[(sV, fin_v)],
            inc=(sAR, 16), tag="arin")
        em("gpsimd", lambda E: E.collective_compute(
            "AllReduce", ALU.add, replica_groups=[list(range(NCORES))],
            ins=[ar_in.ap().opt()], outs=[ar_out.ap().opt()]),
            waits=[(sAR, sAR.n)], inc=(sCC, 1), tag="AR")
        dma("gpsimd", partials[:], ar_out.ap()[:, :], waits=[(sCC, sCC.n)],
            inc=(sAR, 16), tag="arout")
        fst["ar_g"] = sAR.n

    def ffnq_means():
        em("vector", lambda E: E.engine_nop(),
           waits=[(sAR, fst["ar_g"])], inc=(sV, 1), tag="pcast")
        vv = sV.n
        s, ev = acquire_slot()
        em("tensor", (lambda s=s: (lambda E: E.matmul(
            pb[s * 4][:, 0:4], ones_f32[:], partials[:], start=True, stop=True)))(),
            waits=evw(ev) + [(sV, vv)], inc=(sMM, 1), tag="meanMM")
        mmv = sMM.n
        em("vector", (lambda s=s: (lambda E: E.tensor_scalar_mul(
            means[:], pb[s * 4][:, 0:4], 1.0 / NW)))(),
            waits=[(sMM, mmv)], inc=(sEV, 1), tag="means")
        release_slot(s)
        em("vector", lambda E: E.tensor_scalar_max(mcl[:], means[:], EPS_Q), tag="mcl")
        em("vector", lambda E: E.reciprocal(swq[:], mcl[:]), inc=(sV, 1), tag="swq")
        fst["means_v"] = sV.n

    def ffnq_quant_gen():
        for mi in range(3):
            for item in ffnq_quant_stripes(mi):
                yield item
        yield None

    def ffnq_quant_stripes(mi):
        mv = fst["means_v"]
        dst = wq_in[mi]
        for sid, src, ncols in ffnq_stripes(mi):
            yield (mi, sid, src, ncols, dst, mv)

    def ffnq_emit_stripe(work):
        mi, sid, src, ncols, dst, mv = work
        if True:
            gdim = ncols // 256 if mi == 2 else 4
            dma("sync", pstripe[:, 0:ncols].rearrange("p (g d) -> p g d", g=gdim),
                src, waits=[(sV, fst["qv"]), (sS, fst.get("phs", 0))],
                inc=(sST, 16), tag=f"qb{mi}")
            g = sST.n
            em("vector", (lambda ncols=ncols, mi=mi: (lambda E: E.tensor_scalar(
                pstripe[:, 0:ncols], pstripe[:, 0:ncols], swq[:, mi:mi + 1], MAGIC,
                ALU.mult, ALU.add)))(),
                waits=[(sST, g), (sV, mv), (sES, fst.get("qes", 0))],
                tag=f"qr1{mi}")
            em("vector", (lambda ncols=ncols: (lambda E: E.tensor_scalar(
                pstripe[:, 0:ncols], pstripe[:, 0:ncols], -MAGIC, -1.0,
                ALU.add, ALU.max)))(), tag=f"qr2{mi}")
            em("vector", (lambda ncols=ncols: (lambda E: E.tensor_scalar_min(
                qbf[:, 0:ncols], pstripe[:, 0:ncols], 1.0)))(),
                inc=(sV, 1), tag=f"qr3{mi}")
            fst["qv"] = sV.n
            qv = sV.n
            if mi < 2:
                ft = sid
                groups = [(db, [qbf[:, g4 * 256 + db * 128: g4 * 256 + (db + 1) * 128]
                                for g4 in range(4)], 512, dst[db, ft][:, :]) for db in range(2)]
            else:
                st, hh = sid
                groups = [(fb, [qbf[:, g2 * 768 + fb * 128: g2 * 768 + (fb + 1) * 128]
                                for g2 in range(2)], 256,
                           dst[fb, st][:, hh * 256:(hh + 1) * 256]) for fb in range(6)]
            for _, blocks, nbf, dstap in groups:
                s, ev = acquire_slot()
                bank_bf = pb[s * 4][:, 0:512].bitcast(BF16)
                first = True
                for i, blk in enumerate(blocks):
                    w = (evw(ev) + [(sV, qv)]) if first else []
                    em("tensor", (lambda i=i, blk=blk, bank_bf=bank_bf:
                        (lambda E: E.transpose(
                            bank_bf[:, i * 128:(i + 1) * 128], blk, idn[:])))(),
                        waits=w, inc=((sMM, 1) if i == len(blocks) - 1 else None),
                        tag=f"qT{mi}")
                    first = False
                mmv = sMM.n
                ti = tstseq[0] % 3
                tstseq[0] += 1
                em("scalar", (lambda nbf=nbf, bank_bf=bank_bf, ti=ti: (lambda E: E.activation(
                    tst[ti][:, 0:nbf], bank_bf[:, 0:nbf], AF.Copy)))(),
                    waits=[(sMM, mmv), (sQW3[ti], fst.get(f"tw{ti}", 0))],
                    inc=(sES, 1), tag=f"qTe{mi}")
                release_slot(s)
                ev2 = sES.n
                dma("gpsimd", dstap, tst[ti][:, 0:nbf],
                    waits=[(sES, ev2)], inc=(sQW3[ti], 16), tag=f"qw{mi}")
                fst[f"tw{ti}"] = sQW3[ti].n
        fst["qes"] = sES.n
        nst = FT if mi < 2 else 8
        fst[f"nst{mi}"] = fst.get(f"nst{mi}", 0) + 1
        if fst[f"nst{mi}"] == nst:
            fst[f"agq_ready{mi}"] = [(q, q.n) for q in sQW3]


    def ln_stats(r, tiles, discards, pre_s, pre_v, tag):
        for tb in range(4):
            em("vector", (lambda tb=tb: (lambda E: E.reduce_sum(
                stats[:, tb:tb + 1], tiles[tb], axis=AX.X)))(),
                waits=list(pre_v), tag=f"s1{tag}.{tb}")
            em("scalar", (lambda tb=tb: (lambda E: E.activation(
                discards[tb], tiles[tb], AF.Square,
                accum_out=stats[:, 4 + tb:5 + tb])))(),
                waits=list(pre_s), inc=(sS, 1), tag=f"s2{tag}.{tb}")
        sq_s = sS.n
        em("vector", lambda E: E.tensor_scalar_mul(stats[:, 8:12], stats[:, 0:4], 1.0 / H),
           tag=f"mu{tag}")
        em("vector", lambda E: E.tensor_scalar_mul(stats[:, 12:16], stats[:, 4:8], 1.0 / H),
           waits=[(sS, sq_s)], tag=f"e2{tag}")
        em("vector", lambda E: E.tensor_mul(stats[:, 56:60], stats[:, 8:12], stats[:, 8:12]),
           tag=f"msq{tag}")
        em("vector", lambda E: E.tensor_sub(stats[:, 12:16], stats[:, 12:16], stats[:, 56:60]),
           tag=f"var{tag}")
        em("vector", lambda E: E.tensor_scalar_add(stats[:, 12:16], stats[:, 12:16], EPS_LN),
           tag=f"ve{tag}")
        em("vector", lambda E: E.reciprocal(stats[:, 12:16], stats[:, 12:16]),
           inc=(sV, 1), tag=f"rv{tag}")
        rv_v = sV.n
        em("scalar", lambda E: E.activation(stats[:, 16:20], stats[:, 12:16], AF.Sqrt),
           waits=[(sV, rv_v)], inc=(sS, 1), tag=f"sqrt{tag}")
        sq2_s = sS.n
        em("vector", lambda E: E.tensor_scalar_mul(stats[:, 20:24], stats[:, 8:12], -1.0),
           tag=f"nm{tag}")
        em("vector", lambda E: E.tensor_mul(stats[:, 20:24], stats[:, 20:24], stats[:, 16:20]),
           waits=[(sS, sq2_s)], inc=(sV, 1), tag=f"nb{tag}")
        return sS.n, sV.n

    def transpose_batch3(blocks, dst3, wait_list, tag):
        s, ev = acquire_slot()
        bank_bf = pb[s * 4][:, 0:512].bitcast(BF16)
        n = len(blocks)
        first = True
        for i, blk in enumerate(blocks):
            w = (evw(ev) + list(wait_list)) if first else []
            em("tensor", (lambda i=i, blk=blk, bank_bf=bank_bf: (lambda E: E.transpose(
                bank_bf[:, i * 128:(i + 1) * 128], blk, idn[:])))(),
                waits=w, inc=((sMM, 1) if i == n - 1 else None),
                tag=f"T{tag}")
            first = False
        mmv = sMM.n
        bsrc = bank_bf.rearrange("p (i x) -> p i x", x=128)
        em("scalar", (lambda dst3=dst3, bsrc=bsrc, n=n:
            (lambda E: E.activation(dst3, bsrc[:, 0:n, :], AF.Copy)))(),
            waits=[(sMM, mmv)], inc=(sES, 1), tag=f"Te{tag}")
        release_slot(s)

    def transpose_batch(blocks, dst, dst_lo, nbf, wait_list, tag):
        s, ev = acquire_slot()
        bank_bf = pb[s * 4][:, 0:512].bitcast(BF16)
        first = True
        for i, blk in enumerate(blocks):
            w = (evw(ev) + list(wait_list)) if first else []
            em("tensor", (lambda i=i, blk=blk: (lambda E: E.transpose(
                bank_bf[:, i * 128:(i + 1) * 128], blk, idn[:])))(),
                waits=w, inc=((sMM, 1) if i == len(blocks) - 1 else None),
                tag=f"T{tag}")
            first = False
        mmv = sMM.n
        em("scalar", (lambda dst=dst, dst_lo=dst_lo, nbf=nbf, bank_bf=bank_bf:
            (lambda E: E.activation(dst[:, dst_lo:dst_lo + nbf],
                                    bank_bf[:, 0:nbf], AF.Copy)))(),
            waits=[(sMM, mmv)], inc=(sES, 1), tag=f"Te{tag}")
        release_slot(s)

    proj_cfg = [(fV, AF.Tanh), (siV, AF.Sigmoid), (svV, AF.Silu),
                (qV, AF.Copy), (uV, AF.Copy), (sgV, AF.Silu)]
    o2led = [0, 0]
    out_total = [()]

    def emit_round(r):
        x_tiles = [xV[:, tb * 2048:(tb + 1) * 2048] for tb in range(4)]
        pre = [(sX, xload_g[r]), (sEV, ffn2ev[r - 1])]
        ln1_disc = [out1V[:, tb * 2048:(tb + 1) * 2048] for tb in range(4)]
        s_ln1, v_ln1 = ln_stats(r, x_tiles, ln1_disc, pre, pre, f"L1.{r}")
        for tb in range(4):
            em("vector", (lambda tb=tb: (lambda E: E.tensor_scalar(
                yV[:, tb * 2048:(tb + 1) * 2048], x_tiles[tb],
                stats[:, 16 + tb:17 + tb], stats[:, 20 + tb:21 + tb],
                ALU.mult, ALU.add)))(),
                inc=((sV, 1) if tb == 3 else None), tag=f"y{r}.{tb}")
        y_s = sV.n

        for hb2 in range(8):
            blocks = [yV[:, tb * 2048 + (hb2 * 2 + hbl) * 128:
                         tb * 2048 + (hb2 * 2 + hbl) * 128 + 128]
                      for hbl in range(2) for tb in range(4)]
            transpose_batch(blocks, yTV, hb2 * 1024, 1024,
                            [(sV, y_s), (sK, g_consts)], f"y{r}.{hb2}")
        yT_e = sES.n

        proj_ev = {}
        g_v_box = [0]
        if r == 0:
            qgen = ffnq_quant_gen()
            qdone = [False]
            phgen = phA_works()
            phdone = [False]
        for m6 in range(6):
            dst, func = proj_cfg[m6]
            extra_ev = []
            if m6 in (0, 1):
                extra_ev = [(sEV, ffn2ev[r - 1])]
            if m6 in (2, 4):
                extra_ev = list(out_total[0])
            if m6 == 4:
                extra_ev.append((sV, g_v_box[0]))
            if m6 == 5 and r == 0:
                extra_ev = list(retile_done)
            for rbq in range(4):
                s, ev = acquire_slot()
                group_first = True
                for hbq in range(4):
                    si, wv = wchunk_load([(0, wall[rbq][m6, hbq][:, :])],
                                         extra_waits=[(sCC, ag_w_cc[rbq])],
                                         tag=f"pc{r}.{m6}.{rbq}.{hbq}")
                    for hbl in range(4):
                        hb = hbq * 4 + hbl
                        for rbi in range(4):
                            w = []
                            if group_first:
                                w = evw(ev) + [(sES, yT_e)]
                            if hbl == 0 and rbi == 0:
                                w.append((sW2[si], wv))
                            last_of_chunk = (hbl == 3 and rbi == 3)
                            em("tensor", (lambda s=s, rbi=rbi, si=si, hbl=hbl, hb=hb:
                                (lambda E: E.matmul(
                                    pb[s * 4 + rbi][:],
                                    wslot[:, si * 2048 + hbl * 512 + rbi * 128:
                                          si * 2048 + hbl * 512 + (rbi + 1) * 128],
                                    yTV[:, hb * 512:(hb + 1) * 512],
                                    start=(hb == 0), stop=(hb == 15))))(),
                                waits=w, inc=((sMM, 1) if last_of_chunk else None),
                                tag=f"P{r}.{m6}.{rbq}.{hbq}")
                            group_first = False
                    wchunk_done(si)
                mmv = sMM.n
                for rbi in range(4):
                    rb = rbq * 4 + rbi
                    em("scalar", (lambda dst=dst, func=func, rb=rb, s=s, rbi=rbi:
                        (lambda E: E.activation(
                            dst[:, rb * 512:(rb + 1) * 512], pb[s * 4 + rbi][:], func)
                         if func == AF.Copy else E.activation(
                            dst[:, rb * 512:(rb + 1) * 512], pb[s * 4 + rbi][:], func,
                            bias=zcol[:, 0:1])))(),
                        waits=[(sMM, mmv)] + extra_ev, inc=(sES, 1),
                        tag=f"Pe{r}.{m6}.{rbq}.{rbi}")
                release_slot(s)
                if r == 0 and m6 <= 1 and not phdone[0]:
                    for _ in range(4):
                        pw = next(phgen, None)
                        if pw is None:
                            phdone[0] = True
                            break
                        phA_emit(pw)
                if r == 0 and m6 >= 3 and not qdone[0]:
                    for _ in range(3):
                        wk = next(qgen)
                        if wk is None:
                            qdone[0] = True
                            break
                        ffnq_emit_stripe(wk)
            proj_ev[m6] = sES.n
            if m6 == 2:
                em("vector", lambda E: E.tensor_mul(
                    siV[:, 0:8192], siV[:, 0:8192], svV[:, 0:8192]),
                    waits=[(sES, proj_ev[2])], inc=(sV, 1), tag=f"g{r}")
                g_v_box[0] = sV.n
                for rg in range(16):
                    em("vector", (lambda rg=rg: (lambda E: E.tensor_tensor_scan(
                        stV[:, rg * 512:(rg + 1) * 512], fV[:, rg * 512:(rg + 1) * 512],
                        siV[:, rg * 512:(rg + 1) * 512], 0.0, ALU.mult, ALU.add)))(),
                        waits=[(sV, g_v_box[0])], tag=f"scS{r}.{rg}")
                for rg in range(16):
                    em("vector", (lambda rg=rg: (lambda E: E.tensor_tensor_scan(
                        cV[:, rg * 512:(rg + 1) * 512], fV[:, rg * 512:(rg + 1) * 512],
                        fV[:, rg * 512:(rg + 1) * 512], 1.0, ALU.mult, ALU.bypass)))(),
                        tag=f"scC{r}.{rg}")
                cV3 = cV.rearrange("p (rg t) -> p rg t", t=512)
                sV3 = stV.rearrange("p (rg t) -> p rg t", t=512)
                em("vector", lambda E: E.tensor_copy(
                    summ[:, 0:16], cV3[:, :, 511:512].squeeze(2)), tag=f"sP{r}")
                em("vector", lambda E: E.tensor_copy(
                    summ[:, 16:32], sV3[:, :, 511:512].squeeze(2)),
                    inc=(sV, 1), tag=f"sQ{r}")
                summ_v = sV.n
                dma("gpsimd", cin[r].ap()[:, :], summ[:],
                    waits=[(sV, summ_v)], inc=(sCI, 16), tag=f"cin{r}")
                em("gpsimd", (lambda r=r: (lambda E: E.collective_compute(
                    "AllGather", ALU.bypass, replica_groups=[list(range(NCORES))],
                    ins=[cin[r].ap().opt()], outs=[cout[r].ap().opt()])))(),
                    waits=[(sCI, sCI.n)], inc=(sCC, 1), tag=f"AGc{r}")
                dma("gpsimd", s_all[:].rearrange("p (j f) -> p j f", f=32),
                    cout[r].ap().rearrange("(j p) f -> p j f", p=128),
                    waits=[(sCC, sCC.n)], inc=(sSA, 16), tag=f"sall{r}")
                sall_g = sSA.n
            if r == 0 and m6 == 1:
                while not phdone[0]:
                    pw = next(phgen, None)
                    if pw is None:
                        phdone[0] = True
                        break
                    phA_emit(pw)
                phA_finale()
                ffnq_means()

        if r == 0:
            while not qdone[0]:
                wk = next(qgen)
                if wk is None:
                    qdone[0] = True
                    break
                ffnq_emit_stripe(wk)
            for mi in range(3):
                em("gpsimd", (lambda mi=mi: (lambda E: E.collective_compute(
                    "AllGather", ALU.bypass, replica_groups=[list(range(NCORES))],
                    ins=[wq_in[mi].ap().opt()], outs=[wq_ag[mi].ap().opt()])))(),
                    waits=list(fst[f"agq_ready{mi}"]), inc=(sCC, 1), tag=f"AGq{mi}")
                fst[f"ag{mi}_cc"] = sCC.n

        em("vector", lambda E: E.tensor_copy(s_in[:], s_pref[:]),
           waits=[(sSA, sall_g)], tag=f"fi{r}")
        for j in range(8):
            Pj = s_all[:, j * 32:j * 32 + 16]
            Qj = s_all[:, j * 32 + 16:j * 32 + 32]
            em("vector", (lambda Pj=Pj, j=j: (lambda E: E.tensor_scalar(
                t16a[:], Pj, maskm[:, j:j + 1], maskn[:, j:j + 1], ALU.mult, ALU.add)))(),
                tag=f"f1.{j}")
            em("vector", lambda E: E.tensor_mul(s_in[:], s_in[:], t16a[:]), tag=f"f2.{j}")
            em("vector", (lambda Qj=Qj, j=j: (lambda E: E.tensor_scalar_mul(
                t16b[:], Qj, maskm[:, j:j + 1])))(), tag=f"f3.{j}")
            em("vector", lambda E: E.tensor_add(s_in[:], s_in[:], t16b[:]), tag=f"f4.{j}")
        for j in range(8):
            Pj = s_all[:, j * 32:j * 32 + 16]
            Qj = s_all[:, j * 32 + 16:j * 32 + 32]
            em("vector", (lambda Pj=Pj: (lambda E: E.tensor_mul(s_pref[:], s_pref[:], Pj)))(),
               tag=f"f5.{j}")
            em("vector", (lambda Qj=Qj: (lambda E: E.tensor_add(s_pref[:], s_pref[:], Qj)))(),
               tag=f"f6.{j}")
        for rg in range(16):
            em("vector", (lambda rg=rg: (lambda E: E.scalar_tensor_tensor(
                cV[:, rg * 512:(rg + 1) * 512], cV[:, rg * 512:(rg + 1) * 512],
                s_in[:, rg:rg + 1], stV[:, rg * 512:(rg + 1) * 512],
                ALU.mult, ALU.add)))(), tag=f"co{r}.{rg}")
        em("vector", lambda E: E.tensor_mul(stV[:, 0:8192], qV[:, 0:8192], cV[:, 0:8192]),
           waits=[(sES, proj_ev[3])], inc=(sV, 1), tag=f"tmp{r}")
        tmp_v = sV.n
        em("scalar", lambda E: E.activation(qV[:, 0:8192], stV[:, 0:8192], AF.Silu,
                                            bias=zcol[:, 0:1]),
           waits=[(sV, tmp_v)], inc=(sS, 1), tag=f"ro{r}")
        ro_s = sS.n
        em("vector", lambda E: E.tensor_mul(uV[:, 0:8192], uV[:, 0:8192], sgV[:, 0:8192]),
           waits=[(sES, proj_ev[5])], inc=(sV, 1), tag=f"z{r}")
        z_v = sV.n

        em("vector", lambda E: E.memset(stats[:, 0:8], 0.0), tag=f"l2z{r}")
        sqv_led = {}
        for ht in range(4):
            s, ev = acquire_slot()
            group_first = True
            for mseq, m2 in enumerate((1, 0)):
                stat_src = qV if m2 == 0 else zV
                for kq in range(4):
                    si, wv = wchunk_load([(0, wall[ht][6 + m2, kq][:, :])],
                                         extra_waits=[(sCC, ag_w_cc[ht])],
                                         tag=f"mc{r}.{ht}.{m2}.{kq}")
                    for kbl in range(4):
                        kb = kq * 4 + kbl
                        for tbi in range(4):
                            w = []
                            if group_first:
                                w = evw(ev) + [(sV, z_v)]
                            if mseq == 1 and kq == 0 and kbl == 0 and tbi == 0:
                                w.append((sS, ro_s))
                            if kbl == 0 and tbi == 0:
                                w.append((sW2[si], wv))
                            last_of_chunk = (kbl == 3 and tbi == 3)
                            em("tensor", (lambda s=s, tbi=tbi, si=si, kbl=kbl, kb=kb,
                                          stat_src=stat_src, mseq=mseq: (lambda E: E.matmul(
                                pb[s * 4 + tbi][:],
                                stat_src[:, kb * 512 + tbi * 128: kb * 512 + (tbi + 1) * 128],
                                wslot[:, si * 2048 + kbl * 512: si * 2048 + (kbl + 1) * 512],
                                start=(mseq == 0 and kb == 0), stop=(mseq == 1 and kb == 15))))(),
                                waits=w, inc=((sMM, 1) if last_of_chunk else None),
                                tag=f"M{r}.{ht}.{m2}.{kq}")
                            group_first = False
                    wchunk_done(si)
            mmv = sMM.n
            for tbi in range(4):
                em("vector", (lambda s=s, tbi=tbi, ht=ht: (lambda E: E.tensor_tensor(
                    out1V[:, tbi * 2048 + ht * 512: tbi * 2048 + (ht + 1) * 512],
                    pb[s * 4 + tbi][:],
                    xV[:, tbi * 2048 + ht * 512: tbi * 2048 + (ht + 1) * 512],
                    ALU.add)))(),
                    waits=[(sMM, mmv)], inc=(sEV, 1), tag=f"Me{r}.{ht}.{tbi}")
            evht = sEV.n
            for tbi in range(4):
                blk = out1V[:, tbi * 2048 + ht * 512: tbi * 2048 + (ht + 1) * 512]
                em("vector", (lambda blk=blk: (lambda E: E.reduce_sum(
                    stats[:, 59:60], blk, axis=AX.X)))(), tag=f"l2a{r}.{ht}.{tbi}")
                em("vector", (lambda tbi=tbi: (lambda E: E.tensor_add(
                    stats[:, tbi:tbi + 1], stats[:, tbi:tbi + 1],
                    stats[:, 59:60])))(), tag=f"l2b{r}.{ht}.{tbi}")
                sw = [(sEV, evht)]
                if (ht, tbi) in sqv_led:
                    pass
                if ht > 0:
                    sw.append((sV, sqv_led[(ht - 1, tbi)]))
                em("scalar", (lambda blk=blk, tbi=tbi: (lambda E: E.activation(
                    xV[:, tbi * 2048:tbi * 2048 + 512], blk, AF.Square,
                    accum_out=stats[:, 60 + tbi:61 + tbi])))(),
                    waits=sw, inc=(sS, 1), tag=f"l2s{r}.{ht}.{tbi}")
                sqn = sS.n
                em("vector", (lambda tbi=tbi: (lambda E: E.tensor_add(
                    stats[:, 4 + tbi:5 + tbi], stats[:, 4 + tbi:5 + tbi],
                    stats[:, 60 + tbi:61 + tbi])))(),
                    waits=[(sS, sqn)], inc=(sV, 1), tag=f"l2c{r}.{ht}.{tbi}")
                sqv_led[(ht, tbi)] = sV.n
            release_slot(s)
        mix_ev = sEV.n

        o1_tiles = [out1V[:, tb * 2048:(tb + 1) * 2048] for tb in range(4)]
        em("vector", lambda E: E.tensor_scalar_mul(stats[:, 8:12], stats[:, 0:4],
                                                   1.0 / H), tag=f"muL2{r}")
        em("vector", lambda E: E.tensor_scalar_mul(stats[:, 12:16], stats[:, 4:8],
                                                   1.0 / H), tag=f"e2L2{r}")
        em("vector", lambda E: E.tensor_mul(stats[:, 56:60], stats[:, 8:12],
                                            stats[:, 8:12]), tag=f"msqL2{r}")
        em("vector", lambda E: E.tensor_sub(stats[:, 12:16], stats[:, 12:16],
                                            stats[:, 56:60]), tag=f"varL2{r}")
        em("vector", lambda E: E.tensor_scalar_add(stats[:, 12:16], stats[:, 12:16],
                                                   EPS_LN), tag=f"veL2{r}")
        em("vector", lambda E: E.reciprocal(stats[:, 12:16], stats[:, 12:16]),
           inc=(sV, 1), tag=f"rvL2{r}")
        rv2_v = sV.n
        em("scalar", lambda E: E.activation(stats[:, 16:20], stats[:, 12:16], AF.Sqrt),
           waits=[(sV, rv2_v)], inc=(sS, 1), tag=f"sqrtL2{r}")
        s_ln2 = sS.n
        em("vector", lambda E: E.tensor_scalar_mul(stats[:, 20:24], stats[:, 8:12],
                                                   -1.0), tag=f"nmL2{r}")
        em("vector", lambda E: E.tensor_mul(stats[:, 20:24], stats[:, 20:24],
                                            stats[:, 16:20]),
           waits=[(sS, s_ln2)], inc=(sV, 1), tag=f"nbL2{r}")
        rq2_tb = [0, 0, 0, 0]
        y2st2 = [y2stage, eqv(4096, 2048)]
        rq1_done = [0, 0, 0, 0]
        for tb in range(4):
            yw = [(sS, rq1_done[tb - 2])] if tb >= 2 else []
            em("vector", (lambda tb=tb: (lambda E: E.tensor_scalar(
                y2st2[tb % 2], o1_tiles[tb],
                stats[:, 16 + tb:17 + tb], stats[:, 20 + tb:21 + tb],
                ALU.mult, ALU.add)))(), waits=yw, tag=f"y2.{r}.{tb}")
            em("vector", (lambda tb=tb: (lambda E: E.reduce_max(
                stats[:, 24 + tb:25 + tb], y2st2[tb % 2], axis=AX.X,
                apply_absolute_value=True)))(), tag=f"qm{r}.{tb}")
            em("vector", (lambda tb=tb: (lambda E: E.tensor_scalar_max(
                stats[:, 24 + tb:25 + tb], stats[:, 24 + tb:25 + tb], EPS_Q)))(),
                tag=f"qc{r}.{tb}")
            em("vector", (lambda tb=tb: (lambda E: E.tensor_scalar_mul(
                stats[:, 32 + tb:33 + tb], stats[:, 24 + tb:25 + tb], 1.0 / 127.0)))(),
                tag=f"rs{r}.{tb}")
            em("vector", (lambda tb=tb: (lambda E: E.reciprocal(
                stats[:, 28 + tb:29 + tb], stats[:, 24 + tb:25 + tb])))(),
                tag=f"r1{r}.{tb}")
            em("vector", (lambda tb=tb: (lambda E: E.tensor_scalar_mul(
                stats[:, 28 + tb:29 + tb], stats[:, 28 + tb:29 + tb], 127.0)))(),
                inc=(sV, 1), tag=f"r2{r}.{tb}")
            stq_v = sV.n
            em("scalar", (lambda tb=tb: (lambda E: E.activation(
                ln2scr, y2st2[tb % 2], AF.Copy, bias=MAGIC,
                scale=stats[:, 28 + tb:29 + tb])))(),
                waits=[(sV, stq_v)], inc=(sS, 1), tag=f"rq1{r}.{tb}")
            rq1_done[tb] = sS.n
            em("scalar", (lambda tb=tb: (lambda E: E.activation(
                y2qV[:, tb * 2048:(tb + 1) * 2048], ln2scr, AF.Copy,
                bias=-MAGIC)))(),
                inc=(sS, 1), tag=f"rq2{r}.{tb}")
            rq2_tb[tb] = sS.n
        y2q_s = sS.n
        em("vector", lambda E: E.tensor_scalar(
            stats[:, 44:48], stats[:, 32:36], mcl[:, 0:1], None, ALU.mult),
            waits=[(sV, fst.get("means_v", 0))], tag=f"sc1{r}")
        em("vector", lambda E: E.tensor_scalar(
            stats[:, 48:52], stats[:, 32:36], mcl[:, 1:2], None, ALU.mult),
            inc=(sV, 1), tag=f"sc3{r}")
        sc_v = sV.n

        y2qT3 = y2qTV.rearrange("p (dg x) -> p dg x", x=512)
        for tb in range(4):
            for grp in range(2):
                blocks = [y2qV[:, tb * 2048 + (grp * 8 + i) * 128:
                              tb * 2048 + (grp * 8 + i) * 128 + 128] for i in range(8)]
                transpose_batch3(blocks,
                                 y2qT3[:, grp * 8:(grp + 1) * 8,
                                       tb * 128:(tb + 1) * 128],
                                 [(sS, rq2_tb[tb])], f"q{r}.{tb}.{grp}")
        y2qT_e = sES.n

        for mi, (dstV, func, sccol) in enumerate(
                [(aV, AF.Silu, 44), (bV, AF.Copy, 48)]):
            agw = [(sCC, fst.get("ag2_cc", 0))]
            for ft in range(FT):
                s, ev = acquire_slot()
                group_first = True
                for q in range(4):
                    srcs = [(i * 512, wq_ag[mi][4 * q + i, ft][:, :]) for i in range(4)]
                    si, wv = wchunk_load(srcs, extra_waits=agw, tag=f"fc{r}.{mi}.{ft}.{q}")
                    for dgl in range(4):
                        dgb = q * 4 + dgl
                        for tbi in range(4):
                            w = []
                            if group_first:
                                w = evw(ev) + [(sES, y2qT_e), (sV, sc_v)]
                            if dgl == 0 and tbi == 0:
                                w.append((sW2[si], wv))
                            last_of_chunk = (dgl == 3 and tbi == 3)
                            em("tensor", (lambda s=s, tbi=tbi, si=si, dgl=dgl, dgb=dgb:
                                (lambda E: E.matmul(
                                    pb[s * 4 + tbi][:],
                                    y2qTV[:, dgb * 512 + tbi * 128: dgb * 512 + (tbi + 1) * 128],
                                    wslot[:, si * 2048 + dgl * 512: si * 2048 + (dgl + 1) * 512],
                                    start=(dgb == 0), stop=(dgb == 15))))(),
                                waits=w, inc=((sMM, 1) if last_of_chunk else None),
                                tag=f"F1.{r}.{mi}.{ft}.{q}")
                            group_first = False
                    wchunk_done(si)
                mmv = sMM.n
                for tbi in range(4):
                    em("scalar", (lambda dstV=dstV, func=func, tbi=tbi, ft=ft, s=s,
                                  sccol=sccol: (lambda E: E.activation(
                        dstV[:, tbi * FH + ft * 512: tbi * FH + ft * 512 + 512],
                        pb[s * 4 + tbi][:], func,
                        scale=stats[:, sccol + tbi:sccol + tbi + 1])
                        if func == AF.Copy else E.activation(
                        dstV[:, tbi * FH + ft * 512: tbi * FH + ft * 512 + 512],
                        pb[s * 4 + tbi][:], func,
                        bias=zcol[:, 0:1], scale=stats[:, sccol + tbi:sccol + tbi + 1])))(),
                        waits=[(sMM, mmv)], inc=(sES, 1), tag=f"F1e{r}.{mi}.{ft}.{tbi}")
                release_slot(s)
                if mi == 1:
                    if ft == 0:
                        em("vector", lambda E: E.memset(stats[:, 24:28], 0.0),
                           tag=f"hz{r}")
                    ev_g = sES.n
                    for tbi in range(4):
                        lo = tbi * FH + ft * 512
                        em("vector", (lambda lo=lo: (lambda E: E.tensor_mul(
                            aV[:, lo:lo + 512], aV[:, lo:lo + 512],
                            bV[:, lo:lo + 512])))(),
                            waits=([(sES, ev_g)] if tbi == 0 else []),
                            tag=f"h{r}.{ft}.{tbi}")
                        em("vector", (lambda lo=lo: (lambda E: E.reduce_max(
                            stats[:, 60:61], aV[:, lo:lo + 512], axis=AX.X,
                            apply_absolute_value=True)))(), tag=f"hm{r}.{ft}.{tbi}")
                        em("vector", (lambda tbi=tbi: (lambda E: E.tensor_max(
                            stats[:, 24 + tbi:25 + tbi], stats[:, 24 + tbi:25 + tbi],
                            stats[:, 60:61])))(), tag=f"hx{r}.{ft}.{tbi}")
        ffn1_ev = sES.n

        hq_tb = [0, 0, 0, 0]
        st_v = [0, 0, 0, 0]
        for tb in range(4):
            em("vector", (lambda tb=tb: (lambda E: E.tensor_scalar_max(
                stats[:, 24 + tb:25 + tb], stats[:, 24 + tb:25 + tb], EPS_Q)))(),
                tag=f"hc{r}.{tb}")
            em("vector", (lambda tb=tb: (lambda E: E.tensor_scalar_mul(
                stats[:, 40 + tb:41 + tb], stats[:, 24 + tb:25 + tb], 1.0 / 127.0)))(),
                tag=f"hr{r}.{tb}")
            em("vector", (lambda tb=tb: (lambda E: E.reciprocal(
                stats[:, 36 + tb:37 + tb], stats[:, 24 + tb:25 + tb])))(),
                tag=f"h1{r}.{tb}")
            em("vector", (lambda tb=tb: (lambda E: E.tensor_scalar_mul(
                stats[:, 36 + tb:37 + tb], stats[:, 36 + tb:37 + tb], 127.0)))(),
                inc=(sV, 1), tag=f"h2{r}.{tb}")
            st_v[tb] = sV.n
        hqst2 = [hqstage, eqv(14080, 2816)]
        hq1_led = {}
        hq2_led = {}
        for tb in range(4):
            for hh in range(2):
                k = tb * 2 + hh
                w = [(sV, st_v[tb])]
                if k >= 2:
                    w.append((sV, hq2_led[k - 2]))
                em("scalar", (lambda tb=tb, hh=hh, k=k: (lambda E: E.activation(
                    hqst2[k % 2],
                    aV[:, tb * FH + hh * 2816: tb * FH + (hh + 1) * 2816],
                    AF.Copy, bias=MAGIC,
                    scale=stats[:, 36 + tb:37 + tb])))(),
                    waits=w, inc=(sES, 1), tag=f"hq1{r}.{tb}.{hh}")
                hq1_led[k] = sES.n
                em("vector", (lambda tb=tb, hh=hh, k=k: (lambda E: E.tensor_scalar_add(
                    bV[:, tb * FH + hh * 2816: tb * FH + (hh + 1) * 2816],
                    hqst2[k % 2], -MAGIC)))(),
                    waits=[(sES, hq1_led[k])], inc=(sV, 1), tag=f"hq2{r}.{tb}.{hh}")
                hq2_led[k] = sV.n
            hq_tb[tb] = [(sV, hq2_led[tb * 2 + 1])]
        em("vector", lambda E: E.tensor_scalar(
            stats[:, 52:56], stats[:, 40:44], mcl[:, 2:3], None, ALU.mult),
            inc=(sV, 1), tag=f"sc2{r}")
        hq_v = sV.n

        hqT3 = aV.rearrange("p (fb x) -> p fb x", x=512)
        for tb in range(4):
            for grp in range(6):
                fb0 = grp * 8
                nb = min(8, FB - fb0)
                blocks = [bV[:, tb * FH + (fb0 + i) * 128:
                             tb * FH + (fb0 + i) * 128 + 128] for i in range(nb)]
                transpose_batch3(blocks,
                                 hqT3[:, fb0:fb0 + nb, tb * 128:(tb + 1) * 128],
                                 list(hq_tb[tb]), f"hq{r}.{tb}.{grp}")
        hqT_e = sES.n

        agw2 = [(sCC, fst.get("ag2_cc", 0))]
        for ht in range(4):
            s, ev = acquire_slot()
            group_first = True
            for q in range(11):
                srcs = [(i * 512, wq_ag[2][4 * q + i, ht][:, :]) for i in range(4)]
                si, wv = wchunk_load(srcs, extra_waits=agw2, tag=f"gc{r}.{ht}.{q}")
                for fbl in range(4):
                    fb = q * 4 + fbl
                    for tbi in range(4):
                        w = []
                        if group_first:
                            w = evw(ev) + [(sES, hqT_e)]
                        if fbl == 0 and tbi == 0:
                            w.append((sW2[si], wv))
                        last_of_chunk = (fbl == 3 and tbi == 3)
                        em("tensor", (lambda s=s, tbi=tbi, si=si, fbl=fbl, fb=fb:
                            (lambda E: E.matmul(
                                pb[s * 4 + tbi][:],
                                aV[:, fb * 512 + tbi * 128: fb * 512 + (tbi + 1) * 128],
                                wslot[:, si * 2048 + fbl * 512: si * 2048 + (fbl + 1) * 512],
                                start=(fb == 0), stop=(fb == FB - 1))))(),
                            waits=w, inc=((sMM, 1) if last_of_chunk else None),
                            tag=f"F2.{r}.{ht}.{q}")
                        group_first = False
                wchunk_done(si)
            mmv = sMM.n
            for tbi in range(4):
                oi = (ht * 4 + tbi) % 2
                em("vector", (lambda s=s, tbi=tbi, ht=ht, oi=oi: (lambda E: E.scalar_tensor_tensor(
                    o2st[oi], pb[s * 4 + tbi][:], stats[:, 52 + tbi:53 + tbi],
                    out1V[:, tbi * 2048 + ht * 512: tbi * 2048 + (ht + 1) * 512],
                    ALU.mult, ALU.add)))(),
                    waits=[(sMM, mmv), (sO2[oi], o2led[oi])], inc=(sEV, 1),
                    tag=f"F2e{r}.{ht}.{tbi}")
                evn = sEV.n
                dma("gpsimd",
                    out[r * TS + tbi * 128: r * TS + (tbi + 1) * 128,
                        ht * 512:(ht + 1) * 512],
                    o2st[oi], waits=[(sEV, evn)], inc=(sO2[oi], 16), tag=f"o{r}.{ht}.{tbi}")
                o2led[oi] = sO2[oi].n
            release_slot(s)
        ffn2ev[r] = sEV.n
        xfree_mm[r + 1] = sMM.n
        out_total[0] = ((sO2[0], sO2[0].n), (sO2[1], sO2[1].n))

    for r in range(ROUNDS):
        emit_round(r)
        if r + 1 < ROUNDS:
            load_x(r + 1, eng="scalar")

    em("gpsimd", None, waits=[(sO2[0], sO2[0].n), (sO2[1], sO2[1].n)], tag="fin")

    P.check()

    for s in sems:
        s.h = nc.alloc_semaphore(s.name)
    with nc.Block() as block:
        P.emit(block)
    return nc


_nc_cache = {}


def _get_nc():
    if "nc" not in _nc_cache:
        _nc_cache["nc"] = build_nc()
    return _nc_cache["nc"]


def kernel(L0_attn_x, L0_attn_w_f, L0_attn_w_i, L0_attn_w_v, L0_attn_w_q,
           L0_attn_w_ro, L0_attn_w_aux, L0_attn_w_up, L0_attn_w_gate,
           L0_attn_w_down, L0_attn_init_state, L0_ffn_w1, L0_ffn_w2, L0_ffn_w3):
    x = np.ascontiguousarray(np.asarray(L0_attn_x, dtype=np.float32).reshape(16384, 2048))
    wlist = [np.ascontiguousarray(np.asarray(a, dtype=np.float32)) for a in [
        L0_attn_w_f, L0_attn_w_i, L0_attn_w_v, L0_attn_w_q, L0_attn_w_up,
        L0_attn_w_gate, L0_attn_w_ro, L0_attn_w_down]]
    w1 = np.asarray(L0_ffn_w1, dtype=np.float32)
    w2 = np.asarray(L0_ffn_w2, dtype=np.float32)
    w3 = np.asarray(L0_ffn_w3, dtype=np.float32)
    init = np.asarray(L0_attn_init_state, dtype=np.float32)
    init_s = np.ascontiguousarray(init.reshape(16, 128).T)
    idn = np.eye(128, dtype=ml_dtypes.bfloat16)

    nc = _get_nc()
    in_maps = []
    for c in range(NCORES):
        xin = np.ascontiguousarray(np.concatenate(
            [x[(8 * r + c) * TS:(8 * r + c + 1) * TS] for r in range(ROUNDS)], axis=0))
        mask_m = np.zeros((128, 8), dtype=np.float32)
        mask_m[:, :c] = 1.0
        mask_n = np.ascontiguousarray(1.0 - mask_m)
        w1sh = np.ascontiguousarray(w1[:, c * 256:(c + 1) * 256])
        w3sh = np.ascontiguousarray(w3[:, c * 256:(c + 1) * 256])
        w2sh = np.zeros((2048, 768), dtype=np.float32)
        lo = c * 768
        hi = min(FH, lo + 768)
        if lo < FH:
            w2sh[:, :hi - lo] = w2[:, lo:hi]
        im = dict(xin=xin, init_s=init_s, mask_m=mask_m, mask_n=mask_n,
                  idn=idn, w1s=w1sh, w3s=w3sh, w2s=w2sh, wmine=wlist[c])
        in_maps.append(im)

    res = run_bass_kernel_spmd(nc, in_maps, core_ids=list(range(NCORES)))

    full = np.empty((16384, 2048), dtype=np.float32)
    for c in range(NCORES):
        oc = np.asarray(res.results[c]["out"])
        for r in range(ROUNDS):
            full[(8 * r + c) * TS:(8 * r + c + 1) * TS] = oc[r * TS:(r + 1) * TS]
    return full.reshape(1, 16384, 2048)
